# version 20
# baseline (speedup 1.0000x reference)
import numpy as np
from collections import defaultdict

import concourse.bass as bass
import concourse.mybir as mybir
from concourse.bass_utils import run_bass_kernel_spmd
import ml_dtypes

F32 = mybir.dt.float32
BF16 = mybir.dt.bfloat16
AL = mybir.AluOpType
AF = mybir.ActivationFunctionType
AX = mybir.AxisListType

L = 2048
D = 1024
Q = 128
SUP = 512
NSUP = L // SUP
NCQ = SUP // Q
NH = 32
HP = 64
NXS = 16
NXBC = 18
WA_COLS = NXBC * 128
HH = 16
EPS = 1e-5
W6 = float(np.float32(1.0) / np.float32(6.0))
TAU6 = [float(np.float32(t) / np.float32(6.0)) for t in (0.25, 0.75, 1.25, 1.75, 2.5, 3.5, 5.0)]
ENGS = ("sync", "scalar", "vector", "tensor", "gpsimd")
import os
KDBG = set(x for x in os.environ.get("KDBG", "").split(",") if x)
DBGTC = int(os.environ.get("DBGTC", "0"))


class Prog:
    def __init__(self):
        self.steps = []
        self.cnt = defaultdict(int)
        self.res = {}
        self.last_wait = defaultdict(dict)
        self.pending = {}

    def fence(self):
        snap = dict(self.cnt)
        for eng in ENGS:
            pf = self.pending.setdefault(eng, {})
            for o, v in snap.items():
                if o != eng and v > 0 and pf.get(o, -1) < v:
                    pf[o] = v

    def step(self, eng, emit, reads=(), writes=(), dma=False, sreads=(), tag=""):
        waits = {}

        def need(deng, dval, force=False):
            if deng == eng and not dma and not force:
                return
            if waits.get(deng, -1) < dval:
                waits[deng] = dval

        for o, v in self.pending.pop(eng, {}).items():
            need(o, v)
        for r in sreads:
            e = self.res.get(r)
            if e and e["w"]:
                need(*e["w"], force=True)
        for r in reads:
            e = self.res.get(r)
            if e and e["w"]:
                need(*e["w"], force=(e["w"][0] == eng and eng != "tensor"))
        for w in writes:
            e = self.res.get(w)
            if e:
                if e["w"]:
                    need(*e["w"])
                for rr in e["r"]:
                    need(*rr)
        fw = {}
        for s, v in waits.items():
            if self.last_wait[eng].get(s, -1) < v:
                fw[s] = v
                self.last_wait[eng][s] = v
        inc = 16 if dma else 1
        self.cnt[eng] += inc
        self.steps.append((eng, fw, emit, inc, tag))
        dep = (eng, self.cnt[eng])
        for w in writes:
            self.res[w] = {"w": dep, "r": []}
        for r in list(reads) + list(sreads):
            self.res.setdefault(r, {"w": None, "r": []})["r"].append(dep)


def build_program():
    nc = bass.Bass("TRN2", target_bir_lowering=False)
    dr = {}

    def din(name, shape, dt=F32):
        dr[name] = nc.dram_tensor(name, list(shape), dt, kind="ExternalInput").ap()
        return dr[name]

    x_in = din("x", (L, D))
    din("xT", (D, L))
    for d in ("f", "b"):
        din(f"wa_{d}", (D, WA_COLS))
        din(f"wz_{d}", (D, 2048))
        din(f"op_{d}", (D, 2048))
        din(f"ph_{d}", (D, D))
        din(f"convw_{d}", (128, NXBC * 4))
        din(f"convb_{d}", (128, NXBC))
        din(f"dtb_{d}", (32, 1))
        din(f"alog_{d}", (32, 1))
        din(f"dcol_{d}", (128, NXS))
        din(f"rmscol_{d}", (128, NXS))
    din("pbcol", (128, 8))
    din("lnw_a", (1, D))
    din("lnb_a", (1, D))
    din("wg", (2816, D))
    din("wu", (2816, D))
    din("wdn", (D, 2816))
    din("maskF", (128, 128))
    din("maskB", (128, 128))
    din("idf32", (128, 128))
    din("ones_bf", (128, 1), BF16)
    din("i64hi", (128, HP), BF16)

    out = nc.dram_tensor("out", [L, D], F32, kind="ExternalOutput").ap()
    xfm_scr = nc.dram_tensor("xfm_scr", [8, 128, L], BF16, kind="Internal").ap()
    yscr = nc.dram_tensor("yscr", [2, NXS, 128, L], BF16, kind="Internal").ap()
    lg_scr = nc.dram_tensor("lg_scr", [2, 16, 32, Q], F32, kind="Internal").ap()
    x1_scr = nc.dram_tensor("x1_scr", [L, D], F32, kind="Internal").ap()
    rs_scr = nc.dram_tensor("rs_scr", [2, NSUP, 1, SUP], F32, kind="Internal").ap()

    ctxs = []

    def sbuf(name, shape, dt):
        t = nc.sbuf_tensor(name, list(shape), dt)
        h = t.__enter__()
        ctxs.append(t)
        return h

    def psum_t(name):
        t = nc.psum_tensor(name, [128, 512], F32)
        h = t.__enter__()
        ctxs.append(t)
        return h

    consts_b = sbuf("consts_b", [128, 2 + HP], BF16)
    ONES = consts_b[:, 0:1]
    I64HI = consts_b[:, 2:2 + HP]
    masks = sbuf("masks", [128, 3 * 128], F32)
    convw = sbuf("convw", [128, 2 * NXBC * 4], F32)
    convb = sbuf("convb", [128, 2 * NXBC], F32)
    smallf = sbuf("smallf", [128, 16], F32)
    zeros128 = sbuf("zeros128", [128, 128], F32)
    dcol = sbuf("dcol", [128, 2 * NXS], F32)
    rmscol = sbuf("rmscol", [128, 2 * NXS], F32)
    pbcol = sbuf("pbcol_s", [128, 8], F32)
    epst = sbuf("epst", [128, 16], F32)
    hstate = sbuf("hstate", [128, NH * HP], BF16)
    acc = sbuf("acc", [128, 8 * L], BF16)

    ARA = sbuf("arA", [128, 22784], BF16)
    ARB = sbuf("arB", [128, 18432], BF16)
    ARC = sbuf("arC", [128, 18432], BF16)
    ARD = sbuf("arD", [128, 16384], BF16)
    ARE = sbuf("arE", [128, 10240], BF16)

    wab = ARA[:, 0:18432]
    xsup = ARA[:, 18432:22528]
    xbc = ARB[:, 0:9324]
    xsbc = ARB[:, 9344:13952]
    bctm = ARB[:, 13952:14208]
    cbm = ARB[:, 16384:16640]
    bbar = ARB[:, 14336:15360]
    aqi = ARB[:, 15360:16384]
    utm = ARC[:, 0:4096]
    ysb = ARC[:, 4096:8192]
    gam = ARC[:, 8192:10240]
    smeg = ARC[:, 10240:12288]
    almeg = ARC[:, 12288:14336]
    cameg = ARC[:, 14336:16384]
    adf = ARD[:].bitcast(F32)
    convt = adf[:, 0:2304]
    delt = adf[:, 2304:4352]
    alin = adf[:, 4352:6400]
    dtraw = adf[:, 6400:6912]
    dtsp = adf[:, 6912:7424]
    sptmp = adf[:, 7424:7936]
    lgfm = adf[:, 7936:8064]
    dtlg_tm = adf[:, 8064:8192]
    opstage = ARA[:, 0:16384]
    phring = ARA[:, 16384:18432]
    wzb = ARB[:, 0:16384]
    rstdr = ARB[:, 16384:17408].bitcast(F32)
    evtmp = ARB[:, 17408:17920]
    weff = ARC[:, 0:16384]
    ysup = ARD[:, 0:8192]
    zsil = ARD[:, 8192:16384]
    sqb = ysup
    ygat = ARE[:, 0:8192]
    etm = ARE[:, 0:2048]
    xtmf = ARE[:, 2048:4096].bitcast(F32)
    ep1 = ARE[:, 4096:6144].bitcast(F32)
    lnrep = ARE[:, 6144:10240].bitcast(F32)
    wq_g = [ARA[:, i * 2816:(i + 1) * 2816] for i in range(8)]
    wq_u = [ARB[:, i * 2816:(i + 1) * 2816] for i in range(6)] + \
           [ARC[:, i * 2816:(i + 1) * 2816] for i in range(2)]
    wq_d = [ARC[:, 5632 + i * 1024: 5632 + (i + 1) * 1024] for i in range(12)] + \
           [acc[:, i * 1024:(i + 1) * 1024] for i in range(10)]
    ftA = ARD[:, 0:2048].bitcast(F32)
    ftB = ARD[:, 2048:4096].bitcast(F32)
    ftC = ARD[:, 4096:6144].bitcast(F32)
    fsgn = ARD[:, 6144:8192].bitcast(F32)
    fstage = ARD[:, 8192:10240].bitcast(F32)
    fsc = ARD[:, 10240:10624].bitcast(F32)
    ephr = [ARD[:, 10624:11648].bitcast(F32), ARD[:, 13696:14720].bitcast(F32)]
    ep2fr = [ARD[:, 11648:13696].bitcast(F32)[:, 0:1024],
             ARD[:, 8192:10240].bitcast(F32)]
    qout = acc[:, 10240:11264]
    hqtm = acc[:, 11264:14080]
    xqtm = acc[:, 14080:15104]
    xqfm = acc[:, 15104:16128]
    hqfm = ARE[:, 0:5632]
    dntm = ARE[:, 5632:7680]
    ep1f = ARE[:, 7680:9728].bitcast(F32)
    dtmp = ARE[:, 9728:10240]

    pmm = [psum_t("ps_mm0"), psum_t("ps_mm1")]
    pcb = psum_t("ps_cb")
    py = [psum_t("ps_y0"), psum_t("ps_y1")]
    ph_ = [psum_t("ps_h0"), psum_t("ps_h1")]
    pst = psum_t("ps_st")
    PB8 = [pmm[0], pmm[1], pcb, py[0], py[1], ph_[0], ph_[1], pst]
    PB8N = ["pmm0", "pmm1", "pcb", "py0", "py1", "ph0", "ph1", "pst"]

    p = Prog()
    V, S, G, T = "vector", "scalar", "gpsimd", "tensor"
    dbg_outs = []

    def dump(name, src_ap, reads, dt, shape):
        if name not in KDBG:
            return
        o = nc.dram_tensor("dbg_" + name, list(shape), dt, kind="ExternalOutput").ap()
        dbg_outs.append("dbg_" + name)

        def em(e, o=o, s=src_ap):
            return e.dma_start(out=o, in_=s)
        p.step("sync", em, reads=reads, writes=[f"dbg{name}"], dma=True, tag="dump " + name)

    def dma(eng, outap, inap, reads, writes, tag="", tp=False):
        if tp:
            eng = "sync"
        elif eng == "synck":
            eng = "sync"
        elif eng == "sync":
            eng = "scalar"

        def em(e, o=outap, i=inap, tp=tp):
            if tp:
                return e.dma_start(out=o, in_=i, transpose=True)
            return e.dma_start(out=o, in_=i)
        p.step(eng, em, reads=reads, writes=writes, dma=True, tag=tag)

    def st(eng, emit, reads=(), writes=(), sreads=(), tag=""):
        p.step(eng, emit, reads=reads, writes=writes, sreads=sreads, tag=tag)

    def bcast_f(ap2d, inner):
        return ap2d.unsqueeze(2).broadcast_to([ap2d.shape[0], ap2d.shape[1], inner])

    dma("sync", consts_b[:, 0:1], dr["ones_bf"], [], ["ones"])
    dma("sync", consts_b[:, 2:2 + HP], dr["i64hi"], [], ["i64hi"])
    dma("sync", masks[:, 0:128], dr["maskF"], [], ["maskF"])
    dma("sync", masks[:, 128:256], dr["maskB"], [], ["maskB"])
    dma("sync", masks[:, 256:384], dr["idf32"], [], ["idf32"])
    for i, d in enumerate(("f", "b")):
        dma("sync", convw[:, i * NXBC * 4:(i + 1) * NXBC * 4], dr[f"convw_{d}"], [], [f"convw{d}"])
        dma("sync", convb[:, i * NXBC:(i + 1) * NXBC], dr[f"convb_{d}"], [], [f"convb{d}"])
        dma("sync", smallf[64:96, i:i + 1], dr[f"dtb_{d}"], ["smallf"], ["smallf"])
        dma("sync", smallf[64:96, 4 + i:5 + i], dr[f"alog_{d}"], ["smallf"], ["smallf"])
        dma("sync", dcol[:, i * NXS:(i + 1) * NXS], dr[f"dcol_{d}"], [], [f"dcol{d}"])
        dma("sync", rmscol[:, i * NXS:(i + 1) * NXS], dr[f"rmscol_{d}"], [], [f"rmscol{d}"])
    dma("sync", pbcol[:], dr["pbcol"], [], ["pbcol"])
    st(V, lambda e: e.memset(zeros128[:], 0.0), [], ["zeros128"])
    for i, d in enumerate(("f", "b")):
        st(S, lambda e, i=i: e.activation(smallf[64:96, 6 + i:7 + i],
                                          smallf[64:96, 4 + i:5 + i], AF.Exp),
           ["smallf"], ["smallf"], "Aexp")
        st(V, lambda e, i=i: e.tensor_scalar(smallf[64:96, 2 + i:3 + i],
                                             smallf[64:96, 6 + i:7 + i],
                                             -1.0, None, op0=AL.mult),
           ["smallf"], ["smallf"], "Aneg")

    p0f = [ARE[:, 0:4096].bitcast(F32), ARE[:, 4096:8192].bitcast(F32)]
    p0b = [ARD[:, 0:2048], ARD[:, 2048:4096]]
    for fc in range(8):
        r = fc % 2
        dma("gpsimd", p0f[r][:], dr["xT"][fc * 128:(fc + 1) * 128, :],
            [f"p0f{r}"], [f"p0f{r}"])
        st(V, lambda e, r=r: e.tensor_copy(p0b[r][:], p0f[r][:]),
           [f"p0f{r}", f"p0b{r}"], [f"p0b{r}"], "x cast")
        dma("sync", xfm_scr[fc, :, :], p0b[r][:], [f"p0b{r}"], [f"xfms{fc}", f"p0b{r}"])
        st(V, lambda e, fc=fc, r=r: e.tensor_scalar(
            acc[:, fc * L:(fc + 1) * L], p0f[r][:], pbcol[:, fc:fc + 1], None, op0=AL.add),
           [f"p0f{r}", "pbcol"], ["acc"], "accinit")
    XFM = [f"xfms{fc}" for fc in range(8)]

    def pass_a(d):
        di = 0 if d == "f" else 1
        fwd = d == "f"
        for k in range(8):
            dma("gpsimd", wab[:, k * WA_COLS:(k + 1) * WA_COLS],
                dr[f"wa_{d}"][k * 128:(k + 1) * 128, :], ["wab"], ["wab", f"wa_{k}"])
        st(V, lambda e: e.memset(xbc[:], 0.0), ["xbc"], ["xbc"], "zero xbc")
        st(V, lambda e: e.memset(hstate[64:128, :], 0.0), ["hstate"], ["hstate"], "zero h")

        sup_order = list(range(NSUP)) if fwd else list(range(NSUP - 1, -1, -1))
        for si, sp in enumerate(sup_order):
            if si > 0:
                xv = xbc.rearrange("p (c w) -> p c w", c=NXBC)
                if fwd:
                    st(V, lambda e, xv=xv: e.tensor_copy(xv[:, :, 0:3], xv[:, :, SUP:SUP + 3]),
                       ["xbc"], ["xbc"], "halo")
                else:
                    st(V, lambda e, xv=xv: e.tensor_copy(xv[:, :, SUP + 3:SUP + 6], xv[:, :, 3:6]),
                       ["xbc"], ["xbc"], "halo")
            dma("sync", xsup[:].rearrange("p (c t) -> p c t", c=8),
                xfm_scr[:, :, sp * SUP:(sp + 1) * SUP].transpose([1, 0, 2]),
                XFM + ["xsup"], ["xsup"])
            for oc in range(NXBC):
                pb = pmm[oc % 2]
                pres = f"pmm{oc % 2}"

                def em_mm(e, oc=oc, pb=pb):
                    last = None
                    for k in range(8):
                        last = e.matmul(pb[:, 0:SUP],
                                        wab[:, k * WA_COLS + oc * 128: k * WA_COLS + (oc + 1) * 128],
                                        xsup[:, k * SUP:(k + 1) * SUP],
                                        start=(k == 0), stop=(k == 7))
                    return last
                st(T, em_mm, [f"wa_{k}" for k in range(8)] + ["xsup", pres], [pres], "A mm")
                if oc < NXBC - 1:
                    st(V, lambda e, oc=oc, pb=pb: e.tensor_copy(
                        xbc[:, oc * (SUP + 6) + 3: oc * (SUP + 6) + 3 + SUP], pb[:, 0:SUP]),
                       [pres, "xbc"], ["xbc"], "ev xbc")
                else:
                    st(V, lambda e, oc=oc, pb=pb: e.tensor_copy(
                        xbc[0:64, oc * (SUP + 6) + 3: oc * (SUP + 6) + 3 + SUP], pb[0:64, 0:SUP]),
                       [pres, "xbc"], ["xbc"], "ev cdup")
                    st(V, lambda e, pb=pb: e.tensor_copy(dtraw[64:96, :], pb[64:96, 0:SUP]),
                       [pres, "dtraw"], ["dtraw"], "ev dt")
            st(S, lambda e, i=di: e.activation(sptmp[64:96, :], dtraw[64:96, :], AF.Abs,
                                               bias=smallf[64:96, i:i + 1]),
               ["dtraw", "smallf", "sptmp"], ["sptmp"], "sp abs")
            st(S, lambda e: e.activation(sptmp[64:96, :], sptmp[64:96, :], AF.Exp, scale=-1.0),
               ["sptmp"], ["sptmp"], "sp exp")
            st(S, lambda e: e.activation(sptmp[64:96, :], sptmp[64:96, :], AF.Ln, bias=1.0),
               ["sptmp"], ["sptmp"], "sp ln")
            st(V, lambda e, i=di: e.tensor_scalar(dtsp[64:96, :], dtraw[64:96, :],
                                                  smallf[64:96, i:i + 1], 0.0,
                                                  op0=AL.add, op1=AL.max),
               ["dtraw", "smallf", "dtsp"], ["dtsp"], "sp relu")
            st(V, lambda e: e.tensor_tensor(dtsp[64:96, :], dtsp[64:96, :], sptmp[64:96, :],
                                            op=AL.add),
               ["dtsp", "sptmp"], ["dtsp"], "sp add")

            chunk_order = list(range(NCQ)) if fwd else list(range(NCQ - 1, -1, -1))

            def stage_A(ci):
                c_glob = sp * NCQ + ci
                toff = ci * Q
                ring = c_glob % 2
                XS_R = f"xsbc{ring}"
                xsb_o = ring * NXBC * 128
                dtlg = dtlg_tm[:, (ring) * 64:(ring + 1) * 64]
                DTLG = f"dtlg{ring}"
                cbmr = cbm[:, ring * 128:(ring + 1) * 128]
                CBM = f"cbm{ring}"

                cwv = convw[:, di * NXBC * 4:(di + 1) * NXBC * 4].rearrange(
                    "p (c k) -> p c k", k=4)

                xbcv = xbc[:].rearrange("p (c w) -> p c w", c=NXBC)

                def xview(k, ci=ci, fwd=fwd, xbcv=xbcv):
                    base = ci * Q + 3 + ((k - 3) if fwd else k)
                    return xbcv[:, :, base: base + Q]

                def wview(k, fwd=fwd):
                    kk = k if fwd else 3 - k
                    return cwv[:, :, kk:kk + 1].broadcast_to([128, NXBC, Q])

                cbv = convb[:, di * NXBC:(di + 1) * NXBC].unsqueeze(2) \
                    .broadcast_to([128, NXBC, Q])
                mtmp = xsbc[:, xsb_o:xsb_o + NXBC * 128].rearrange(
                    "p (c w) -> p c w", w=128)
                cvv = convt[:].rearrange("p (c w) -> p c w", w=128)
                st(V, lambda e: e.tensor_tensor(cvv, xview(0), wview(0), op=AL.mult),
                   ["xbc", f"convw{d}", "convt"], ["convt"], "conv0")
                for k in (1, 2, 3):
                    st(V, lambda e, k=k: e.tensor_tensor(mtmp, xview(k), wview(k), op=AL.mult),
                       ["xbc", f"convw{d}", XS_R], [XS_R], f"convm{k}")
                    st(V, lambda e: e.tensor_tensor(cvv, cvv, mtmp, op=AL.add),
                       [XS_R, "convt"], ["convt"], f"conva{k}")
                st(V, lambda e: e.tensor_tensor(cvv, cvv, cbv, op=AL.add),
                   [f"convb{d}", "convt"], ["convt"], "convb")
                st(S, lambda e, o=xsb_o: e.activation(xsbc[:, o:o + NXBC * 128], convt[:], AF.Silu),
                   ["convt", XS_R], [XS_R], "silu")
                if fwd:
                    st(V, lambda e, toff=toff, i=di: e.tensor_scalar(
                        sptmp[64:96, toff:toff + Q], dtsp[64:96, toff:toff + Q],
                        smallf[64:96, 2 + i:3 + i], None, op0=AL.mult),
                       ["dtsp", "smallf", "sptmp"], ["sptmp"], "dtA")
                else:
                    st(V, lambda e, toff=toff, i=di: e.tensor_scalar(
                        sptmp[64:96, toff:toff + Q], dtsp[64:96, toff:toff + Q],
                        smallf[64:96, 2 + i:3 + i], 0.0, op0=AL.mult, op1=AL.add,
                        accum_out=smallf[64:96, 8:9]),
                       ["dtsp", "smallf", "sptmp"], ["sptmp", "smallf"], "dtA")
                st(V, lambda e, toff=toff: e.tensor_tensor_scan(
                    lgfm[64:96, :], sptmp[64:96, toff:toff + Q], zeros128[64:96, :],
                    0.0, op0=AL.add, op1=AL.add),
                   ["sptmp", "zeros128", "lgfm"], ["lgfm"], "scan")
                if not fwd:
                    st(V, lambda e, toff=toff: e.scalar_tensor_tensor(
                        lgfm[64:96, :], lgfm[64:96, :], -1.0, sptmp[64:96, toff:toff + Q],
                        op0=AL.mult, op1=AL.add),
                       ["lgfm", "sptmp"], ["lgfm"], "suf1")
                    st(V, lambda e: e.tensor_scalar(lgfm[64:96, :], lgfm[64:96, :],
                                                    smallf[64:96, 8:9], None, op0=AL.add),
                       ["lgfm"], ["lgfm"], sreads=["smallf"], tag="suf2")
                dma("sync", lg_scr[di, c_glob, :, :], lgfm[64:96, :],
                    ["lgfm"], [f"lgscr{di}_{c_glob}"])
                id32 = masks[64:96, 256 + 64:256 + 96]
                st(T, lambda e, toff=toff, id32=id32: e.transpose(
                    pcb[:, 128:160], dtsp[64:96, toff:toff + Q], id32),
                   ["dtsp", "idf32", "pcb"], ["pcb"], "tp dt")
                st(T, lambda e, id32=id32: e.transpose(pcb[:, 160:192], lgfm[64:96, :], id32),
                   ["lgfm", "idf32", "pcb"], ["pcb"], "tp lg")
                st(V, lambda e, dtlg=dtlg: e.tensor_copy(dtlg, pcb[:, 128:192]),
                   ["pcb", DTLG], [DTLG], "ev dtlg")
                for fc in range(NXS):
                    dma("scalar", utm[:, ring * 2048 + fc * 128: ring * 2048 + (fc + 1) * 128],
                        xsbc[:, xsb_o + fc * 128: xsb_o + (fc + 1) * 128],
                        [XS_R, f"utm{ring}_{fc}"], [f"utm{ring}_{fc}"], tp=True)
                dma("scalar", bctm[:, ring * 128:(ring + 1) * 128],
                    xsbc[:, xsb_o + 16 * 128: xsb_o + 17 * 128],
                    [XS_R, f"bctm{ring}"], [f"bctm{ring}"], tp=True)

            def stage_A_post(ci):
                c_glob = sp * NCQ + ci
                ring = c_glob % 2
                XS_R = f"xsbc{ring}"
                xsb_o = ring * NXBC * 128
                dtlg = dtlg_tm[:, (ring) * 64:(ring + 1) * 64]
                DTLG = f"dtlg{ring}"
                cbmr = cbm[:, ring * 128:(ring + 1) * 128]
                CBM = f"cbm{ring}"
                UTM = [f"utm{ring}_{fc}" for fc in range(NXS)]
                ur = utm[:, ring * 2048:(ring + 1) * 2048]
                st(V, lambda e, ur=ur, dtlg=dtlg: e.tensor_tensor(
                    ur.rearrange("p (h w) -> p h w", h=NH),
                    ur.rearrange("p (h w) -> p h w", h=NH),
                    bcast_f(dtlg[:, 0:32], HP), op=AL.mult),
                   UTM + [DTLG], UTM, "U dt")
                st(T, lambda e, o=xsb_o: e.matmul(
                    pcb[:, 0:128],
                    xsbc[0:64, o + 16 * 128: o + 16 * 128 + 128],
                    xsbc[0:64, o + 17 * 128: o + 17 * 128 + 128],
                    start=True, stop=True),
                   [XS_R, "pcb"], ["pcb"], "CB")
                st(V, lambda e, cbmr=cbmr, m=(masks[:, 0:128] if fwd else masks[:, 128:256]):
                   e.tensor_tensor(cbmr, pcb[:, 0:128], m, op=AL.mult),
                   ["pcb", "maskF" if fwd else "maskB", CBM], [CBM], "CBm")

            def stage_B(ci, hh):
                c_glob = sp * NCQ + ci
                ring = c_glob % 2
                XS_R = f"xsbc{ring}"
                xsb_o = ring * NXBC * 128
                dtlg = dtlg_tm[:, ring * 64:(ring + 1) * 64]
                DTLG = f"dtlg{ring}"
                cbmr = cbm[:, ring * 128:(ring + 1) * 128]
                CBM = f"cbm{ring}"
                wt = (Q - 1) if fwd else 0
                dma("sync", delt[:],
                    lg_scr[di, c_glob, hh * HH:(hh + 1) * HH, :]
                    .rearrange("h t -> (h t)").unsqueeze(0).partition_broadcast(128),
                    [f"lgscr{di}_{c_glob}", "delt"], ["delt"])
                dma("sync", alin[:],
                    lg_scr[di, c_glob, hh * HH:(hh + 1) * HH, :]
                    .rearrange("h t -> (h t)").unsqueeze(0).partition_broadcast(128),
                    [f"lgscr{di}_{c_glob}", "alin"], ["alin"])
                st(V, lambda e, hh=hh, dtlg=dtlg: e.tensor_tensor(
                    delt[:].rearrange("p (h t) -> p h t", h=HH),
                    delt[:].rearrange("p (h t) -> p h t", h=HH),
                    bcast_f(dtlg[:, 32 + hh * HH:32 + (hh + 1) * HH], Q),
                    op=AL.subtract),
                   ["delt", DTLG], ["delt"], "dsub")
                st(V, lambda e: e.tensor_scalar(delt[:], delt[:], 0.0, None, op0=AL.min),
                   ["delt"], ["delt"], "dmin")
                st(S, lambda e: e.activation(gam[:], delt[:], AF.Exp),
                   ["delt", "gam"], ["gam"], "gamma")
                st(S, lambda e: e.activation(almeg[64:128, :], alin[64:128, :], AF.Exp),
                   ["alin", "almeg"], ["almeg"], "alpha")
                st(V, lambda e, cbmr=cbmr: e.tensor_tensor(
                    smeg[:].rearrange("p (h t) -> p h t", h=HH),
                    gam[:].rearrange("p (h t) -> p h t", h=HH),
                    cbmr.unsqueeze(1).broadcast_to([128, HH, Q]), op=AL.mult),
                   ["gam", CBM, "smeg"], ["smeg"], "S")
                st(V, lambda e, o=xsb_o: e.tensor_tensor(
                    cameg[64:128, :].rearrange("p (h t) -> p h t", h=HH),
                    xsbc[64:128, o + 16 * 128: o + 17 * 128]
                    .unsqueeze(1).broadcast_to([64, HH, Q]),
                    almeg[64:128, :].rearrange("p (h t) -> p h t", h=HH), op=AL.mult),
                   [XS_R, "almeg", "cameg"], ["cameg"], "Ca")
                st(V, lambda e, ring=ring, wt=wt: e.tensor_tensor(
                    bbar[:].rearrange("p (h w) -> p h w", h=HH),
                    bctm[:, ring * 128: ring * 128 + 64].unsqueeze(1)
                    .broadcast_to([128, HH, HP]),
                    gam[:].rearrange("p (h t) -> p h t", h=HH)[:, :, wt:wt + 1]
                    .broadcast_to([128, HH, HP]), op=AL.mult),
                   ["gam", f"bctm{ring}", "bbar"], ["bbar"], "Bbar")
                st(V, lambda e, wt=wt: e.tensor_tensor(
                    aqi[64:128, :].rearrange("p (h w) -> p h w", h=HH),
                    I64HI[64:128, :].unsqueeze(1).broadcast_to([64, HH, HP]),
                    almeg[64:128, :].rearrange("p (h t) -> p h t", h=HH)[:, :, wt:wt + 1]
                    .broadcast_to([64, HH, HP]), op=AL.mult),
                   ["almeg", "i64hi", "aqi"], ["aqi"], "aQI")

            def stage_C_mm(ci, hh):
                c_glob = sp * NCQ + ci
                ring = c_glob % 2
                XS_R = f"xsbc{ring}"
                xsb_o = ring * NXBC * 128

                def em_pairs(e, hh=hh, ring=ring):
                    last = None
                    started = {}
                    for pl in range(8):
                        pc = hh * 8 + pl
                        bidx = pl // 4
                        yb = py[bidx]
                        ycol = (pl % 4) * 128
                        for par in range(2):
                            hloc = 2 * pl + par
                            last = e.matmul(
                                yb[par * 64:(par + 1) * 64, ycol:ycol + 128],
                                utm[:, ring * 2048 + pc * 128 + par * 64:
                                    ring * 2048 + pc * 128 + par * 64 + 64],
                                smeg[:, hloc * Q:(hloc + 1) * Q],
                                start=not started.get((bidx, par), False), stop=False,
                                tile_position=(0, par * 64))
                            started[(bidx, par)] = True
                    for pl in range(8):
                        pc = hh * 8 + pl
                        yb = py[pl // 4]
                        ycol = (pl % 4) * 128
                        for par in range(2):
                            hloc = 2 * pl + par
                            h_glob = 2 * pc + par
                            last = e.matmul(
                                yb[par * 64:(par + 1) * 64, ycol:ycol + 128],
                                hstate[64:128, h_glob * HP:(h_glob + 1) * HP],
                                cameg[64:128, hloc * Q:(hloc + 1) * Q],
                                start=False,
                                stop=(pl % 4 == 3 and par == 1),
                                tile_position=(64, par * 64))
                    return last
                st(T, em_pairs, [f"utm{ring}_{fc}" for fc in range(NXS)]
                   + ["smeg", "hstate", "cameg", "py0", "py1"],
                   ["py0", "py1"], "Y mms")

                def em_upd(e, hh=hh, ring=ring):
                    last = None
                    started = [False, False]
                    for hl in range(HH):
                        h_glob = hh * HH + hl
                        bidx = hl // 8
                        hb = ph_[bidx]
                        hcol = (hl % 8) * 64
                        last = e.matmul(hb[64:128, hcol:hcol + 64],
                                        bbar[:, hl * HP:(hl + 1) * HP],
                                        utm[:, ring * 2048 + h_glob * HP:
                                            ring * 2048 + (h_glob + 1) * HP],
                                        start=not started[bidx], stop=False,
                                        tile_position=(0, 64))
                        started[bidx] = True
                    for hl in range(HH):
                        h_glob = hh * HH + hl
                        hb = ph_[hl // 8]
                        hcol = (hl % 8) * 64
                        last = e.matmul(hb[64:128, hcol:hcol + 64],
                                        aqi[64:128, hl * HP:(hl + 1) * HP],
                                        hstate[64:128, h_glob * HP:(h_glob + 1) * HP],
                                        start=False, stop=(hl % 8 == 7),
                                        tile_position=(64, 64))
                    return last
                st(T, em_upd, [f"utm{ring}_{fc}" for fc in range(NXS)]
                   + ["bbar", "aqi", "hstate", "ph0", "ph1"],
                   ["ph0", "ph1"], "h mms")

            def stage_C_ev(ci, hh):
                c_glob = sp * NCQ + ci
                ring = c_glob % 2
                XS_R = f"xsbc{ring}"
                xsb_o = ring * NXBC * 128
                for pl in range(8):
                    pc = hh * 8 + pl
                    yb = py[pl // 4]
                    ycol = (pl % 4) * 128
                    st(V, lambda e, pc=pc, yb=yb, ycol=ycol, ring=ring, i=di, o=xsb_o:
                       e.scalar_tensor_tensor(
                           ysb[:, ring * 2048 + pc * 128: ring * 2048 + (pc + 1) * 128],
                           xsbc[:, o + pc * 128: o + (pc + 1) * 128],
                           dcol[:, i * NXS + pc: i * NXS + pc + 1],
                           yb[:, ycol:ycol + 128], op0=AL.mult, op1=AL.add),
                       [f"py{pl // 4}", XS_R, f"dcol{d}", f"ysb{ring}"],
                       [f"ysb{ring}"], "Y ev")
                for bb in range(2):
                    st(V, lambda e, bb=bb, hh=hh: e.tensor_copy(
                        hstate[64:128, hh * HH * HP + bb * 512:
                               hh * HH * HP + (bb + 1) * 512],
                        ph_[bb][64:128, 0:512]),
                       [f"ph{bb}", "hstate"], ["hstate"], "h ev")

            def stage_store(ci):
                c_glob = sp * NCQ + ci
                ring = c_glob % 2
                dma("sync", yscr[di, :, :, c_glob * Q:(c_glob + 1) * Q].transpose([1, 0, 2]),
                    ysb[:, ring * 2048:(ring + 1) * 2048].rearrange("p (c t) -> p c t", c=NXS),
                    [f"ysb{ring}"], [f"yscr{di}_{c_glob}"])

            stage_A(chunk_order[0])
            stage_A_post(chunk_order[0])
            stage_B(chunk_order[0], 0)
            for idx, ci in enumerate(chunk_order):
                stage_C_mm(ci, 0)
                if idx + 1 < len(chunk_order):
                    stage_A(chunk_order[idx + 1])
                stage_B(ci, 1)
                stage_C_ev(ci, 0)
                if idx + 1 < len(chunk_order):
                    stage_A_post(chunk_order[idx + 1])
                stage_C_mm(ci, 1)
                if idx + 1 < len(chunk_order):
                    stage_B(chunk_order[idx + 1], 0)
                stage_C_ev(ci, 1)
                stage_store(ci)

    def pass_b(d):
        di = 0 if d == "f" else 1
        for k in range(8):
            dma("gpsimd", opstage[:, k * 2048:(k + 1) * 2048],
                dr[f"op_{d}"][k * 128:(k + 1) * 128, :], ["opstage"], ["opstage", f"op_{k}"])
        phr = [phring[:, 0:1024], phring[:, 1024:2048]]
        for grp in range(4):
            for k in range(8):
                sl = k % 2
                dma("gpsimd", phr[sl], dr[f"ph_{d}"][k * 128:(k + 1) * 128, :],
                    [f"phr{sl}"], [f"phr{sl}"])
                for fci in range(4):
                    fc = grp * 4 + fci
                    for half in range(2):
                        bi = fci * 2 + half
                        pb = PB8[bi]
                        pres = PB8N[bi]

                        def em_we(e, fc=fc, half=half, pb=pb, k=k, sl=sl):
                            return e.matmul(
                                pb[:, 0:512],
                                opstage[:, k * 2048 + fc * 128: k * 2048 + (fc + 1) * 128],
                                phr[sl][:, half * 512:(half + 1) * 512],
                                start=(k == 0), stop=(k == 7))
                        st(T, em_we, [f"op_{k}", f"phr{sl}"] + ([pres] if k == 0 else []),
                           [pres], "Weff mm")
            for fci in range(4):
                fc = grp * 4 + fci
                for half in range(2):
                    bi = fci * 2 + half
                    st(V, lambda e, fc=fc, half=half, pb=PB8[bi], i=di: e.tensor_scalar(
                        weff[:, fc * D + half * 512: fc * D + (half + 1) * 512],
                        pb[:, 0:512], rmscol[:, i * NXS + fc: i * NXS + fc + 1],
                        None, op0=AL.mult),
                       [PB8N[bi], f"rmscol{d}", "weff"], ["weff"], "Weff ev")
        for k in range(8):
            dma("gpsimd", wzb[:, k * 2048:(k + 1) * 2048],
                dr[f"wz_{d}"][k * 128:(k + 1) * 128, :],
                ["wzb", "xbc", "xsbc0", "xsbc1", "bctm0", "bctm1", "cbm", "bbar", "aqi"],
                ["wzb", f"wz_{k}"])
        for sp in range(NSUP):
            dma("sync", xsup[:].rearrange("p (c t) -> p c t", c=8),
                xfm_scr[:, :, sp * SUP:(sp + 1) * SUP].transpose([1, 0, 2]),
                XFM + ["xsup"], ["xsup"])
            dma("sync", ysup[:].rearrange("p (c t) -> p c t", c=NXS),
                yscr[di, :, :, sp * SUP:(sp + 1) * SUP].transpose([1, 0, 2]),
                [f"yscr{di}_{sp * NCQ + i}" for i in range(NCQ)] +
                ["ysup", "convt", "delt", "alin", "dtraw", "dtsp", "sptmp", "lgfm", "dtlg"],
                ["ysup"])
            for fc in range(NXS):
                pb = pmm[fc % 2]
                pres = f"pmm{fc % 2}"

                def em_z(e, fc=fc, pb=pb):
                    last = None
                    for k in range(8):
                        last = e.matmul(pb[:, 0:SUP],
                                        wzb[:, k * 2048 + fc * 128: k * 2048 + (fc + 1) * 128],
                                        xsup[:, k * SUP:(k + 1) * SUP],
                                        start=(k == 0), stop=(k == 7))
                    return last
                st(T, em_z, [f"wz_{k}" for k in range(8)] + ["xsup", pres], [pres], "z mm")
                st(S, lambda e, fc=fc, pb=pb: e.activation(
                    zsil[:, fc * SUP:(fc + 1) * SUP], pb[:, 0:SUP], AF.Silu),
                   [pres, "zsil"], ["zsil"], "z silu")
            st(V, lambda e: e.tensor_tensor(ygat[:], ysup[:], zsil[:], op=AL.mult),
               ["ysup", "zsil", "ygat", "phr0", "phr1"], ["ygat"], "gate")
            st(S, lambda e: e.activation(sqb[:], ygat[:], AF.Square),
               ["ygat", "ysup"], ["ysup"], "sq")

            def em_rms(e):
                last = None
                for fc in range(NXS):
                    last = e.matmul(pst[0:1, 0:SUP], ONES,
                                    sqb[:, fc * SUP:(fc + 1) * SUP],
                                    start=(fc == 0), stop=(fc == NXS - 1))
                return last
            st(T, em_rms, ["ysup", "ones", "pst"], ["pst"], "rms sum")
            st(V, lambda e: e.tensor_scalar(rstdr[0:1, 0:SUP], pst[0:1, 0:SUP],
                                            1.0 / 2048.0, EPS, op0=AL.mult, op1=AL.add),
               ["pst", "rstdr"], ["rstdr"], "rms pre")
            dma("sync", rs_scr[di, sp, :, :], rstdr[0:1, 0:SUP], ["rstdr"], [f"rs{di}{sp}"])
            dma("sync", rstdr[:], rs_scr[di, sp, :, :].partition_broadcast(128),
                [f"rs{di}{sp}", "rstdr"], ["rstdr"])
            st(V, lambda e: e.reciprocal(rstdr[:], rstdr[:]), ["rstdr"], ["rstdr"], "recip")
            st(S, lambda e: e.activation(rstdr[:], rstdr[:], AF.Sqrt), ["rstdr"], ["rstdr"], "sqrt")
            for dm in range(8):
                pb = pmm[dm % 2]
                pres = f"pmm{dm % 2}"

                def em_op(e, dm=dm, pb=pb):
                    last = None
                    for fc in range(NXS):
                        last = e.matmul(pb[:, 0:SUP],
                                        weff[:, fc * D + dm * 128: fc * D + (dm + 1) * 128],
                                        ygat[:, fc * SUP:(fc + 1) * SUP],
                                        start=(fc == 0), stop=(fc == NXS - 1))
                    return last
                st(T, em_op, ["weff", "ygat", pres], [pres], "oproj mm")
                st(V, lambda e, pb=pb: e.tensor_tensor(
                    evtmp[:], pb[:, 0:SUP], rstdr[:], op=AL.mult),
                   [pres, "rstdr", "evtmp"], ["evtmp"], "oproj ev")
                st(V, lambda e, dm=dm, sp=sp: e.tensor_tensor(
                    acc[:, dm * L + sp * SUP: dm * L + (sp + 1) * SUP],
                    acc[:, dm * L + sp * SUP: dm * L + (sp + 1) * SUP],
                    evtmp[:], op=AL.add),
                   ["evtmp", "acc"], ["acc"], "acc add")

    def epilogue(units=()):
        uit = iter(units)
        dma("sync", lnrep[:, 0:D], dr["lnw_a"].partition_broadcast(128),
            ["lnrep"], ["lnrep"])
        dma("sync", lnrep[:, D:2 * D], dr["lnb_a"].partition_broadcast(128),
            ["lnrep"], ["lnrep"])
        for tc in range(16):
            for dc in range(8):
                dma("scalar", etm[:, dc * 128:(dc + 1) * 128],
                    acc[:, dc * L + tc * 128: dc * L + (tc + 1) * 128],
                    ["acc", "etm"], ["etm"], tp=True)
            dma("sync", xtmf[:], x_in[tc * 128:(tc + 1) * 128, :], ["xtmf"], ["xtmf"])
            st(V, lambda e: e.tensor_reduce(epst[:, 0:1], etm[:, 0:D], axis=AX.X, op=AL.add),
               ["etm", "epst"], ["epst"])
            st(S, lambda e: e.activation(ep1[:], etm[:, 0:D], AF.Square),
               ["etm", "ep1"], ["ep1"])
            st(V, lambda e: e.tensor_reduce(epst[:, 1:2], ep1[:], axis=AX.X, op=AL.add),
               ["ep1", "epst"], ["epst"])
            st(V, lambda e: e.tensor_scalar(epst[:, 2:3], epst[:, 0:1], 1.0 / D, None,
                                            op0=AL.mult), ["epst"], ["epst"])
            st(V, lambda e: e.tensor_tensor(epst[:, 3:4], epst[:, 2:3], epst[:, 2:3],
                                            op=AL.mult), ["epst"], ["epst"])
            st(V, lambda e: e.tensor_scalar(epst[:, 4:5], epst[:, 1:2], 1.0 / D, EPS,
                                            op0=AL.mult, op1=AL.add), ["epst"], ["epst"])
            st(V, lambda e: e.tensor_tensor(epst[:, 4:5], epst[:, 4:5], epst[:, 3:4],
                                            op=AL.subtract), ["epst"], ["epst"])
            st(V, lambda e: e.reciprocal(epst[:, 5:6], epst[:, 4:5]), ["epst"], ["epst"])
            st(S, lambda e: e.activation(epst[:, 6:7], epst[:, 5:6], AF.Sqrt),
               ["epst"], ["epst"])
            st(V, lambda e: e.tensor_scalar(ep1[:], etm[:, 0:D], epst[:, 2:3], epst[:, 6:7],
                                            op0=AL.subtract, op1=AL.mult),
               ["etm", "ep1"], ["ep1"], sreads=["epst"], tag="ln1")
            st(V, lambda e: e.tensor_tensor(ep1[:], ep1[:], lnrep[:, 0:D], op=AL.mult),
               ["ep1", "lnrep"], ["ep1"])
            st(V, lambda e: e.tensor_tensor(ep1[:], ep1[:], lnrep[:, D:2 * D], op=AL.add),
               ["ep1", "lnrep"], ["ep1"])
            st(V, lambda e: e.tensor_tensor(ep1[:], ep1[:], xtmf[:], op=AL.add),
               ["ep1", "xtmf"], ["ep1"], "h1")
            st(V, lambda e: e.tensor_reduce(epst[:, 8:9], ep1[:], axis=AX.X, op=AL.add),
               ["ep1", "epst"], ["epst"])
            st(S, lambda e: e.activation(xtmf[:], ep1[:], AF.Square),
               ["ep1", "xtmf"], ["xtmf"])
            st(V, lambda e: e.tensor_reduce(epst[:, 9:10], xtmf[:], axis=AX.X, op=AL.add),
               ["xtmf", "epst"], ["epst"])
            st(V, lambda e: e.tensor_scalar(epst[:, 10:11], epst[:, 8:9], 1.0 / D, None,
                                            op0=AL.mult), ["epst"], ["epst"])
            st(V, lambda e: e.tensor_tensor(epst[:, 11:12], epst[:, 10:11], epst[:, 10:11],
                                            op=AL.mult), ["epst"], ["epst"])
            st(V, lambda e: e.tensor_scalar(epst[:, 12:13], epst[:, 9:10], 1.0 / D, EPS,
                                            op0=AL.mult, op1=AL.add), ["epst"], ["epst"])
            st(V, lambda e: e.tensor_tensor(epst[:, 12:13], epst[:, 12:13], epst[:, 11:12],
                                            op=AL.subtract), ["epst"], ["epst"])
            st(V, lambda e: e.reciprocal(epst[:, 13:14], epst[:, 12:13]), ["epst"], ["epst"])
            st(S, lambda e: e.activation(epst[:, 14:15], epst[:, 13:14], AF.Sqrt),
               ["epst"], ["epst"])
            st(V, lambda e: e.tensor_scalar(ep1[:], ep1[:], epst[:, 10:11], epst[:, 14:15],
                                            op0=AL.subtract, op1=AL.mult),
               ["ep1"], ["ep1"], sreads=["epst"], tag="x1")
            dma("sync", x1_scr[tc * 128:(tc + 1) * 128, :], ep1[:], ["ep1"], [f"x1s{tc}"])
            for _ in range(4):
                u = next(uit, None)
                if u is not None:
                    u()
        for u in uit:
            u()

    def quant(src_ap, width, res_in, dst_ap, dst_res, tag, ceng=V):
        nb = width // 16
        dump_fsc = tag == f"hq{DBGTC}_0" and "fsc0" in KDBG
        dump_q5 = tag == f"hq{DBGTC}_5" and "q5" in KDBG
        st(V, lambda e: e.tensor_scalar(fsgn[:, 0:width], src_ap, 0.0, 2.0,
                                        op0=AL.is_ge, op1=AL.mult),
           list(res_in) + ["fsgn"], ["fsgn"], tag + " sgn0")
        st(V, lambda e: e.scalar_tensor_tensor(ftA[:, 0:width], fsgn[:, 0:width],
                                               -1.0, src_ap, op0=AL.add, op1=AL.mult),
           list(res_in) + ["fsgn", "ftA"], ["ftA"], tag + " abs")

        av = ftA[:, 0:nb * 16].rearrange("p (b w) -> p b w", w=16)
        r8 = ftC[:, 0:nb * 8].rearrange("p (b w) -> p b w", w=8)
        r4 = ftC[:, nb * 8:nb * 12].rearrange("p (b w) -> p b w", w=4)
        r2 = ftC[:, nb * 12:nb * 14].rearrange("p (b w) -> p b w", w=2)
        st(V, lambda e, nb=nb, av=av: e.tensor_tensor(
            ftC[:, 0:nb * 8], av[:, :, 0:8], av[:, :, 8:16], op=AL.max),
           ["ftA", "ftC"], ["ftC"], tag + " bm0")
        st(V, lambda e, nb=nb, r8=r8: e.tensor_tensor(
            ftC[:, nb * 8:nb * 12], r8[:, :, 0:4], r8[:, :, 4:8], op=AL.max),
           ["ftC"], ["ftC"], tag + " bm1")
        st(V, lambda e, nb=nb, r4=r4: e.tensor_tensor(
            ftC[:, nb * 12:nb * 14], r4[:, :, 0:2], r4[:, :, 2:4], op=AL.max),
           ["ftC"], ["ftC"], tag + " bm2")
        st(V, lambda e, nb=nb, r2=r2: e.tensor_tensor(
            fsc[:, 0:nb], r2[:, :, 0], r2[:, :, 1], op=AL.max),
           ["ftC", "fsc"], ["fsc"], tag + " bm3")
        if dump_q5:
            dump("q5ftA", ftA[:], ["ftA"], F32, (128, 1024))
            dump("q5fscraw", fsc[:], ["fsc"], F32, (128, 192))
        st(V, lambda e: e.tensor_scalar(fsc[:, 64:64 + nb], fsc[:, 0:nb], 6e-12, None,
                                        op0=AL.max),
           ["fsc"], ["fsc"], tag + " clamp")
        st(V, lambda e: e.reciprocal(fsc[:, 128:128 + nb], fsc[:, 64:64 + nb]),
           ["fsc"], ["fsc"], tag + " recip")
        if dump_fsc:
            dump("fsc0", fsc[:], ["fsc"], F32, (128, 192))
        st(V, lambda e: e.tensor_tensor(
            ftB[:, 0:width].rearrange("p (b w) -> p b w", w=16),
            ftA[:, 0:width].rearrange("p (b w) -> p b w", w=16),
            bcast_f(fsc[:, 128:128 + nb], 16), op=AL.mult),
           ["ftA", "fsc", "ftB"], ["ftB"], tag + " xn")

        if ceng == V:
            def em_cmp(e):
                a = ftB[:, 0:width]
                g = ftC[:, 0:width]
                u = ftA[:, 0:width]
                e.tensor_scalar(g, a, TAU6[0], None, op0=AL.is_gt)
                for i in (1, 2, 3):
                    e.scalar_tensor_tensor(g, a, TAU6[i], g, op0=AL.is_gt, op1=AL.add)
                e.tensor_scalar(u, a, TAU6[4], None, op0=AL.is_gt)
                e.scalar_tensor_tensor(u, a, TAU6[5], u, op0=AL.is_gt, op1=AL.add)
                e.scalar_tensor_tensor(g, g, 0.5, u, op0=AL.mult, op1=AL.add)
                e.tensor_scalar(u, a, TAU6[6], 2.0, op0=AL.is_gt, op1=AL.mult)
                return e.tensor_tensor(g, g, u, op=AL.add)
        else:
            def em_cmp(e):
                a = ftB[:, 0:width]
                g = ftC[:, 0:width]
                u = ftA[:, 0:width]
                e.tensor_scalar(g, a, TAU6[0], 0.5, op0=AL.is_gt, op1=AL.mult)
                for i, w in ((1, 0.5), (2, 0.5), (3, 0.5), (4, 1.0), (5, 1.0), (6, 2.0)):
                    e.tensor_scalar(u, a, TAU6[i], w, op0=AL.is_gt, op1=AL.mult)
                    e.tensor_tensor(g, g, u, op=AL.add)
                return e.engine_nop()
        st(ceng, em_cmp, ["ftB", "ftA", "ftC"], ["ftC", "ftA"], tag + " cmp")
        st(V, lambda e: e.scalar_tensor_tensor(
            ftB[:, 0:width].rearrange("p (b w) -> p b w", w=16),
            ftC[:, 0:width].rearrange("p (b w) -> p b w", w=16),
            W6, bcast_f(fsc[:, 64:64 + nb], 16), op0=AL.mult, op1=AL.mult),
           ["ftC", "fsc", "ftB"], ["ftB"], tag + " scale")
        st(V, lambda e: e.scalar_tensor_tensor(
            dst_ap[:, 0:width] if dst_ap.shape[1] != width else dst_ap,
            fsgn[:, 0:width], -1.0, ftB[:, 0:width], op0=AL.add, op1=AL.mult),
           ["ftB", "fsgn", dst_res], [dst_res], tag + " q")

    wq_ping = [0]
    fstgr = [fstage, ARD[:, 11648:13696].bitcast(F32)]

    def wq_unit_gu(wname, pc):
        hbm = dr["wg"] if wname == "g" else dr["wu"]
        wregs = wq_g if wname == "g" else wq_u
        ID32 = masks[:, 256:384]
        sl = wq_ping[0] % 2
        wq_ping[0] += 1
        FS = "fstg" if sl == 0 else "fstg1"
        dma("synck", fstgr[sl][:, 0:1024], hbm[pc * 128:(pc + 1) * 128, :],
            [FS], [FS])
        quant(fstgr[sl][:, 0:1024], 1024, [FS], ftA, "ftA", f"wq{wname}{pc}")
        for kc in range(8):
            bk = py[kc % 2]
            bn = f"py{kc % 2}"
            col = (kc // 2) * 128
            st(T, lambda e, bk=bk, col=col, kc=kc: e.transpose(
                bk[:, col:col + 128], ftA[:, kc * 128:(kc + 1) * 128], ID32),
               ["ftA", "idf32", bn], [bn], "wtp")
            st(S, lambda e, bk=bk, col=col, kc=kc, pc=pc, wregs=wregs: e.activation(
                wregs[kc][:, pc * 128:(pc + 1) * 128], bk[:, col:col + 128], AF.Copy),
               [bn, f"wq{wname}_{pc}_{kc}"], [f"wq{wname}_{pc}_{kc}"], "wtpev")

    def wq_unit_d(pc, ss, wdt):
        ID32 = masks[:, 256:384]
        sl = wq_ping[0] % 2
        wq_ping[0] += 1
        FS = "fstg" if sl == 0 else "fstg1"
        dma("synck", fstgr[sl][:, 0:wdt],
            dr["wdn"][pc * 128:(pc + 1) * 128, ss:ss + wdt], [FS], [FS])
        quant(fstgr[sl][:, 0:wdt], wdt, [FS], ftA, "ftA", f"wqd{pc}_{ss}")
        for ic in range(wdt // 128):
            icg = ss // 128 + ic
            bk = py[ic % 2]
            bn = f"py{ic % 2}"
            col = (ic // 2) * 128
            st(T, lambda e, bk=bk, col=col, ic=ic: e.transpose(
                bk[:, col:col + 128], ftA[:, ic * 128:(ic + 1) * 128], ID32),
               ["ftA", "idf32", bn], [bn], "wtp")
            st(S, lambda e, bk=bk, col=col, icg=icg, pc=pc: e.activation(
                wq_d[icg][:, pc * 128:(pc + 1) * 128], bk[:, col:col + 128], AF.Copy),
               [bn, f"wqd_{pc}_{icg}"], [f"wqd_{pc}_{icg}"], "wtpev")

    def wq_safe_units():
        us = []
        for wname in ("g", "u"):
            for pc in range(22):
                us.append(lambda wname=wname, pc=pc: wq_unit_gu(wname, pc))
        for pc in range(8):
            us.append(lambda pc=pc: wq_unit_d(pc, 0, 1024))
        return us

    def ffn():
        for pc in range(8):
            for ss, wdt in ((1024, 1024), (2048, 768)):
                wq_unit_d(pc, ss, wdt)
        dump("wqg0", wq_g[0][:], [f"wqg_{pc}_0" for pc in range(22)], BF16, (128, 2816))
        WQG = [f"wq{w}_{pc}_{kc}" for w in "gu" for pc in range(22) for kc in range(8)]
        WQD = [f"wqd_{pc}_{icg}" for pc in range(8) for icg in range(22)]
        ISL = [(0, 512), (512, 512), (1024, 512), (1536, 512), (2048, 512), (2560, 256)]
        NH2 = 2
        gb = [pmm[0], pcb]
        ub = [pmm[1], pst]
        gbn = ["pmm0", "pcb"]
        ubn = ["pmm1", "pst"]

        def x1_load(tc):
            sl = tc % 2
            wr = ["fstg"] if sl == 1 else ["fstg1"]
            dma("sync", ep2fr[sl][:], x1_scr[tc * 128:(tc + 1) * 128, :],
                [f"ep2f{sl}"] + wr, [f"ep2f{sl}"] + wr)

        def ffn_mm(isl, which):
            io, iw = ISL[isl]
            b = (gb if which == "g" else ub)[isl % 2]
            bn = (gbn if which == "g" else ubn)[isl % 2]
            wq = wq_g if which == "g" else wq_u

            def em(e, io=io, iw=iw, b=b, wq=wq):
                last = None
                for kc in range(8):
                    last = e.matmul(b[:, 0:iw],
                                    xqfm[:, kc * 128:(kc + 1) * 128],
                                    wq[kc][:, io:io + iw],
                                    start=(kc == 0), stop=(kc == 7))
                return last
            st(T, em, [f"xqf{k}" for k in range(8)] + WQG + [bn], [bn],
               which + "mm")

        def ffn_act(tc, isl, cj):
            io, iw = ISL[isl]
            ep = ephr[isl % 2]
            EPR = f"eph{isl % 2}"
            st(S, lambda e, iw=iw, ep=ep: e.activation(ep[:, 0:iw],
                                                       gb[isl % 2][:, 0:iw], AF.Silu),
               [gbn[isl % 2], EPR], [EPR], "fsilu")
            st(V, lambda e, iw=iw, ep=ep: e.tensor_tensor(ep[:, 0:iw], ep[:, 0:iw],
                                                          ub[isl % 2][:, 0:iw], op=AL.mult),
               [ubn[isl % 2], EPR], [EPR], "hmul")
            quant(ep[:, 0:iw], iw, [EPR], hqtm[:, io:io + iw], "hqtm", f"hq{tc}_{isl}")
            for ic in range(io // 128, (io + iw) // 128):
                dma("scalar", hqfm[:, ic * NH2 * 128 + cj * 128:
                                   ic * NH2 * 128 + (cj + 1) * 128],
                    hqtm[:, ic * 128:(ic + 1) * 128],
                    ["hqtm", f"hqf{cj}_{ic}"], [f"hqf{cj}_{ic}"], tp=True)

        def ffn_dn(grp):
            HQF = [f"hqf{cj}_{ic}" for cj in range(NH2) for ic in range(22)]
            for dm in range(8):
                pb = pmm[dm % 2]
                pres = f"pmm{dm % 2}"

                def em_dn(e, dm=dm, pb=pb):
                    last = None
                    for ic in range(22):
                        last = e.matmul(pb[:, 0:NH2 * 128],
                                        wq_d[ic][:, dm * 128:(dm + 1) * 128],
                                        hqfm[:, ic * NH2 * 128:(ic + 1) * NH2 * 128],
                                        start=(ic == 0), stop=(ic == 21))
                    return last
                st(T, em_dn, HQF + WQD + [pres], [pres], "dn mm")
                st(V, lambda e, pb=pb: e.tensor_copy(dtmp[:, 0:NH2 * 128], pb[:, 0:NH2 * 128]),
                   [pres, "dtmp"], ["dtmp"], "dn ev")
                for cj in range(NH2):
                    dma("scalar", dntm[:, cj * D + dm * 128: cj * D + (dm + 1) * 128],
                        dtmp[:, cj * 128:(cj + 1) * 128],
                        ["dtmp", f"dntp{cj}_{dm}"], [f"dntp{cj}_{dm}"], tp=True)
            if grp == DBGTC // 2:
                dump("dn0", dntm[:, (DBGTC % 2) * D:(DBGTC % 2) * D + D],
                     [f"dntp{DBGTC % 2}_{dm}" for dm in range(8)], BF16, (128, D))

        def ffn_ln2(grp):
            for cj in range(NH2):
                tc = grp * NH2 + cj
                DNT = [f"dntp{cj}_{dm}" for dm in range(8)]
                dma("scalar", ep1f[:], x1_scr[tc * 128:(tc + 1) * 128, :],
                    ["ep1f"], ["ep1f"])
                st(V, lambda e, cj=cj: e.tensor_tensor(
                    ep1f[:], ep1f[:], dntm[:, cj * D:(cj + 1) * D], op=AL.add),
                   ["ep1f"] + DNT, ["ep1f"], "h2")
                st(V, lambda e: e.tensor_reduce(epst[:, 0:1], ep1f[:], axis=AX.X, op=AL.add),
                   ["ep1f", "epst"], ["epst"])
                st(S, lambda e: e.activation(ftB[:, 0:D], ep1f[:], AF.Square),
                   ["ep1f", "ftB"], ["ftB"])
                st(V, lambda e: e.tensor_reduce(epst[:, 1:2], ftB[:, 0:D], axis=AX.X, op=AL.add),
                   ["ftB", "epst"], ["epst"])
                st(V, lambda e: e.tensor_scalar(epst[:, 2:3], epst[:, 0:1], 1.0 / D, None,
                                                op0=AL.mult), ["epst"], ["epst"])
                st(V, lambda e: e.tensor_tensor(epst[:, 3:4], epst[:, 2:3], epst[:, 2:3],
                                                op=AL.mult), ["epst"], ["epst"])
                st(V, lambda e: e.tensor_scalar(epst[:, 4:5], epst[:, 1:2], 1.0 / D, EPS,
                                                op0=AL.mult, op1=AL.add), ["epst"], ["epst"])
                st(V, lambda e: e.tensor_tensor(epst[:, 4:5], epst[:, 4:5], epst[:, 3:4],
                                                op=AL.subtract), ["epst"], ["epst"])
                st(V, lambda e: e.reciprocal(epst[:, 5:6], epst[:, 4:5]), ["epst"], ["epst"])
                st(S, lambda e: e.activation(epst[:, 6:7], epst[:, 5:6], AF.Sqrt),
                   ["epst"], ["epst"])
                st(V, lambda e: e.tensor_scalar(ep1f[:], ep1f[:], epst[:, 2:3], epst[:, 6:7],
                                                op0=AL.subtract, op1=AL.mult),
                   ["ep1f"], ["ep1f"], sreads=["epst"], tag="x2")
                dma("sync", out[tc * 128:(tc + 1) * 128, :], ep1f[:], ["ep1f"], [f"o{tc}"])

        x1_load(0)
        for grp in range(L // (NH2 * 128)):
            for cj in range(NH2):
                tc = grp * NH2 + cj
                sl = tc % 2
                quant(ep2fr[sl][:], D, [f"ep2f{sl}"], xqtm, "xqtm", f"x1q{tc}")
                if tc + 1 < 16:
                    x1_load(tc + 1)
                if tc == DBGTC:
                    dump("xq0", xqtm[:], ["xqtm"], BF16, (128, D))
                for kc in range(8):
                    dma("scalar", xqfm[:, kc * 128:(kc + 1) * 128],
                        xqtm[:, kc * 128:(kc + 1) * 128], ["xqtm", f"xqf{kc}"],
                        [f"xqf{kc}"], tp=True)
                ffn_mm(0, "g")
                ffn_mm(0, "u")
                for isl in range(len(ISL)):
                    if isl + 1 < len(ISL):
                        ffn_mm(isl + 1, "g")
                        ffn_mm(isl + 1, "u")
                    ffn_act(tc, isl, cj)
                if tc == DBGTC:
                    dump("hq0", hqtm[:], ["hqtm"], BF16, (128, 2816))
            if grp > 0:
                ffn_ln2(grp - 1)
            ffn_dn(grp)
        ffn_ln2(L // (NH2 * 128) - 1)

    p.fence()
    pass_a("f")
    dump("yscrf", yscr[0], [f"yscr0_{c}" for c in range(16)], BF16, (NXS, 128, L))
    dump("lgscrf", lg_scr[0], [f"lgscr0_{c}" for c in range(16)], F32, (16, 32, Q))
    p.fence()
    pass_b("f")
    dump("accf", acc[:], ["acc"], BF16, (128, 8 * L))
    p.fence()
    pass_a("b")
    p.fence()
    pass_b("b")
    dump("yscrb", yscr[1], [f"yscr1_{c}" for c in range(16)], BF16, (NXS, 128, L))
    dump("lgscrb", lg_scr[1], [f"lgscr1_{c}" for c in range(16)], F32, (16, 32, Q))
    dump("accb", acc[:], ["acc"], BF16, (128, 8 * L))
    p.fence()
    epilogue(wq_safe_units())
    dump("x1", x1_scr, [f"x1s{tc}" for tc in range(16)], F32, (L, D))
    dump("xfm", xfm_scr, XFM, BF16, (8, 128, L))
    p.fence()
    ffn()
    st(G, lambda e: e.engine_nop(), [f"o{tc}" for tc in range(16)], [], "final")

    with (
        nc.semaphore("s_sync") as s0,
        nc.semaphore("s_scalar") as s1,
        nc.semaphore("s_vector") as s2,
        nc.semaphore("s_tensor") as s3,
        nc.semaphore("s_gpsimd") as s4,
        nc.Block() as blk,
    ):
        sems = dict(zip(ENGS, (s0, s1, s2, s3, s4)))

        def body_for(name):
            def body(e):
                my = sems[name]
                for eng, waits, emit, inc, tag in p.steps:
                    if eng != name:
                        continue
                    for s, v in waits.items():
                        e.wait_ge(sems[s], v)
                    last = emit(e)
                    last.then_inc(my, inc)
            return body

        blk.sync(body_for("sync"))
        blk.scalar(body_for("scalar"))
        blk.vector(body_for("vector"))
        blk.tensor(body_for("tensor"))
        blk.gpsimd(body_for("gpsimd"))

    for t in reversed(ctxs):
        t.__exit__(None, None, None)
    return nc, dbg_outs


def host_inputs(inputs, b):
    g = lambda k: np.asarray(inputs[k])
    f32 = np.float32
    m = {}
    m["x"] = np.ascontiguousarray(g("L0_attn_x")[b], f32)
    m["xT"] = np.ascontiguousarray(m["x"].T)
    for d in ("f", "b"):
        W = np.ascontiguousarray(g(f"L0_attn_{d}_in_proj_w"), f32)
        WT = np.ascontiguousarray(W.T)
        wa = np.concatenate([
            WT[:, 2048:4224],
            WT[:, 4160:4224],
            WT[:, 4224:4256],
            np.zeros((D, 32), f32),
        ], axis=1)
        m[f"wa_{d}"] = np.ascontiguousarray(wa)
        m[f"wz_{d}"] = np.ascontiguousarray(WT[:, 0:2048])
        m[f"op_{d}"] = np.ascontiguousarray(g(f"L0_attn_{d}_out_proj_w"), f32)
        half = slice(0, D) if d == "f" else slice(D, 2 * D)
        m[f"ph_{d}"] = np.ascontiguousarray(g("L0_attn_proj_w")[:, half].T, f32)
        cw = np.asarray(g(f"L0_attn_{d}_conv_w"), f32)[:, 0, :]
        cb = np.asarray(g(f"L0_attn_{d}_conv_b"), f32)
        cwc = np.zeros((128, NXBC * 4), f32)
        cbc = np.zeros((128, NXBC), f32)
        for ch in range(2176):
            c, pp = divmod(ch, 128)
            cwc[pp, c * 4:(c + 1) * 4] = cw[:, ch]
            cbc[pp, c] = cb[ch]
        for j in range(64):
            ch = 2112 + j
            cwc[j, 17 * 4:18 * 4] = cw[:, ch]
            cbc[j, 17] = cb[ch]
        m[f"convw_{d}"] = cwc
        m[f"convb_{d}"] = cbc
        m[f"dtb_{d}"] = np.ascontiguousarray(g(f"L0_attn_{d}_dt_bias").reshape(32, 1), f32)
        m[f"alog_{d}"] = np.ascontiguousarray(g(f"L0_attn_{d}_A_log").reshape(32, 1), f32)
        Dp = np.asarray(g(f"L0_attn_{d}_D"), f32)
        dc = np.zeros((128, NXS), f32)
        for pc in range(NXS):
            dc[0:64, pc] = Dp[2 * pc]
            dc[64:128, pc] = Dp[2 * pc + 1]
        m[f"dcol_{d}"] = dc
        rw = np.asarray(g(f"L0_attn_{d}_rms_w"), f32)
        rc = np.zeros((128, NXS), f32)
        for fc in range(NXS):
            rc[:, fc] = rw[fc * 128:(fc + 1) * 128]
        m[f"rmscol_{d}"] = rc
    pb = np.asarray(g("L0_attn_proj_b"), f32)
    pbc = np.zeros((128, 8), f32)
    for dcx in range(8):
        pbc[:, dcx] = pb[dcx * 128:(dcx + 1) * 128]
    m["pbcol"] = pbc
    m["lnw_a"] = np.ascontiguousarray(g("L0_attn_ln_w").reshape(1, D), f32)
    m["lnb_a"] = np.ascontiguousarray(g("L0_attn_ln_b").reshape(1, D), f32)
    m["wg"] = np.ascontiguousarray(g("L0_ffn_w_gate"), f32)
    m["wu"] = np.ascontiguousarray(g("L0_ffn_w_up"), f32)
    m["wdn"] = np.ascontiguousarray(g("L0_ffn_w_down"), f32)
    si = np.arange(128)[:, None]
    ti = np.arange(128)[None, :]
    m["maskF"] = (si <= ti).astype(f32)
    m["maskB"] = (si >= ti).astype(f32)
    m["idf32"] = np.eye(128, dtype=f32)
    m["ones_bf"] = np.ones((128, 1), np.float32).astype(ml_dtypes.bfloat16)
    ih = np.zeros((128, HP), np.float32)
    ih[64:128, :] = np.eye(64, dtype=np.float32)
    m["i64hi"] = ih.astype(ml_dtypes.bfloat16)
    return m


def kernel(**inputs):
    nc, _ = build_program()
    in_maps = [host_inputs(inputs, b) for b in range(8)]
    res = run_bass_kernel_spmd(nc, in_maps, core_ids=list(range(8)))
    return np.stack([res.results[b]["out"] for b in range(8)], axis=0)


# version 23
# speedup vs baseline: 1.0343x; 1.0343x over previous
import numpy as np
from collections import defaultdict

import concourse.bass as bass
import concourse.mybir as mybir
from concourse.bass_utils import run_bass_kernel_spmd
import ml_dtypes

F32 = mybir.dt.float32
BF16 = mybir.dt.bfloat16
AL = mybir.AluOpType
AF = mybir.ActivationFunctionType
AX = mybir.AxisListType

L = 2048
D = 1024
Q = 128
SUP = 512
NSUP = L // SUP
NCQ = SUP // Q
NH = 32
HP = 64
NXS = 16
NXBC = 18
WA_COLS = NXBC * 128
HH = 16
EPS = 1e-5
W6 = float(np.float32(1.0) / np.float32(6.0))
TAU6 = [float(np.float32(t) / np.float32(6.0)) for t in (0.25, 0.75, 1.25, 1.75, 2.5, 3.5, 5.0)]
ENGS = ("sync", "scalar", "vector", "tensor", "gpsimd")
import os
KDBG = set(x for x in os.environ.get("KDBG", "").split(",") if x)
DBGTC = int(os.environ.get("DBGTC", "0"))


class Prog:
    def __init__(self):
        self.steps = []
        self.cnt = defaultdict(int)
        self.res = {}
        self.last_wait = defaultdict(dict)
        self.pending = {}

    def fence(self):
        snap = dict(self.cnt)
        for eng in ENGS:
            pf = self.pending.setdefault(eng, {})
            for o, v in snap.items():
                if o != eng and v > 0 and pf.get(o, -1) < v:
                    pf[o] = v

    def step(self, eng, emit, reads=(), writes=(), dma=False, sreads=(), tag=""):
        waits = {}

        def need(deng, dval, force=False):
            if deng == eng and not dma and not force:
                return
            if waits.get(deng, -1) < dval:
                waits[deng] = dval

        for o, v in self.pending.pop(eng, {}).items():
            need(o, v)
        for r in sreads:
            e = self.res.get(r)
            if e and e["w"]:
                need(*e["w"], force=True)
        for r in reads:
            e = self.res.get(r)
            if e and e["w"]:
                need(*e["w"], force=(e["w"][0] == eng and eng != "tensor"))
        for w in writes:
            e = self.res.get(w)
            if e:
                if e["w"]:
                    need(*e["w"])
                for rr in e["r"]:
                    need(*rr)
        fw = {}
        for s, v in waits.items():
            if self.last_wait[eng].get(s, -1) < v:
                fw[s] = v
                self.last_wait[eng][s] = v
        inc = 16 if dma else 1
        self.cnt[eng] += inc
        self.steps.append((eng, fw, emit, inc, tag))
        dep = (eng, self.cnt[eng])
        for w in writes:
            self.res[w] = {"w": dep, "r": []}
        for r in list(reads) + list(sreads):
            self.res.setdefault(r, {"w": None, "r": []})["r"].append(dep)


def build_program():
    nc = bass.Bass("TRN2", target_bir_lowering=False)
    dr = {}

    def din(name, shape, dt=F32):
        dr[name] = nc.dram_tensor(name, list(shape), dt, kind="ExternalInput").ap()
        return dr[name]

    x_in = din("x", (L, D))
    din("xT", (D, L))
    for d in ("f", "b"):
        din(f"wa_{d}", (D, WA_COLS))
        din(f"wz_{d}", (D, 2048))
        din(f"op_{d}", (D, 2048))
        din(f"ph_{d}", (D, D))
        din(f"convw_{d}", (128, NXBC * 4))
        din(f"convb_{d}", (128, NXBC))
        din(f"dtb_{d}", (32, 1))
        din(f"alog_{d}", (32, 1))
        din(f"dcol_{d}", (128, NXS))
        din(f"rmscol_{d}", (128, NXS))
    din("pbcol", (128, 8))
    din("lnw_a", (1, D))
    din("lnb_a", (1, D))
    din("wg", (2816, D))
    din("wu", (2816, D))
    din("wdn", (D, 2816))
    din("maskF", (128, 128))
    din("maskB", (128, 128))
    din("idf32", (128, 128))
    din("ones_bf", (128, 1), BF16)
    din("i64hi", (128, HP), BF16)

    out = nc.dram_tensor("out", [L, D], F32, kind="ExternalOutput").ap()
    xfm_scr = nc.dram_tensor("xfm_scr", [8, 128, L], BF16, kind="Internal").ap()
    yscr = nc.dram_tensor("yscr", [2, NXS, 128, L], BF16, kind="Internal").ap()
    lg_scr = nc.dram_tensor("lg_scr", [2, 16, 32, Q], F32, kind="Internal").ap()
    x1_scr = nc.dram_tensor("x1_scr", [L, D], F32, kind="Internal").ap()
    rs_scr = nc.dram_tensor("rs_scr", [2, NSUP, 1, SUP], F32, kind="Internal").ap()

    ctxs = []

    def sbuf(name, shape, dt):
        t = nc.sbuf_tensor(name, list(shape), dt)
        h = t.__enter__()
        ctxs.append(t)
        return h

    def psum_t(name):
        t = nc.psum_tensor(name, [128, 512], F32)
        h = t.__enter__()
        ctxs.append(t)
        return h

    consts_b = sbuf("consts_b", [128, 2 + HP], BF16)
    ONES = consts_b[:, 0:1]
    I64HI = consts_b[:, 2:2 + HP]
    masks = sbuf("masks", [128, 3 * 128], F32)
    convw = sbuf("convw", [128, 2 * NXBC * 4], F32)
    convb = sbuf("convb", [128, 2 * NXBC], F32)
    smallf = sbuf("smallf", [128, 16], F32)
    zeros128 = sbuf("zeros128", [128, 128], F32)
    dcol = sbuf("dcol", [128, 2 * NXS], F32)
    rmscol = sbuf("rmscol", [128, 2 * NXS], F32)
    pbcol = sbuf("pbcol_s", [128, 8], F32)
    epst = sbuf("epst", [128, 16], F32)
    hstate = sbuf("hstate", [128, NH * HP], BF16)
    acc = sbuf("acc", [128, 8 * L], BF16)

    ARA = sbuf("arA", [128, 22784], BF16)
    ARB = sbuf("arB", [128, 18432], BF16)
    ARC = sbuf("arC", [128, 18432], BF16)
    ARD = sbuf("arD", [128, 16384], BF16)
    ARE = sbuf("arE", [128, 10240], BF16)

    wab = ARA[:, 0:18432]
    xsup = ARA[:, 18432:22528]
    xbc = ARB[:, 0:9324]
    xsbc = ARB[:, 9344:13952]
    bctm = ARB[:, 13952:14208]
    cbm = ARB[:, 16384:16640]
    bbar = ARB[:, 14336:15360]
    aqi = ARB[:, 15360:16384]
    utm = ARC[:, 0:4096]
    ysb = ARC[:, 4096:8192]
    gam = ARC[:, 8192:10240]
    smeg = ARC[:, 10240:12288]
    almeg = ARC[:, 12288:14336]
    cameg = ARC[:, 14336:16384]
    adf = ARD[:].bitcast(F32)
    convt = adf[:, 0:2304]
    delt = adf[:, 2304:4352]
    alin = adf[:, 4352:6400]
    dtraw = adf[:, 6400:6912]
    dtsp = adf[:, 6912:7424]
    sptmp = adf[:, 7424:7936]
    lgfm = adf[:, 7936:8064]
    dtlg_tm = adf[:, 8064:8192]
    opstage = ARA[:, 0:16384]
    phring = ARA[:, 16384:18432]
    wzb = ARB[:, 0:16384]
    rstdr = ARB[:, 16384:17408].bitcast(F32)
    evtmp = ARB[:, 17408:17920]
    weff = ARC[:, 0:16384]
    ysup = ARD[:, 0:8192]
    zsil = ARD[:, 8192:16384]
    sqb = ysup
    ygat = ARE[:, 0:8192]
    etm = ARE[:, 0:2048]
    xtmf = ARE[:, 2048:4096].bitcast(F32)
    ep1 = ARE[:, 4096:6144].bitcast(F32)
    lnrep = ARE[:, 6144:10240].bitcast(F32)
    wq_g = [ARA[:, i * 2816:(i + 1) * 2816] for i in range(8)]
    wq_u = [ARB[:, i * 2816:(i + 1) * 2816] for i in range(6)] + \
           [ARC[:, i * 2816:(i + 1) * 2816] for i in range(2)]
    wq_d = [ARC[:, 5632 + i * 1024: 5632 + (i + 1) * 1024] for i in range(12)] + \
           [acc[:, i * 1024:(i + 1) * 1024] for i in range(10)]
    ftA = ARD[:, 0:2048].bitcast(F32)
    ftB = ARD[:, 2048:4096].bitcast(F32)
    ftC = ARD[:, 4096:6144].bitcast(F32)
    fsgn = ARD[:, 6144:8192].bitcast(F32)
    fstage = ARD[:, 8192:10240].bitcast(F32)
    fsc = ARD[:, 10240:10624].bitcast(F32)
    ephr = [ARD[:, 10624:11648].bitcast(F32), ARD[:, 13696:14720].bitcast(F32)]
    ep2fr = [ARD[:, 11648:13696].bitcast(F32)[:, 0:1024],
             ARD[:, 8192:10240].bitcast(F32)]
    qout = acc[:, 10240:11264]
    hqtm = acc[:, 11264:14080]
    xqtm = acc[:, 14080:15104]
    xqfm = acc[:, 15104:16128]
    hqfm = ARE[:, 0:5632]
    dntm = ARE[:, 5632:7680]
    ep1f = ARE[:, 7680:9728].bitcast(F32)
    dtmp = ARE[:, 9728:10240]

    pmm = [psum_t("ps_mm0"), psum_t("ps_mm1")]
    pcb = psum_t("ps_cb")
    py = [psum_t("ps_y0"), psum_t("ps_y1")]
    ph_ = [psum_t("ps_h0"), psum_t("ps_h1")]
    pst = psum_t("ps_st")
    PB8 = [pmm[0], pmm[1], pcb, py[0], py[1], ph_[0], ph_[1], pst]
    PB8N = ["pmm0", "pmm1", "pcb", "py0", "py1", "ph0", "ph1", "pst"]

    p = Prog()
    V, S, G, T = "vector", "scalar", "gpsimd", "tensor"
    dbg_outs = []

    def dump(name, src_ap, reads, dt, shape):
        if name not in KDBG:
            return
        o = nc.dram_tensor("dbg_" + name, list(shape), dt, kind="ExternalOutput").ap()
        dbg_outs.append("dbg_" + name)

        def em(e, o=o, s=src_ap):
            return e.dma_start(out=o, in_=s)
        p.step("sync", em, reads=reads, writes=[f"dbg{name}"], dma=True, tag="dump " + name)

    def dma(eng, outap, inap, reads, writes, tag="", tp=False):
        if tp:
            eng = "sync"
        elif eng == "synck":
            eng = "sync"
        elif eng == "sync":
            eng = "scalar"

        def em(e, o=outap, i=inap, tp=tp):
            if tp:
                return e.dma_start(out=o, in_=i, transpose=True)
            return e.dma_start(out=o, in_=i)
        p.step(eng, em, reads=reads, writes=writes, dma=True, tag=tag)

    def st(eng, emit, reads=(), writes=(), sreads=(), tag=""):
        p.step(eng, emit, reads=reads, writes=writes, sreads=sreads, tag=tag)

    def bcast_f(ap2d, inner):
        return ap2d.unsqueeze(2).broadcast_to([ap2d.shape[0], ap2d.shape[1], inner])

    dma("sync", consts_b[:, 0:1], dr["ones_bf"], [], ["ones"])
    dma("sync", consts_b[:, 2:2 + HP], dr["i64hi"], [], ["i64hi"])
    dma("sync", masks[:, 0:128], dr["maskF"], [], ["maskF"])
    dma("sync", masks[:, 128:256], dr["maskB"], [], ["maskB"])
    dma("sync", masks[:, 256:384], dr["idf32"], [], ["idf32"])
    for i, d in enumerate(("f", "b")):
        dma("sync", convw[:, i * NXBC * 4:(i + 1) * NXBC * 4], dr[f"convw_{d}"], [], [f"convw{d}"])
        dma("sync", convb[:, i * NXBC:(i + 1) * NXBC], dr[f"convb_{d}"], [], [f"convb{d}"])
        dma("sync", smallf[64:96, i:i + 1], dr[f"dtb_{d}"], ["smallf"], ["smallf"])
        dma("sync", smallf[64:96, 4 + i:5 + i], dr[f"alog_{d}"], ["smallf"], ["smallf"])
        dma("sync", dcol[:, i * NXS:(i + 1) * NXS], dr[f"dcol_{d}"], [], [f"dcol{d}"])
        dma("sync", rmscol[:, i * NXS:(i + 1) * NXS], dr[f"rmscol_{d}"], [], [f"rmscol{d}"])
    dma("sync", pbcol[:], dr["pbcol"], [], ["pbcol"])
    st(V, lambda e: e.memset(zeros128[:], 0.0), [], ["zeros128"])
    for i, d in enumerate(("f", "b")):
        st(S, lambda e, i=i: e.activation(smallf[64:96, 6 + i:7 + i],
                                          smallf[64:96, 4 + i:5 + i], AF.Exp),
           ["smallf"], ["smallf"], "Aexp")
        st(V, lambda e, i=i: e.tensor_scalar(smallf[64:96, 2 + i:3 + i],
                                             smallf[64:96, 6 + i:7 + i],
                                             -1.0, None, op0=AL.mult),
           ["smallf"], ["smallf"], "Aneg")

    p0f = [ARE[:, 0:4096].bitcast(F32), ARE[:, 4096:8192].bitcast(F32)]
    p0b = [ARD[:, 0:2048], ARD[:, 2048:4096]]
    for fc in range(8):
        r = fc % 2
        dma("gpsimd", p0f[r][:], dr["xT"][fc * 128:(fc + 1) * 128, :],
            [f"p0f{r}"], [f"p0f{r}"])
        st(V, lambda e, r=r: e.tensor_copy(p0b[r][:], p0f[r][:]),
           [f"p0f{r}", f"p0b{r}"], [f"p0b{r}"], "x cast")
        dma("sync", xfm_scr[fc, :, :], p0b[r][:], [f"p0b{r}"], [f"xfms{fc}", f"p0b{r}"])
        st(V, lambda e, fc=fc, r=r: e.tensor_scalar(
            acc[:, fc * L:(fc + 1) * L], p0f[r][:], pbcol[:, fc:fc + 1], None, op0=AL.add),
           [f"p0f{r}", "pbcol"], ["acc"], "accinit")
    XFM = [f"xfms{fc}" for fc in range(8)]

    def pass_a(d):
        di = 0 if d == "f" else 1
        fwd = d == "f"
        for k in range(8):
            dma("gpsimd", wab[:, k * WA_COLS:(k + 1) * WA_COLS],
                dr[f"wa_{d}"][k * 128:(k + 1) * 128, :], ["wab"], ["wab", f"wa_{k}"])
        st(V, lambda e: e.memset(xbc[:], 0.0), ["xbc"], ["xbc"], "zero xbc")
        st(V, lambda e: e.memset(hstate[64:128, :], 0.0), ["hstate"], ["hstate"], "zero h")

        sup_order = list(range(NSUP)) if fwd else list(range(NSUP - 1, -1, -1))
        for si, sp in enumerate(sup_order):
            if si > 0:
                xv = xbc.rearrange("p (c w) -> p c w", c=NXBC)
                if fwd:
                    st(V, lambda e, xv=xv: e.tensor_copy(xv[:, :, 0:3], xv[:, :, SUP:SUP + 3]),
                       ["xbc"], ["xbc"], "halo")
                else:
                    st(V, lambda e, xv=xv: e.tensor_copy(xv[:, :, SUP + 3:SUP + 6], xv[:, :, 3:6]),
                       ["xbc"], ["xbc"], "halo")
            dma("sync", xsup[:].rearrange("p (c t) -> p c t", c=8),
                xfm_scr[:, :, sp * SUP:(sp + 1) * SUP].transpose([1, 0, 2]),
                XFM + ["xsup"], ["xsup"])
            for oc in range(NXBC):
                pb = pmm[oc % 2]
                pres = f"pmm{oc % 2}"

                def em_mm(e, oc=oc, pb=pb):
                    last = None
                    for k in range(8):
                        last = e.matmul(pb[:, 0:SUP],
                                        wab[:, k * WA_COLS + oc * 128: k * WA_COLS + (oc + 1) * 128],
                                        xsup[:, k * SUP:(k + 1) * SUP],
                                        start=(k == 0), stop=(k == 7))
                    return last
                st(T, em_mm, [f"wa_{k}" for k in range(8)] + ["xsup", pres], [pres], "A mm")
                if oc < NXBC - 1:
                    st(S, lambda e, oc=oc, pb=pb: e.activation(
                        xbc[:, oc * (SUP + 6) + 3: oc * (SUP + 6) + 3 + SUP],
                        pb[:, 0:SUP], AF.Copy),
                       [pres, "xbc"], ["xbc"], "ev xbc")
                else:
                    st(S, lambda e, oc=oc, pb=pb: e.activation(
                        xbc[0:64, oc * (SUP + 6) + 3: oc * (SUP + 6) + 3 + SUP],
                        pb[0:64, 0:SUP], AF.Copy),
                       [pres, "xbc"], ["xbc"], "ev cdup")
                    st(S, lambda e, pb=pb: e.activation(dtraw[64:96, :],
                                                        pb[64:96, 0:SUP], AF.Copy),
                       [pres, "dtraw"], ["dtraw"], "ev dt")
            st(S, lambda e, i=di: e.activation(sptmp[64:96, :], dtraw[64:96, :], AF.Abs,
                                               bias=smallf[64:96, i:i + 1]),
               ["dtraw", "smallf", "sptmp"], ["sptmp"], "sp abs")
            st(S, lambda e: e.activation(sptmp[64:96, :], sptmp[64:96, :], AF.Exp, scale=-1.0),
               ["sptmp"], ["sptmp"], "sp exp")
            st(S, lambda e: e.activation(sptmp[64:96, :], sptmp[64:96, :], AF.Ln, bias=1.0),
               ["sptmp"], ["sptmp"], "sp ln")
            st(V, lambda e, i=di: e.tensor_scalar(dtsp[64:96, :], dtraw[64:96, :],
                                                  smallf[64:96, i:i + 1], 0.0,
                                                  op0=AL.add, op1=AL.max),
               ["dtraw", "smallf", "dtsp"], ["dtsp"], "sp relu")
            st(V, lambda e: e.tensor_tensor(dtsp[64:96, :], dtsp[64:96, :], sptmp[64:96, :],
                                            op=AL.add),
               ["dtsp", "sptmp"], ["dtsp"], "sp add")

            chunk_order = list(range(NCQ)) if fwd else list(range(NCQ - 1, -1, -1))

            def stage_A(ci):
                c_glob = sp * NCQ + ci
                toff = ci * Q
                ring = c_glob % 2
                XS_R = f"xsbc{ring}"
                xsb_o = ring * NXBC * 128
                dtlg = dtlg_tm[:, (ring) * 64:(ring + 1) * 64]
                DTLG = f"dtlg{ring}"
                cbmr = cbm[:, ring * 128:(ring + 1) * 128]
                CBM = f"cbm{ring}"

                cwv = convw[:, di * NXBC * 4:(di + 1) * NXBC * 4].rearrange(
                    "p (c k) -> p c k", k=4)

                xbcv = xbc[:].rearrange("p (c w) -> p c w", c=NXBC)

                def xview(k, ci=ci, fwd=fwd, xbcv=xbcv):
                    base = ci * Q + 3 + ((k - 3) if fwd else k)
                    return xbcv[:, :, base: base + Q]

                def wview(k, fwd=fwd):
                    kk = k if fwd else 3 - k
                    return cwv[:, :, kk:kk + 1].broadcast_to([128, NXBC, Q])

                cbv = convb[:, di * NXBC:(di + 1) * NXBC].unsqueeze(2) \
                    .broadcast_to([128, NXBC, Q])
                mtmp = xsbc[:, xsb_o:xsb_o + NXBC * 128].rearrange(
                    "p (c w) -> p c w", w=128)
                cvv = convt[:].rearrange("p (c w) -> p c w", w=128)
                st(V, lambda e: e.tensor_tensor(cvv, xview(0), wview(0), op=AL.mult),
                   ["xbc", f"convw{d}", "convt"], ["convt"], "conv0")
                for k in (1, 2, 3):
                    st(V, lambda e, k=k: e.tensor_tensor(mtmp, xview(k), wview(k), op=AL.mult),
                       ["xbc", f"convw{d}", XS_R], [XS_R], f"convm{k}")
                    st(V, lambda e: e.tensor_tensor(cvv, cvv, mtmp, op=AL.add),
                       [XS_R, "convt"], ["convt"], f"conva{k}")
                st(V, lambda e: e.tensor_tensor(cvv, cvv, cbv, op=AL.add),
                   [f"convb{d}", "convt"], ["convt"], "convb")
                st(S, lambda e, o=xsb_o: e.activation(xsbc[:, o:o + NXBC * 128], convt[:], AF.Silu),
                   ["convt", XS_R], [XS_R], "silu")
                if fwd:
                    st(V, lambda e, toff=toff, i=di: e.tensor_scalar(
                        sptmp[64:96, toff:toff + Q], dtsp[64:96, toff:toff + Q],
                        smallf[64:96, 2 + i:3 + i], None, op0=AL.mult),
                       ["dtsp", "smallf", "sptmp"], ["sptmp"], "dtA")
                else:
                    st(V, lambda e, toff=toff, i=di: e.tensor_scalar(
                        sptmp[64:96, toff:toff + Q], dtsp[64:96, toff:toff + Q],
                        smallf[64:96, 2 + i:3 + i], 0.0, op0=AL.mult, op1=AL.add,
                        accum_out=smallf[64:96, 8:9]),
                       ["dtsp", "smallf", "sptmp"], ["sptmp", "smallf"], "dtA")
                st(V, lambda e, toff=toff: e.tensor_tensor_scan(
                    lgfm[64:96, :], sptmp[64:96, toff:toff + Q], zeros128[64:96, :],
                    0.0, op0=AL.add, op1=AL.add),
                   ["sptmp", "zeros128", "lgfm"], ["lgfm"], "scan")
                if not fwd:
                    st(V, lambda e, toff=toff: e.scalar_tensor_tensor(
                        lgfm[64:96, :], lgfm[64:96, :], -1.0, sptmp[64:96, toff:toff + Q],
                        op0=AL.mult, op1=AL.add),
                       ["lgfm", "sptmp"], ["lgfm"], "suf1")
                    st(V, lambda e: e.tensor_scalar(lgfm[64:96, :], lgfm[64:96, :],
                                                    smallf[64:96, 8:9], None, op0=AL.add),
                       ["lgfm"], ["lgfm"], sreads=["smallf"], tag="suf2")
                dma("sync", lg_scr[di, c_glob, :, :], lgfm[64:96, :],
                    ["lgfm"], [f"lgscr{di}_{c_glob}"])
                id32 = masks[64:96, 256 + 64:256 + 96]
                st(T, lambda e, toff=toff, id32=id32: e.transpose(
                    pcb[:, 128:160], dtsp[64:96, toff:toff + Q], id32),
                   ["dtsp", "idf32", "pcb"], ["pcb"], "tp dt")
                st(T, lambda e, id32=id32: e.transpose(pcb[:, 160:192], lgfm[64:96, :], id32),
                   ["lgfm", "idf32", "pcb"], ["pcb"], "tp lg")
                st(V, lambda e, dtlg=dtlg: e.tensor_copy(dtlg, pcb[:, 128:192]),
                   ["pcb", DTLG], [DTLG], "ev dtlg")
                for fc in range(NXS):
                    dma("scalar", utm[:, ring * 2048 + fc * 128: ring * 2048 + (fc + 1) * 128],
                        xsbc[:, xsb_o + fc * 128: xsb_o + (fc + 1) * 128],
                        [XS_R, f"utm{ring}_{fc}"], [f"utm{ring}_{fc}"], tp=True)
                dma("scalar", bctm[:, ring * 128:(ring + 1) * 128],
                    xsbc[:, xsb_o + 16 * 128: xsb_o + 17 * 128],
                    [XS_R, f"bctm{ring}"], [f"bctm{ring}"], tp=True)

            def stage_A_post(ci):
                c_glob = sp * NCQ + ci
                ring = c_glob % 2
                XS_R = f"xsbc{ring}"
                xsb_o = ring * NXBC * 128
                dtlg = dtlg_tm[:, (ring) * 64:(ring + 1) * 64]
                DTLG = f"dtlg{ring}"
                cbmr = cbm[:, ring * 128:(ring + 1) * 128]
                CBM = f"cbm{ring}"
                UTM = [f"utm{ring}_{fc}" for fc in range(NXS)]
                ur = utm[:, ring * 2048:(ring + 1) * 2048]
                st(V, lambda e, ur=ur, dtlg=dtlg: e.tensor_tensor(
                    ur.rearrange("p (h w) -> p h w", h=NH),
                    ur.rearrange("p (h w) -> p h w", h=NH),
                    bcast_f(dtlg[:, 0:32], HP), op=AL.mult),
                   UTM + [DTLG], UTM, "U dt")
                st(T, lambda e, o=xsb_o: e.matmul(
                    pcb[:, 0:128],
                    xsbc[0:64, o + 16 * 128: o + 16 * 128 + 128],
                    xsbc[0:64, o + 17 * 128: o + 17 * 128 + 128],
                    start=True, stop=True),
                   [XS_R, "pcb"], ["pcb"], "CB")
                st(V, lambda e, cbmr=cbmr, m=(masks[:, 0:128] if fwd else masks[:, 128:256]):
                   e.tensor_tensor(cbmr, pcb[:, 0:128], m, op=AL.mult),
                   ["pcb", "maskF" if fwd else "maskB", CBM], [CBM], "CBm")

            def stage_B(ci, hh):
                c_glob = sp * NCQ + ci
                ring = c_glob % 2
                XS_R = f"xsbc{ring}"
                xsb_o = ring * NXBC * 128
                dtlg = dtlg_tm[:, ring * 64:(ring + 1) * 64]
                DTLG = f"dtlg{ring}"
                cbmr = cbm[:, ring * 128:(ring + 1) * 128]
                CBM = f"cbm{ring}"
                wt = (Q - 1) if fwd else 0
                dma("sync", alin[:],
                    lg_scr[di, c_glob, hh * HH:(hh + 1) * HH, :]
                    .rearrange("h t -> (h t)").unsqueeze(0).partition_broadcast(128),
                    [f"lgscr{di}_{c_glob}", "alin"], ["alin"])
                st(V, lambda e, hh=hh, dtlg=dtlg: e.tensor_tensor(
                    delt[:].rearrange("p (h t) -> p h t", h=HH),
                    alin[:].rearrange("p (h t) -> p h t", h=HH),
                    bcast_f(dtlg[:, 32 + hh * HH:32 + (hh + 1) * HH], Q),
                    op=AL.subtract),
                   ["alin", DTLG, "delt"], ["delt"], "dsub")
                st(S, lambda e: e.activation(gam[:], delt[:], AF.Exp),
                   ["delt", "gam"], ["gam"], "gamma")
                st(S, lambda e: e.activation(almeg[64:128, :], alin[64:128, :], AF.Exp),
                   ["alin", "almeg"], ["almeg"], "alpha")
                st(V, lambda e, cbmr=cbmr: e.scalar_tensor_tensor(
                    smeg[:].rearrange("p (h t) -> p h t", h=HH),
                    gam[:].rearrange("p (h t) -> p h t", h=HH),
                    1.0,
                    cbmr.unsqueeze(1).broadcast_to([128, HH, Q]),
                    op0=AL.min, op1=AL.mult),
                   ["gam", CBM, "smeg"], ["smeg"], "S")
                st(V, lambda e, wt=wt: e.tensor_scalar(
                    epst[:, 0:HH],
                    gam[:].rearrange("p (h t) -> p h t", h=HH)[:, :, wt],
                    1.0, None, op0=AL.min),
                   ["gam", "epst"], ["epst"], "gcl")
                st(V, lambda e, o=xsb_o: e.tensor_tensor(
                    cameg[64:128, :].rearrange("p (h t) -> p h t", h=HH),
                    xsbc[64:128, o + 16 * 128: o + 17 * 128]
                    .unsqueeze(1).broadcast_to([64, HH, Q]),
                    almeg[64:128, :].rearrange("p (h t) -> p h t", h=HH), op=AL.mult),
                   [XS_R, "almeg", "cameg"], ["cameg"], "Ca")
                st(V, lambda e, ring=ring: e.tensor_tensor(
                    bbar[:].rearrange("p (h w) -> p h w", h=HH),
                    bctm[:, ring * 128: ring * 128 + 64].unsqueeze(1)
                    .broadcast_to([128, HH, HP]),
                    bcast_f(epst[:, 0:HH], HP), op=AL.mult),
                   ["epst", f"bctm{ring}", "bbar"], ["bbar"], "Bbar")
                st(V, lambda e, wt=wt: e.tensor_tensor(
                    aqi[64:128, :].rearrange("p (h w) -> p h w", h=HH),
                    I64HI[64:128, :].unsqueeze(1).broadcast_to([64, HH, HP]),
                    almeg[64:128, :].rearrange("p (h t) -> p h t", h=HH)[:, :, wt:wt + 1]
                    .broadcast_to([64, HH, HP]), op=AL.mult),
                   ["almeg", "i64hi", "aqi"], ["aqi"], "aQI")

            def stage_C_mm(ci, hh):
                c_glob = sp * NCQ + ci
                ring = c_glob % 2
                XS_R = f"xsbc{ring}"
                xsb_o = ring * NXBC * 128

                def em_pairs(e, hh=hh, ring=ring):
                    last = None
                    started = {}
                    for pl in range(8):
                        pc = hh * 8 + pl
                        bidx = pl // 4
                        yb = py[bidx]
                        ycol = (pl % 4) * 128
                        for par in range(2):
                            hloc = 2 * pl + par
                            last = e.matmul(
                                yb[par * 64:(par + 1) * 64, ycol:ycol + 128],
                                utm[:, ring * 2048 + pc * 128 + par * 64:
                                    ring * 2048 + pc * 128 + par * 64 + 64],
                                smeg[:, hloc * Q:(hloc + 1) * Q],
                                start=not started.get((bidx, par), False), stop=False,
                                tile_position=(0, par * 64))
                            started[(bidx, par)] = True
                    for pl in range(8):
                        pc = hh * 8 + pl
                        yb = py[pl // 4]
                        ycol = (pl % 4) * 128
                        for par in range(2):
                            hloc = 2 * pl + par
                            h_glob = 2 * pc + par
                            last = e.matmul(
                                yb[par * 64:(par + 1) * 64, ycol:ycol + 128],
                                hstate[64:128, h_glob * HP:(h_glob + 1) * HP],
                                cameg[64:128, hloc * Q:(hloc + 1) * Q],
                                start=False,
                                stop=(pl % 4 == 3 and par == 1),
                                tile_position=(64, par * 64))
                    return last
                st(T, em_pairs, [f"utm{ring}_{fc}" for fc in range(NXS)]
                   + ["smeg", "hstate", "cameg", "py0", "py1"],
                   ["py0", "py1"], "Y mms")

                def em_upd(e, hh=hh, ring=ring):
                    last = None
                    started = [False, False]
                    for hl in range(HH):
                        h_glob = hh * HH + hl
                        bidx = hl // 8
                        hb = ph_[bidx]
                        hcol = (hl % 8) * 64
                        last = e.matmul(hb[64:128, hcol:hcol + 64],
                                        bbar[:, hl * HP:(hl + 1) * HP],
                                        utm[:, ring * 2048 + h_glob * HP:
                                            ring * 2048 + (h_glob + 1) * HP],
                                        start=not started[bidx], stop=False,
                                        tile_position=(0, 64))
                        started[bidx] = True
                    for hl in range(HH):
                        h_glob = hh * HH + hl
                        hb = ph_[hl // 8]
                        hcol = (hl % 8) * 64
                        last = e.matmul(hb[64:128, hcol:hcol + 64],
                                        aqi[64:128, hl * HP:(hl + 1) * HP],
                                        hstate[64:128, h_glob * HP:(h_glob + 1) * HP],
                                        start=False, stop=(hl % 8 == 7),
                                        tile_position=(64, 64))
                    return last
                st(T, em_upd, [f"utm{ring}_{fc}" for fc in range(NXS)]
                   + ["bbar", "aqi", "hstate", "ph0", "ph1"],
                   ["ph0", "ph1"], "h mms")

            def stage_C_ev(ci, hh):
                c_glob = sp * NCQ + ci
                ring = c_glob % 2
                XS_R = f"xsbc{ring}"
                xsb_o = ring * NXBC * 128
                for pl in range(8):
                    pc = hh * 8 + pl
                    yb = py[pl // 4]
                    ycol = (pl % 4) * 128
                    st(V, lambda e, pc=pc, yb=yb, ycol=ycol, ring=ring, i=di, o=xsb_o:
                       e.scalar_tensor_tensor(
                           ysb[:, ring * 2048 + pc * 128: ring * 2048 + (pc + 1) * 128],
                           xsbc[:, o + pc * 128: o + (pc + 1) * 128],
                           dcol[:, i * NXS + pc: i * NXS + pc + 1],
                           yb[:, ycol:ycol + 128], op0=AL.mult, op1=AL.add),
                       [f"py{pl // 4}", XS_R, f"dcol{d}", f"ysb{ring}"],
                       [f"ysb{ring}"], "Y ev")
                for bb in range(2):
                    st(V, lambda e, bb=bb, hh=hh: e.tensor_copy(
                        hstate[64:128, hh * HH * HP + bb * 512:
                               hh * HH * HP + (bb + 1) * 512],
                        ph_[bb][64:128, 0:512]),
                       [f"ph{bb}", "hstate"], ["hstate"], "h ev")

            def stage_store(ci):
                c_glob = sp * NCQ + ci
                ring = c_glob % 2
                dma("sync", yscr[di, :, :, c_glob * Q:(c_glob + 1) * Q].transpose([1, 0, 2]),
                    ysb[:, ring * 2048:(ring + 1) * 2048].rearrange("p (c t) -> p c t", c=NXS),
                    [f"ysb{ring}"], [f"yscr{di}_{c_glob}"])

            stage_A(chunk_order[0])
            stage_A_post(chunk_order[0])
            stage_B(chunk_order[0], 0)
            for idx, ci in enumerate(chunk_order):
                stage_C_mm(ci, 0)
                if idx + 1 < len(chunk_order):
                    stage_A(chunk_order[idx + 1])
                stage_B(ci, 1)
                stage_C_ev(ci, 0)
                if idx + 1 < len(chunk_order):
                    stage_A_post(chunk_order[idx + 1])
                stage_C_mm(ci, 1)
                if idx + 1 < len(chunk_order):
                    stage_B(chunk_order[idx + 1], 0)
                stage_C_ev(ci, 1)
                stage_store(ci)

    def pass_b(d):
        di = 0 if d == "f" else 1
        for k in range(8):
            dma("gpsimd", opstage[:, k * 2048:(k + 1) * 2048],
                dr[f"op_{d}"][k * 128:(k + 1) * 128, :], ["opstage"], ["opstage", f"op_{k}"])
        phr = [phring[:, 0:1024], phring[:, 1024:2048]]
        for grp in range(4):
            for k in range(8):
                sl = k % 2
                dma("gpsimd", phr[sl], dr[f"ph_{d}"][k * 128:(k + 1) * 128, :],
                    [f"phr{sl}"], [f"phr{sl}"])
                for fci in range(4):
                    fc = grp * 4 + fci
                    for half in range(2):
                        bi = fci * 2 + half
                        pb = PB8[bi]
                        pres = PB8N[bi]

                        def em_we(e, fc=fc, half=half, pb=pb, k=k, sl=sl):
                            return e.matmul(
                                pb[:, 0:512],
                                opstage[:, k * 2048 + fc * 128: k * 2048 + (fc + 1) * 128],
                                phr[sl][:, half * 512:(half + 1) * 512],
                                start=(k == 0), stop=(k == 7))
                        st(T, em_we, [f"op_{k}", f"phr{sl}"] + ([pres] if k == 0 else []),
                           [pres], "Weff mm")
            for fci in range(4):
                fc = grp * 4 + fci
                for half in range(2):
                    bi = fci * 2 + half
                    st(V, lambda e, fc=fc, half=half, pb=PB8[bi], i=di: e.tensor_scalar(
                        weff[:, fc * D + half * 512: fc * D + (half + 1) * 512],
                        pb[:, 0:512], rmscol[:, i * NXS + fc: i * NXS + fc + 1],
                        None, op0=AL.mult),
                       [PB8N[bi], f"rmscol{d}", "weff"], ["weff"], "Weff ev")
        for k in range(8):
            dma("gpsimd", wzb[:, k * 2048:(k + 1) * 2048],
                dr[f"wz_{d}"][k * 128:(k + 1) * 128, :],
                ["wzb", "xbc", "xsbc0", "xsbc1", "bctm0", "bctm1", "cbm", "bbar", "aqi"],
                ["wzb", f"wz_{k}"])
        for sp in range(NSUP):
            dma("sync", xsup[:].rearrange("p (c t) -> p c t", c=8),
                xfm_scr[:, :, sp * SUP:(sp + 1) * SUP].transpose([1, 0, 2]),
                XFM + ["xsup"], ["xsup"])
            dma("sync", ysup[:].rearrange("p (c t) -> p c t", c=NXS),
                yscr[di, :, :, sp * SUP:(sp + 1) * SUP].transpose([1, 0, 2]),
                [f"yscr{di}_{sp * NCQ + i}" for i in range(NCQ)] +
                ["ysup", "convt", "delt", "alin", "dtraw", "dtsp", "sptmp", "lgfm", "dtlg"],
                ["ysup"])
            for fc in range(NXS):
                pb = pmm[fc % 2]
                pres = f"pmm{fc % 2}"

                def em_z(e, fc=fc, pb=pb):
                    last = None
                    for k in range(8):
                        last = e.matmul(pb[:, 0:SUP],
                                        wzb[:, k * 2048 + fc * 128: k * 2048 + (fc + 1) * 128],
                                        xsup[:, k * SUP:(k + 1) * SUP],
                                        start=(k == 0), stop=(k == 7))
                    return last
                st(T, em_z, [f"wz_{k}" for k in range(8)] + ["xsup", pres], [pres], "z mm")
                st(S, lambda e, fc=fc, pb=pb: e.activation(
                    zsil[:, fc * SUP:(fc + 1) * SUP], pb[:, 0:SUP], AF.Silu),
                   [pres, "zsil"], ["zsil"], "z silu")
            st(V, lambda e: e.tensor_tensor(ygat[:], ysup[:], zsil[:], op=AL.mult),
               ["ysup", "zsil", "ygat", "phr0", "phr1"], ["ygat"], "gate")
            st(S, lambda e: e.activation(sqb[:], ygat[:], AF.Square),
               ["ygat", "ysup"], ["ysup"], "sq")

            def em_rms(e):
                last = None
                for fc in range(NXS):
                    last = e.matmul(pst[0:1, 0:SUP], ONES,
                                    sqb[:, fc * SUP:(fc + 1) * SUP],
                                    start=(fc == 0), stop=(fc == NXS - 1))
                return last
            st(T, em_rms, ["ysup", "ones", "pst"], ["pst"], "rms sum")
            st(V, lambda e: e.tensor_scalar(rstdr[0:1, 0:SUP], pst[0:1, 0:SUP],
                                            1.0 / 2048.0, EPS, op0=AL.mult, op1=AL.add),
               ["pst", "rstdr"], ["rstdr"], "rms pre")
            dma("sync", rs_scr[di, sp, :, :], rstdr[0:1, 0:SUP], ["rstdr"], [f"rs{di}{sp}"])
            dma("sync", rstdr[:], rs_scr[di, sp, :, :].partition_broadcast(128),
                [f"rs{di}{sp}", "rstdr"], ["rstdr"])
            st(V, lambda e: e.reciprocal(rstdr[:], rstdr[:]), ["rstdr"], ["rstdr"], "recip")
            st(S, lambda e: e.activation(rstdr[:], rstdr[:], AF.Sqrt), ["rstdr"], ["rstdr"], "sqrt")
            for dm in range(8):
                pb = pmm[dm % 2]
                pres = f"pmm{dm % 2}"

                def em_op(e, dm=dm, pb=pb):
                    last = None
                    for fc in range(NXS):
                        last = e.matmul(pb[:, 0:SUP],
                                        weff[:, fc * D + dm * 128: fc * D + (dm + 1) * 128],
                                        ygat[:, fc * SUP:(fc + 1) * SUP],
                                        start=(fc == 0), stop=(fc == NXS - 1))
                    return last
                st(T, em_op, ["weff", "ygat", pres], [pres], "oproj mm")
                st(V, lambda e, pb=pb: e.tensor_tensor(
                    evtmp[:], pb[:, 0:SUP], rstdr[:], op=AL.mult),
                   [pres, "rstdr", "evtmp"], ["evtmp"], "oproj ev")
                st(V, lambda e, dm=dm, sp=sp: e.tensor_tensor(
                    acc[:, dm * L + sp * SUP: dm * L + (sp + 1) * SUP],
                    acc[:, dm * L + sp * SUP: dm * L + (sp + 1) * SUP],
                    evtmp[:], op=AL.add),
                   ["evtmp", "acc"], ["acc"], "acc add")

    def epilogue(units=()):
        uit = iter(units)
        dma("sync", lnrep[:, 0:D], dr["lnw_a"].partition_broadcast(128),
            ["lnrep"], ["lnrep"])
        dma("sync", lnrep[:, D:2 * D], dr["lnb_a"].partition_broadcast(128),
            ["lnrep"], ["lnrep"])
        for tc in range(16):
            for dc in range(8):
                dma("scalar", etm[:, dc * 128:(dc + 1) * 128],
                    acc[:, dc * L + tc * 128: dc * L + (tc + 1) * 128],
                    ["acc", "etm"], ["etm"], tp=True)
            dma("sync", xtmf[:], x_in[tc * 128:(tc + 1) * 128, :], ["xtmf"], ["xtmf"])
            st(V, lambda e: e.tensor_reduce(epst[:, 0:1], etm[:, 0:D], axis=AX.X, op=AL.add),
               ["etm", "epst"], ["epst"])
            st(S, lambda e: e.activation(ep1[:], etm[:, 0:D], AF.Square),
               ["etm", "ep1"], ["ep1"])
            st(V, lambda e: e.tensor_reduce(epst[:, 1:2], ep1[:], axis=AX.X, op=AL.add),
               ["ep1", "epst"], ["epst"])
            st(V, lambda e: e.tensor_scalar(epst[:, 2:3], epst[:, 0:1], 1.0 / D, None,
                                            op0=AL.mult), ["epst"], ["epst"])
            st(V, lambda e: e.tensor_tensor(epst[:, 3:4], epst[:, 2:3], epst[:, 2:3],
                                            op=AL.mult), ["epst"], ["epst"])
            st(V, lambda e: e.tensor_scalar(epst[:, 4:5], epst[:, 1:2], 1.0 / D, EPS,
                                            op0=AL.mult, op1=AL.add), ["epst"], ["epst"])
            st(V, lambda e: e.tensor_tensor(epst[:, 4:5], epst[:, 4:5], epst[:, 3:4],
                                            op=AL.subtract), ["epst"], ["epst"])
            st(V, lambda e: e.reciprocal(epst[:, 5:6], epst[:, 4:5]), ["epst"], ["epst"])
            st(S, lambda e: e.activation(epst[:, 6:7], epst[:, 5:6], AF.Sqrt),
               ["epst"], ["epst"])
            st(V, lambda e: e.tensor_scalar(ep1[:], etm[:, 0:D], epst[:, 2:3], epst[:, 6:7],
                                            op0=AL.subtract, op1=AL.mult),
               ["etm", "ep1"], ["ep1"], sreads=["epst"], tag="ln1")
            st(V, lambda e: e.tensor_tensor(ep1[:], ep1[:], lnrep[:, 0:D], op=AL.mult),
               ["ep1", "lnrep"], ["ep1"])
            st(V, lambda e: e.tensor_tensor(ep1[:], ep1[:], lnrep[:, D:2 * D], op=AL.add),
               ["ep1", "lnrep"], ["ep1"])
            st(V, lambda e: e.tensor_tensor(ep1[:], ep1[:], xtmf[:], op=AL.add),
               ["ep1", "xtmf"], ["ep1"], "h1")
            st(V, lambda e: e.tensor_reduce(epst[:, 8:9], ep1[:], axis=AX.X, op=AL.add),
               ["ep1", "epst"], ["epst"])
            st(S, lambda e: e.activation(xtmf[:], ep1[:], AF.Square),
               ["ep1", "xtmf"], ["xtmf"])
            st(V, lambda e: e.tensor_reduce(epst[:, 9:10], xtmf[:], axis=AX.X, op=AL.add),
               ["xtmf", "epst"], ["epst"])
            st(V, lambda e: e.tensor_scalar(epst[:, 10:11], epst[:, 8:9], 1.0 / D, None,
                                            op0=AL.mult), ["epst"], ["epst"])
            st(V, lambda e: e.tensor_tensor(epst[:, 11:12], epst[:, 10:11], epst[:, 10:11],
                                            op=AL.mult), ["epst"], ["epst"])
            st(V, lambda e: e.tensor_scalar(epst[:, 12:13], epst[:, 9:10], 1.0 / D, EPS,
                                            op0=AL.mult, op1=AL.add), ["epst"], ["epst"])
            st(V, lambda e: e.tensor_tensor(epst[:, 12:13], epst[:, 12:13], epst[:, 11:12],
                                            op=AL.subtract), ["epst"], ["epst"])
            st(V, lambda e: e.reciprocal(epst[:, 13:14], epst[:, 12:13]), ["epst"], ["epst"])
            st(S, lambda e: e.activation(epst[:, 14:15], epst[:, 13:14], AF.Sqrt),
               ["epst"], ["epst"])
            st(V, lambda e: e.tensor_scalar(ep1[:], ep1[:], epst[:, 10:11], epst[:, 14:15],
                                            op0=AL.subtract, op1=AL.mult),
               ["ep1"], ["ep1"], sreads=["epst"], tag="x1")
            dma("sync", x1_scr[tc * 128:(tc + 1) * 128, :], ep1[:], ["ep1"], [f"x1s{tc}"])
            for _ in range(4):
                u = next(uit, None)
                if u is not None:
                    u()
        for u in uit:
            u()

    def quant(src_ap, width, res_in, dst_ap, dst_res, tag, ceng=V):
        nb = width // 16
        dump_fsc = tag == f"hq{DBGTC}_0" and "fsc0" in KDBG
        dump_q5 = tag == f"hq{DBGTC}_5" and "q5" in KDBG
        st(V, lambda e: e.tensor_scalar(fsgn[:, 0:width], src_ap, 0.0, 2.0,
                                        op0=AL.is_ge, op1=AL.mult),
           list(res_in) + ["fsgn"], ["fsgn"], tag + " sgn0")
        st(V, lambda e: e.scalar_tensor_tensor(ftA[:, 0:width], fsgn[:, 0:width],
                                               -1.0, src_ap, op0=AL.add, op1=AL.mult),
           list(res_in) + ["fsgn", "ftA"], ["ftA"], tag + " abs")

        av = ftA[:, 0:nb * 16].rearrange("p (b w) -> p b w", w=16)
        r8 = ftC[:, 0:nb * 8].rearrange("p (b w) -> p b w", w=8)
        r4 = ftC[:, nb * 8:nb * 12].rearrange("p (b w) -> p b w", w=4)
        r2 = ftC[:, nb * 12:nb * 14].rearrange("p (b w) -> p b w", w=2)
        st(V, lambda e, nb=nb, av=av: e.tensor_tensor(
            ftC[:, 0:nb * 8], av[:, :, 0:8], av[:, :, 8:16], op=AL.max),
           ["ftA", "ftC"], ["ftC"], tag + " bm0")
        st(V, lambda e, nb=nb, r8=r8: e.tensor_tensor(
            ftC[:, nb * 8:nb * 12], r8[:, :, 0:4], r8[:, :, 4:8], op=AL.max),
           ["ftC"], ["ftC"], tag + " bm1")
        st(V, lambda e, nb=nb, r4=r4: e.tensor_tensor(
            ftC[:, nb * 12:nb * 14], r4[:, :, 0:2], r4[:, :, 2:4], op=AL.max),
           ["ftC"], ["ftC"], tag + " bm2")
        st(V, lambda e, nb=nb, r2=r2: e.tensor_tensor(
            fsc[:, 0:nb], r2[:, :, 0], r2[:, :, 1], op=AL.max),
           ["ftC", "fsc"], ["fsc"], tag + " bm3")
        if dump_q5:
            dump("q5ftA", ftA[:], ["ftA"], F32, (128, 1024))
            dump("q5fscraw", fsc[:], ["fsc"], F32, (128, 192))
        st(V, lambda e: e.tensor_scalar(fsc[:, 64:64 + nb], fsc[:, 0:nb], 6e-12, None,
                                        op0=AL.max),
           ["fsc"], ["fsc"], tag + " clamp")
        st(V, lambda e: e.reciprocal(fsc[:, 128:128 + nb], fsc[:, 64:64 + nb]),
           ["fsc"], ["fsc"], tag + " recip")
        if dump_fsc:
            dump("fsc0", fsc[:], ["fsc"], F32, (128, 192))
        st(V, lambda e: e.tensor_tensor(
            ftB[:, 0:width].rearrange("p (b w) -> p b w", w=16),
            ftA[:, 0:width].rearrange("p (b w) -> p b w", w=16),
            bcast_f(fsc[:, 128:128 + nb], 16), op=AL.mult),
           ["ftA", "fsc", "ftB"], ["ftB"], tag + " xn")

        if ceng == V:
            def em_cmp(e):
                a = ftB[:, 0:width]
                g = ftC[:, 0:width]
                u = ftA[:, 0:width]
                e.tensor_scalar(g, a, TAU6[0], None, op0=AL.is_gt)
                for i in (1, 2, 3):
                    e.scalar_tensor_tensor(g, a, TAU6[i], g, op0=AL.is_gt, op1=AL.add)
                e.tensor_scalar(u, a, TAU6[4], None, op0=AL.is_gt)
                e.scalar_tensor_tensor(u, a, TAU6[5], u, op0=AL.is_gt, op1=AL.add)
                e.scalar_tensor_tensor(g, g, 0.5, u, op0=AL.mult, op1=AL.add)
                e.tensor_scalar(u, a, TAU6[6], 2.0, op0=AL.is_gt, op1=AL.mult)
                return e.tensor_tensor(g, g, u, op=AL.add)
        else:
            def em_cmp(e):
                a = ftB[:, 0:width]
                g = ftC[:, 0:width]
                u = ftA[:, 0:width]
                e.tensor_scalar(g, a, TAU6[0], 0.5, op0=AL.is_gt, op1=AL.mult)
                for i, w in ((1, 0.5), (2, 0.5), (3, 0.5), (4, 1.0), (5, 1.0), (6, 2.0)):
                    e.tensor_scalar(u, a, TAU6[i], w, op0=AL.is_gt, op1=AL.mult)
                    e.tensor_tensor(g, g, u, op=AL.add)
                return e.engine_nop()
        st(ceng, em_cmp, ["ftB", "ftA", "ftC"], ["ftC", "ftA"], tag + " cmp")
        st(V, lambda e: e.scalar_tensor_tensor(
            ftB[:, 0:width].rearrange("p (b w) -> p b w", w=16),
            ftC[:, 0:width].rearrange("p (b w) -> p b w", w=16),
            W6, bcast_f(fsc[:, 64:64 + nb], 16), op0=AL.mult, op1=AL.mult),
           ["ftC", "fsc", "ftB"], ["ftB"], tag + " scale")
        st(V, lambda e: e.scalar_tensor_tensor(
            dst_ap[:, 0:width] if dst_ap.shape[1] != width else dst_ap,
            fsgn[:, 0:width], -1.0, ftB[:, 0:width], op0=AL.add, op1=AL.mult),
           ["ftB", "fsgn", dst_res], [dst_res], tag + " q")

    wq_ping = [0]
    fstgr = [fstage, ARD[:, 11648:13696].bitcast(F32)]

    def wq_unit_gu(wname, pc):
        hbm = dr["wg"] if wname == "g" else dr["wu"]
        wregs = wq_g if wname == "g" else wq_u
        ID32 = masks[:, 256:384]
        sl = wq_ping[0] % 2
        wq_ping[0] += 1
        FS = "fstg" if sl == 0 else "fstg1"
        dma("synck", fstgr[sl][:, 0:1024], hbm[pc * 128:(pc + 1) * 128, :],
            [FS], [FS])
        quant(fstgr[sl][:, 0:1024], 1024, [FS], ftA, "ftA", f"wq{wname}{pc}")
        for kc in range(8):
            bk = py[kc % 2]
            bn = f"py{kc % 2}"
            col = (kc // 2) * 128
            st(T, lambda e, bk=bk, col=col, kc=kc: e.transpose(
                bk[:, col:col + 128], ftA[:, kc * 128:(kc + 1) * 128], ID32),
               ["ftA", "idf32", bn], [bn], "wtp")
            st(S, lambda e, bk=bk, col=col, kc=kc, pc=pc, wregs=wregs: e.activation(
                wregs[kc][:, pc * 128:(pc + 1) * 128], bk[:, col:col + 128], AF.Copy),
               [bn, f"wq{wname}_{pc}_{kc}"], [f"wq{wname}_{pc}_{kc}"], "wtpev")

    def wq_unit_d(pc, ss, wdt):
        ID32 = masks[:, 256:384]
        sl = wq_ping[0] % 2
        wq_ping[0] += 1
        FS = "fstg" if sl == 0 else "fstg1"
        dma("synck", fstgr[sl][:, 0:wdt],
            dr["wdn"][pc * 128:(pc + 1) * 128, ss:ss + wdt], [FS], [FS])
        quant(fstgr[sl][:, 0:wdt], wdt, [FS], ftA, "ftA", f"wqd{pc}_{ss}")
        for ic in range(wdt // 128):
            icg = ss // 128 + ic
            bk = py[ic % 2]
            bn = f"py{ic % 2}"
            col = (ic // 2) * 128
            st(T, lambda e, bk=bk, col=col, ic=ic: e.transpose(
                bk[:, col:col + 128], ftA[:, ic * 128:(ic + 1) * 128], ID32),
               ["ftA", "idf32", bn], [bn], "wtp")
            st(S, lambda e, bk=bk, col=col, icg=icg, pc=pc: e.activation(
                wq_d[icg][:, pc * 128:(pc + 1) * 128], bk[:, col:col + 128], AF.Copy),
               [bn, f"wqd_{pc}_{icg}"], [f"wqd_{pc}_{icg}"], "wtpev")

    def wq_safe_units():
        us = []
        for wname in ("g", "u"):
            for pc in range(22):
                us.append(lambda wname=wname, pc=pc: wq_unit_gu(wname, pc))
        for pc in range(8):
            us.append(lambda pc=pc: wq_unit_d(pc, 0, 1024))
        return us

    def ffn():
        for pc in range(8):
            for ss, wdt in ((1024, 1024), (2048, 768)):
                wq_unit_d(pc, ss, wdt)
        dump("wqg0", wq_g[0][:], [f"wqg_{pc}_0" for pc in range(22)], BF16, (128, 2816))
        WQG = [f"wq{w}_{pc}_{kc}" for w in "gu" for pc in range(22) for kc in range(8)]
        WQD = [f"wqd_{pc}_{icg}" for pc in range(8) for icg in range(22)]
        ISL = [(0, 512), (512, 512), (1024, 512), (1536, 512), (2048, 512), (2560, 256)]
        NH2 = 2
        gb = [pmm[0], pcb]
        ub = [pmm[1], pst]
        gbn = ["pmm0", "pcb"]
        ubn = ["pmm1", "pst"]

        def x1_load(tc):
            sl = tc % 2
            wr = ["fstg"] if sl == 1 else ["fstg1"]
            dma("sync", ep2fr[sl][:], x1_scr[tc * 128:(tc + 1) * 128, :],
                [f"ep2f{sl}"] + wr, [f"ep2f{sl}"] + wr)

        def ffn_mm(isl, which):
            io, iw = ISL[isl]
            b = (gb if which == "g" else ub)[isl % 2]
            bn = (gbn if which == "g" else ubn)[isl % 2]
            wq = wq_g if which == "g" else wq_u

            def em(e, io=io, iw=iw, b=b, wq=wq):
                last = None
                for kc in range(8):
                    last = e.matmul(b[:, 0:iw],
                                    xqfm[:, kc * 128:(kc + 1) * 128],
                                    wq[kc][:, io:io + iw],
                                    start=(kc == 0), stop=(kc == 7))
                return last
            st(T, em, [f"xqf{k}" for k in range(8)] + WQG + [bn], [bn],
               which + "mm")

        def ffn_act(tc, isl, cj):
            io, iw = ISL[isl]
            ep = ephr[isl % 2]
            EPR = f"eph{isl % 2}"
            st(S, lambda e, iw=iw, ep=ep: e.activation(ep[:, 0:iw],
                                                       gb[isl % 2][:, 0:iw], AF.Silu),
               [gbn[isl % 2], EPR], [EPR], "fsilu")
            st(V, lambda e, iw=iw, ep=ep: e.tensor_tensor(ep[:, 0:iw], ep[:, 0:iw],
                                                          ub[isl % 2][:, 0:iw], op=AL.mult),
               [ubn[isl % 2], EPR], [EPR], "hmul")
            quant(ep[:, 0:iw], iw, [EPR], hqtm[:, io:io + iw], "hqtm", f"hq{tc}_{isl}")
            for ic in range(io // 128, (io + iw) // 128):
                dma("scalar", hqfm[:, ic * NH2 * 128 + cj * 128:
                                   ic * NH2 * 128 + (cj + 1) * 128],
                    hqtm[:, ic * 128:(ic + 1) * 128],
                    ["hqtm", f"hqf{cj}_{ic}"], [f"hqf{cj}_{ic}"], tp=True)

        def ffn_dn(grp):
            HQF = [f"hqf{cj}_{ic}" for cj in range(NH2) for ic in range(22)]
            for dm in range(8):
                pb = pmm[dm % 2]
                pres = f"pmm{dm % 2}"

                def em_dn(e, dm=dm, pb=pb):
                    last = None
                    for ic in range(22):
                        last = e.matmul(pb[:, 0:NH2 * 128],
                                        wq_d[ic][:, dm * 128:(dm + 1) * 128],
                                        hqfm[:, ic * NH2 * 128:(ic + 1) * NH2 * 128],
                                        start=(ic == 0), stop=(ic == 21))
                    return last
                st(T, em_dn, HQF + WQD + [pres], [pres], "dn mm")
                st(S, lambda e, pb=pb: e.activation(dtmp[:, 0:NH2 * 128],
                                                    pb[:, 0:NH2 * 128], AF.Copy),
                   [pres, "dtmp"], ["dtmp"], "dn ev")
                for cj in range(NH2):
                    dma("scalar", dntm[:, cj * D + dm * 128: cj * D + (dm + 1) * 128],
                        dtmp[:, cj * 128:(cj + 1) * 128],
                        ["dtmp", f"dntp{cj}_{dm}"], [f"dntp{cj}_{dm}"], tp=True)
            if grp == DBGTC // 2:
                dump("dn0", dntm[:, (DBGTC % 2) * D:(DBGTC % 2) * D + D],
                     [f"dntp{DBGTC % 2}_{dm}" for dm in range(8)], BF16, (128, D))

        def ffn_ln2_chunk(grp, cj):
            if True:
                tc = grp * NH2 + cj
                DNT = [f"dntp{cj}_{dm}" for dm in range(8)]
                dma("scalar", ep1f[:], x1_scr[tc * 128:(tc + 1) * 128, :],
                    ["ep1f"], ["ep1f"])
                st(V, lambda e, cj=cj: e.tensor_tensor(
                    ep1f[:], ep1f[:], dntm[:, cj * D:(cj + 1) * D], op=AL.add),
                   ["ep1f"] + DNT, ["ep1f"], "h2")
                st(V, lambda e: e.tensor_reduce(epst[:, 0:1], ep1f[:], axis=AX.X, op=AL.add),
                   ["ep1f", "epst"], ["epst"])
                st(S, lambda e: e.activation(ftB[:, 0:D], ep1f[:], AF.Square),
                   ["ep1f", "ftB"], ["ftB"])
                st(V, lambda e: e.tensor_reduce(epst[:, 1:2], ftB[:, 0:D], axis=AX.X, op=AL.add),
                   ["ftB", "epst"], ["epst"])
                st(V, lambda e: e.tensor_scalar(epst[:, 2:3], epst[:, 0:1], 1.0 / D, None,
                                                op0=AL.mult), ["epst"], ["epst"])
                st(V, lambda e: e.tensor_tensor(epst[:, 3:4], epst[:, 2:3], epst[:, 2:3],
                                                op=AL.mult), ["epst"], ["epst"])
                st(V, lambda e: e.tensor_scalar(epst[:, 4:5], epst[:, 1:2], 1.0 / D, EPS,
                                                op0=AL.mult, op1=AL.add), ["epst"], ["epst"])
                st(V, lambda e: e.tensor_tensor(epst[:, 4:5], epst[:, 4:5], epst[:, 3:4],
                                                op=AL.subtract), ["epst"], ["epst"])
                st(V, lambda e: e.reciprocal(epst[:, 5:6], epst[:, 4:5]), ["epst"], ["epst"])
                st(S, lambda e: e.activation(epst[:, 6:7], epst[:, 5:6], AF.Sqrt),
                   ["epst"], ["epst"])
                st(V, lambda e: e.tensor_scalar(ep1f[:], ep1f[:], epst[:, 2:3], epst[:, 6:7],
                                                op0=AL.subtract, op1=AL.mult),
                   ["ep1f"], ["ep1f"], sreads=["epst"], tag="x2")
                dma("sync", out[tc * 128:(tc + 1) * 128, :], ep1f[:], ["ep1f"], [f"o{tc}"])

        def ffn_ln2(grp):
            for cj in range(NH2):
                ffn_ln2_chunk(grp, cj)

        x1_load(0)
        for grp in range(L // (NH2 * 128)):
            for cj in range(NH2):
                tc = grp * NH2 + cj
                sl = tc % 2
                quant(ep2fr[sl][:], D, [f"ep2f{sl}"], xqtm, "xqtm", f"x1q{tc}")
                if grp > 0:
                    ffn_ln2_chunk(grp - 1, cj)
                if tc + 1 < 16:
                    x1_load(tc + 1)
                if tc == DBGTC:
                    dump("xq0", xqtm[:], ["xqtm"], BF16, (128, D))
                for kc in range(8):
                    dma("scalar", xqfm[:, kc * 128:(kc + 1) * 128],
                        xqtm[:, kc * 128:(kc + 1) * 128], ["xqtm", f"xqf{kc}"],
                        [f"xqf{kc}"], tp=True)
                ffn_mm(0, "g")
                ffn_mm(0, "u")
                for isl in range(len(ISL)):
                    if isl + 1 < len(ISL):
                        ffn_mm(isl + 1, "g")
                        ffn_mm(isl + 1, "u")
                    ffn_act(tc, isl, cj)
                if tc == DBGTC:
                    dump("hq0", hqtm[:], ["hqtm"], BF16, (128, 2816))
            ffn_dn(grp)
        ffn_ln2(L // (NH2 * 128) - 1)

    p.fence()
    pass_a("f")
    dump("yscrf", yscr[0], [f"yscr0_{c}" for c in range(16)], BF16, (NXS, 128, L))
    dump("lgscrf", lg_scr[0], [f"lgscr0_{c}" for c in range(16)], F32, (16, 32, Q))
    p.fence()
    pass_b("f")
    dump("accf", acc[:], ["acc"], BF16, (128, 8 * L))
    p.fence()
    pass_a("b")
    p.fence()
    pass_b("b")
    dump("yscrb", yscr[1], [f"yscr1_{c}" for c in range(16)], BF16, (NXS, 128, L))
    dump("lgscrb", lg_scr[1], [f"lgscr1_{c}" for c in range(16)], F32, (16, 32, Q))
    dump("accb", acc[:], ["acc"], BF16, (128, 8 * L))
    p.fence()
    epilogue(wq_safe_units())
    dump("x1", x1_scr, [f"x1s{tc}" for tc in range(16)], F32, (L, D))
    dump("xfm", xfm_scr, XFM, BF16, (8, 128, L))
    p.fence()
    ffn()
    st(G, lambda e: e.engine_nop(), [f"o{tc}" for tc in range(16)], [], "final")

    with (
        nc.semaphore("s_sync") as s0,
        nc.semaphore("s_scalar") as s1,
        nc.semaphore("s_vector") as s2,
        nc.semaphore("s_tensor") as s3,
        nc.semaphore("s_gpsimd") as s4,
        nc.Block() as blk,
    ):
        sems = dict(zip(ENGS, (s0, s1, s2, s3, s4)))

        def body_for(name):
            def body(e):
                my = sems[name]
                for eng, waits, emit, inc, tag in p.steps:
                    if eng != name:
                        continue
                    for s, v in waits.items():
                        e.wait_ge(sems[s], v)
                    last = emit(e)
                    last.then_inc(my, inc)
            return body

        blk.sync(body_for("sync"))
        blk.scalar(body_for("scalar"))
        blk.vector(body_for("vector"))
        blk.tensor(body_for("tensor"))
        blk.gpsimd(body_for("gpsimd"))

    for t in reversed(ctxs):
        t.__exit__(None, None, None)
    return nc, dbg_outs


def host_inputs(inputs, b):
    g = lambda k: np.asarray(inputs[k])
    f32 = np.float32
    m = {}
    m["x"] = np.ascontiguousarray(g("L0_attn_x")[b], f32)
    m["xT"] = np.ascontiguousarray(m["x"].T)
    for d in ("f", "b"):
        W = np.ascontiguousarray(g(f"L0_attn_{d}_in_proj_w"), f32)
        WT = np.ascontiguousarray(W.T)
        wa = np.concatenate([
            WT[:, 2048:4224],
            WT[:, 4160:4224],
            WT[:, 4224:4256],
            np.zeros((D, 32), f32),
        ], axis=1)
        m[f"wa_{d}"] = np.ascontiguousarray(wa)
        m[f"wz_{d}"] = np.ascontiguousarray(WT[:, 0:2048])
        m[f"op_{d}"] = np.ascontiguousarray(g(f"L0_attn_{d}_out_proj_w"), f32)
        half = slice(0, D) if d == "f" else slice(D, 2 * D)
        m[f"ph_{d}"] = np.ascontiguousarray(g("L0_attn_proj_w")[:, half].T, f32)
        cw = np.asarray(g(f"L0_attn_{d}_conv_w"), f32)[:, 0, :]
        cb = np.asarray(g(f"L0_attn_{d}_conv_b"), f32)
        cwc = np.zeros((128, NXBC * 4), f32)
        cbc = np.zeros((128, NXBC), f32)
        for ch in range(2176):
            c, pp = divmod(ch, 128)
            cwc[pp, c * 4:(c + 1) * 4] = cw[:, ch]
            cbc[pp, c] = cb[ch]
        for j in range(64):
            ch = 2112 + j
            cwc[j, 17 * 4:18 * 4] = cw[:, ch]
            cbc[j, 17] = cb[ch]
        m[f"convw_{d}"] = cwc
        m[f"convb_{d}"] = cbc
        m[f"dtb_{d}"] = np.ascontiguousarray(g(f"L0_attn_{d}_dt_bias").reshape(32, 1), f32)
        m[f"alog_{d}"] = np.ascontiguousarray(g(f"L0_attn_{d}_A_log").reshape(32, 1), f32)
        Dp = np.asarray(g(f"L0_attn_{d}_D"), f32)
        dc = np.zeros((128, NXS), f32)
        for pc in range(NXS):
            dc[0:64, pc] = Dp[2 * pc]
            dc[64:128, pc] = Dp[2 * pc + 1]
        m[f"dcol_{d}"] = dc
        rw = np.asarray(g(f"L0_attn_{d}_rms_w"), f32)
        rc = np.zeros((128, NXS), f32)
        for fc in range(NXS):
            rc[:, fc] = rw[fc * 128:(fc + 1) * 128]
        m[f"rmscol_{d}"] = rc
    pb = np.asarray(g("L0_attn_proj_b"), f32)
    pbc = np.zeros((128, 8), f32)
    for dcx in range(8):
        pbc[:, dcx] = pb[dcx * 128:(dcx + 1) * 128]
    m["pbcol"] = pbc
    m["lnw_a"] = np.ascontiguousarray(g("L0_attn_ln_w").reshape(1, D), f32)
    m["lnb_a"] = np.ascontiguousarray(g("L0_attn_ln_b").reshape(1, D), f32)
    m["wg"] = np.ascontiguousarray(g("L0_ffn_w_gate"), f32)
    m["wu"] = np.ascontiguousarray(g("L0_ffn_w_up"), f32)
    m["wdn"] = np.ascontiguousarray(g("L0_ffn_w_down"), f32)
    si = np.arange(128)[:, None]
    ti = np.arange(128)[None, :]
    m["maskF"] = (si <= ti).astype(f32)
    m["maskB"] = (si >= ti).astype(f32)
    m["idf32"] = np.eye(128, dtype=f32)
    m["ones_bf"] = np.ones((128, 1), np.float32).astype(ml_dtypes.bfloat16)
    ih = np.zeros((128, HP), np.float32)
    ih[64:128, :] = np.eye(64, dtype=np.float32)
    m["i64hi"] = ih.astype(ml_dtypes.bfloat16)
    return m


def kernel(**inputs):
    nc, _ = build_program()
    in_maps = [host_inputs(inputs, b) for b in range(8)]
    res = run_bass_kernel_spmd(nc, in_maps, core_ids=list(range(8)))
    return np.stack([res.results[b]["out"] for b in range(8)], axis=0)


# version 24
# speedup vs baseline: 1.0414x; 1.0069x over previous
import numpy as np
from collections import defaultdict

import concourse.bass as bass
import concourse.mybir as mybir
from concourse.bass_utils import run_bass_kernel_spmd
import ml_dtypes

F32 = mybir.dt.float32
BF16 = mybir.dt.bfloat16
AL = mybir.AluOpType
AF = mybir.ActivationFunctionType
AX = mybir.AxisListType

L = 2048
D = 1024
Q = 128
SUP = 512
NSUP = L // SUP
NCQ = SUP // Q
NH = 32
HP = 64
NXS = 16
NXBC = 18
WA_COLS = NXBC * 128
HH = 16
EPS = 1e-5
W6 = float(np.float32(1.0) / np.float32(6.0))
TAU6 = [float(np.float32(t) / np.float32(6.0)) for t in (0.25, 0.75, 1.25, 1.75, 2.5, 3.5, 5.0)]
ENGS = ("sync", "scalar", "vector", "tensor", "gpsimd")
import os
KDBG = set(x for x in os.environ.get("KDBG", "").split(",") if x)
DBGTC = int(os.environ.get("DBGTC", "0"))


class Prog:
    def __init__(self):
        self.steps = []
        self.cnt = defaultdict(int)
        self.res = {}
        self.last_wait = defaultdict(dict)
        self.pending = {}

    def fence(self):
        snap = dict(self.cnt)
        for eng in ENGS:
            pf = self.pending.setdefault(eng, {})
            for o, v in snap.items():
                if o != eng and v > 0 and pf.get(o, -1) < v:
                    pf[o] = v

    def step(self, eng, emit, reads=(), writes=(), dma=False, sreads=(), tag=""):
        waits = {}

        def need(deng, dval, force=False):
            if deng == eng and not dma and not force:
                return
            if waits.get(deng, -1) < dval:
                waits[deng] = dval

        for o, v in self.pending.pop(eng, {}).items():
            need(o, v)
        for r in sreads:
            e = self.res.get(r)
            if e and e["w"]:
                need(*e["w"], force=True)
        for r in reads:
            e = self.res.get(r)
            if e and e["w"]:
                need(*e["w"], force=(e["w"][0] == eng and eng != "tensor"))
        for w in writes:
            e = self.res.get(w)
            if e:
                if e["w"]:
                    need(*e["w"])
                for rr in e["r"]:
                    need(*rr)
        fw = {}
        for s, v in waits.items():
            if self.last_wait[eng].get(s, -1) < v:
                fw[s] = v
                self.last_wait[eng][s] = v
        inc = 16 if dma else 1
        self.cnt[eng] += inc
        self.steps.append((eng, fw, emit, inc, tag))
        dep = (eng, self.cnt[eng])
        for w in writes:
            self.res[w] = {"w": dep, "r": []}
        for r in list(reads) + list(sreads):
            self.res.setdefault(r, {"w": None, "r": []})["r"].append(dep)


def build_program():
    nc = bass.Bass("TRN2", target_bir_lowering=False)
    dr = {}

    def din(name, shape, dt=F32):
        dr[name] = nc.dram_tensor(name, list(shape), dt, kind="ExternalInput").ap()
        return dr[name]

    x_in = din("x", (L, D))
    din("xT", (D, L))
    for d in ("f", "b"):
        din(f"wa_{d}", (D, WA_COLS))
        din(f"wz_{d}", (D, 2048))
        din(f"op_{d}", (D, 2048))
        din(f"ph_{d}", (D, D))
        din(f"convw_{d}", (128, NXBC * 4))
        din(f"convb_{d}", (128, NXBC))
        din(f"dtb_{d}", (32, 1))
        din(f"alog_{d}", (32, 1))
        din(f"dcol_{d}", (128, NXS))
        din(f"rmscol_{d}", (128, NXS))
    din("pbcol", (128, 8))
    din("lnw_a", (1, D))
    din("lnb_a", (1, D))
    din("wg", (2816, D))
    din("wu", (2816, D))
    din("wdn", (D, 2816))
    din("maskF", (128, 128))
    din("maskB", (128, 128))
    din("idf32", (128, 128))
    din("ones_bf", (128, 1), BF16)
    din("i64hi", (128, HP), BF16)

    out = nc.dram_tensor("out", [L, D], F32, kind="ExternalOutput").ap()
    xfm_scr = nc.dram_tensor("xfm_scr", [8, 128, L], BF16, kind="Internal").ap()
    yscr = nc.dram_tensor("yscr", [2, NXS, 128, L], BF16, kind="Internal").ap()
    lg_scr = nc.dram_tensor("lg_scr", [2, 16, 32, Q], F32, kind="Internal").ap()
    x1_scr = nc.dram_tensor("x1_scr", [L, D], F32, kind="Internal").ap()
    rs_scr = nc.dram_tensor("rs_scr", [2, NSUP, 1, SUP], F32, kind="Internal").ap()

    ctxs = []

    def sbuf(name, shape, dt):
        t = nc.sbuf_tensor(name, list(shape), dt)
        h = t.__enter__()
        ctxs.append(t)
        return h

    def psum_t(name):
        t = nc.psum_tensor(name, [128, 512], F32)
        h = t.__enter__()
        ctxs.append(t)
        return h

    consts_b = sbuf("consts_b", [128, 2 + HP], BF16)
    ONES = consts_b[:, 0:1]
    I64HI = consts_b[:, 2:2 + HP]
    masks = sbuf("masks", [128, 3 * 128], F32)
    convw = sbuf("convw", [128, 2 * NXBC * 4], F32)
    convb = sbuf("convb", [128, 2 * NXBC], F32)
    smallf = sbuf("smallf", [128, 16], F32)
    zeros128 = sbuf("zeros128", [128, 128], F32)
    dcol = sbuf("dcol", [128, 2 * NXS], F32)
    rmscol = sbuf("rmscol", [128, 2 * NXS], F32)
    pbcol = sbuf("pbcol_s", [128, 8], F32)
    epst = sbuf("epst", [128, 16], F32)
    hstate = sbuf("hstate", [128, NH * HP], BF16)
    acc = sbuf("acc", [128, 8 * L], BF16)

    ARA = sbuf("arA", [128, 22784], BF16)
    ARB = sbuf("arB", [128, 18432], BF16)
    ARC = sbuf("arC", [128, 18432], BF16)
    ARD = sbuf("arD", [128, 16384], BF16)
    ARE = sbuf("arE", [128, 10240], BF16)

    wab = ARA[:, 0:18432]
    xsup = ARA[:, 18432:22528]
    xbc = ARB[:, 0:9324]
    xsbc = ARB[:, 9344:13952]
    bctm = ARB[:, 13952:14208]
    cbm = ARB[:, 16384:16640]
    bbar = ARB[:, 14336:15360]
    aqi = ARB[:, 15360:16384]
    utm = ARC[:, 0:4096]
    ysb = ARC[:, 4096:8192]
    gam = ARC[:, 8192:10240]
    smeg = ARC[:, 10240:12288]
    almeg = ARC[:, 12288:14336]
    cameg = ARC[:, 14336:16384]
    adf = ARD[:].bitcast(F32)
    convt = adf[:, 0:2304]
    delt = adf[:, 2304:4352]
    alin = adf[:, 4352:6400]
    dtraw = adf[:, 6400:6912]
    dtsp = adf[:, 6912:7424]
    sptmp = adf[:, 7424:7936]
    lgfm = adf[:, 7936:8064]
    dtlg_tm = adf[:, 8064:8192]
    opstage = ARA[:, 0:16384]
    phring = ARA[:, 16384:18432]
    wzb = ARB[:, 0:16384]
    rstdr = ARB[:, 16384:17408].bitcast(F32)
    evtmp = ARB[:, 17408:17920]
    weff = ARC[:, 0:16384]
    ysup = ARD[:, 0:8192]
    zsil = ARD[:, 8192:16384]
    sqb = ysup
    ygat = ARE[:, 0:8192]
    etm = ARE[:, 0:2048]
    xtmf = ARE[:, 2048:4096].bitcast(F32)
    ep1 = ARE[:, 4096:6144].bitcast(F32)
    lnrep = ARE[:, 6144:10240].bitcast(F32)
    wq_g = [ARA[:, i * 2816:(i + 1) * 2816] for i in range(8)]
    wq_u = [ARB[:, i * 2816:(i + 1) * 2816] for i in range(6)] + \
           [ARC[:, i * 2816:(i + 1) * 2816] for i in range(2)]
    wq_d = [ARC[:, 5632 + i * 1024: 5632 + (i + 1) * 1024] for i in range(12)] + \
           [acc[:, i * 1024:(i + 1) * 1024] for i in range(10)]
    ftA = ARD[:, 0:2048].bitcast(F32)
    ftB = ARD[:, 2048:4096].bitcast(F32)
    ftC = ARD[:, 4096:6144].bitcast(F32)
    fsgn = ARD[:, 6144:8192].bitcast(F32)
    fstage = ARD[:, 8192:10240].bitcast(F32)
    fsc = ARD[:, 10240:10624].bitcast(F32)
    ephr = [ARD[:, 10624:11648].bitcast(F32), ARD[:, 13696:14720].bitcast(F32)]
    ep2fr = [ARD[:, 11648:13696].bitcast(F32)[:, 0:1024],
             ARD[:, 8192:10240].bitcast(F32)]
    qout = acc[:, 10240:11264]
    hqtm = acc[:, 11264:14080]
    xqtm = acc[:, 14080:15104]
    xqfm = acc[:, 15104:16128]
    hqfm = ARE[:, 0:5632]
    dntm = ARE[:, 5632:7680]
    ep1f = ARE[:, 7680:9728].bitcast(F32)
    dtmp = ARE[:, 9728:10240]

    pmm = [psum_t("ps_mm0"), psum_t("ps_mm1")]
    pcb = psum_t("ps_cb")
    py = [psum_t("ps_y0"), psum_t("ps_y1")]
    ph_ = [psum_t("ps_h0"), psum_t("ps_h1")]
    pst = psum_t("ps_st")
    PB8 = [pmm[0], pmm[1], pcb, py[0], py[1], ph_[0], ph_[1], pst]
    PB8N = ["pmm0", "pmm1", "pcb", "py0", "py1", "ph0", "ph1", "pst"]

    p = Prog()
    V, S, G, T = "vector", "scalar", "gpsimd", "tensor"
    dbg_outs = []

    def dump(name, src_ap, reads, dt, shape):
        if name not in KDBG:
            return
        o = nc.dram_tensor("dbg_" + name, list(shape), dt, kind="ExternalOutput").ap()
        dbg_outs.append("dbg_" + name)

        def em(e, o=o, s=src_ap):
            return e.dma_start(out=o, in_=s)
        p.step("sync", em, reads=reads, writes=[f"dbg{name}"], dma=True, tag="dump " + name)

    def dma(eng, outap, inap, reads, writes, tag="", tp=False):
        if tp:
            eng = "sync"
        elif eng == "synck":
            eng = "sync"
        elif eng == "sync":
            eng = "scalar"

        def em(e, o=outap, i=inap, tp=tp):
            if tp:
                return e.dma_start(out=o, in_=i, transpose=True)
            return e.dma_start(out=o, in_=i)
        p.step(eng, em, reads=reads, writes=writes, dma=True, tag=tag)

    def st(eng, emit, reads=(), writes=(), sreads=(), tag=""):
        p.step(eng, emit, reads=reads, writes=writes, sreads=sreads, tag=tag)

    def bcast_f(ap2d, inner):
        return ap2d.unsqueeze(2).broadcast_to([ap2d.shape[0], ap2d.shape[1], inner])

    dma("sync", consts_b[:, 0:1], dr["ones_bf"], [], ["ones"])
    dma("sync", consts_b[:, 2:2 + HP], dr["i64hi"], [], ["i64hi"])
    dma("sync", masks[:, 0:128], dr["maskF"], [], ["maskF"])
    dma("sync", masks[:, 128:256], dr["maskB"], [], ["maskB"])
    dma("sync", masks[:, 256:384], dr["idf32"], [], ["idf32"])
    for i, d in enumerate(("f", "b")):
        dma("sync", convw[:, i * NXBC * 4:(i + 1) * NXBC * 4], dr[f"convw_{d}"], [], [f"convw{d}"])
        dma("sync", convb[:, i * NXBC:(i + 1) * NXBC], dr[f"convb_{d}"], [], [f"convb{d}"])
        dma("sync", smallf[64:96, i:i + 1], dr[f"dtb_{d}"], ["smallf"], ["smallf"])
        dma("sync", smallf[64:96, 4 + i:5 + i], dr[f"alog_{d}"], ["smallf"], ["smallf"])
        dma("sync", dcol[:, i * NXS:(i + 1) * NXS], dr[f"dcol_{d}"], [], [f"dcol{d}"])
        dma("sync", rmscol[:, i * NXS:(i + 1) * NXS], dr[f"rmscol_{d}"], [], [f"rmscol{d}"])
    dma("sync", pbcol[:], dr["pbcol"], [], ["pbcol"])
    st(V, lambda e: e.memset(zeros128[:], 0.0), [], ["zeros128"])
    for i, d in enumerate(("f", "b")):
        st(S, lambda e, i=i: e.activation(smallf[64:96, 6 + i:7 + i],
                                          smallf[64:96, 4 + i:5 + i], AF.Exp),
           ["smallf"], ["smallf"], "Aexp")
        st(V, lambda e, i=i: e.tensor_scalar(smallf[64:96, 2 + i:3 + i],
                                             smallf[64:96, 6 + i:7 + i],
                                             -1.0, None, op0=AL.mult),
           ["smallf"], ["smallf"], "Aneg")

    p0f = [ARE[:, 0:4096].bitcast(F32), ARE[:, 4096:8192].bitcast(F32)]
    p0b = [ARD[:, 0:2048], ARD[:, 2048:4096]]
    for fc in range(8):
        r = fc % 2
        dma("gpsimd", p0f[r][:], dr["xT"][fc * 128:(fc + 1) * 128, :],
            [f"p0f{r}"], [f"p0f{r}"])
        st(V, lambda e, r=r: e.tensor_copy(p0b[r][:], p0f[r][:]),
           [f"p0f{r}", f"p0b{r}"], [f"p0b{r}"], "x cast")
        dma("sync", xfm_scr[fc, :, :], p0b[r][:], [f"p0b{r}"], [f"xfms{fc}", f"p0b{r}"])
        st(V, lambda e, fc=fc, r=r: e.tensor_scalar(
            acc[:, fc * L:(fc + 1) * L], p0f[r][:], pbcol[:, fc:fc + 1], None, op0=AL.add),
           [f"p0f{r}", "pbcol"], ["acc"], "accinit")
    XFM = [f"xfms{fc}" for fc in range(8)]

    def pass_a(d):
        di = 0 if d == "f" else 1
        fwd = d == "f"
        for k in range(8):
            dma("gpsimd", wab[:, k * WA_COLS:(k + 1) * WA_COLS],
                dr[f"wa_{d}"][k * 128:(k + 1) * 128, :], ["wab"], ["wab", f"wa_{k}"])
        st(V, lambda e: e.memset(xbc[:], 0.0), ["xbc"], ["xbc"], "zero xbc")
        st(V, lambda e: e.memset(hstate[64:128, :], 0.0), ["hstate"], ["hstate"], "zero h")

        sup_order = list(range(NSUP)) if fwd else list(range(NSUP - 1, -1, -1))
        for si, sp in enumerate(sup_order):
            if si > 0:
                xv = xbc.rearrange("p (c w) -> p c w", c=NXBC)
                if fwd:
                    st(V, lambda e, xv=xv: e.tensor_copy(xv[:, :, 0:3], xv[:, :, SUP:SUP + 3]),
                       ["xbc"], ["xbc"], "halo")
                else:
                    st(V, lambda e, xv=xv: e.tensor_copy(xv[:, :, SUP + 3:SUP + 6], xv[:, :, 3:6]),
                       ["xbc"], ["xbc"], "halo")
            dma("sync", xsup[:].rearrange("p (c t) -> p c t", c=8),
                xfm_scr[:, :, sp * SUP:(sp + 1) * SUP].transpose([1, 0, 2]),
                XFM + ["xsup"], ["xsup"])
            for oc in range(NXBC):
                pb = pmm[oc % 2]
                pres = f"pmm{oc % 2}"

                def em_mm(e, oc=oc, pb=pb):
                    last = None
                    for k in range(8):
                        last = e.matmul(pb[:, 0:SUP],
                                        wab[:, k * WA_COLS + oc * 128: k * WA_COLS + (oc + 1) * 128],
                                        xsup[:, k * SUP:(k + 1) * SUP],
                                        start=(k == 0), stop=(k == 7))
                    return last
                st(T, em_mm, [f"wa_{k}" for k in range(8)] + ["xsup", pres], [pres], "A mm")
                if oc < NXBC - 1:
                    st(S, lambda e, oc=oc, pb=pb: e.activation(
                        xbc[:, oc * (SUP + 6) + 3: oc * (SUP + 6) + 3 + SUP],
                        pb[:, 0:SUP], AF.Copy),
                       [pres, "xbc"], ["xbc"], "ev xbc")
                else:
                    st(S, lambda e, oc=oc, pb=pb: e.activation(
                        xbc[0:64, oc * (SUP + 6) + 3: oc * (SUP + 6) + 3 + SUP],
                        pb[0:64, 0:SUP], AF.Copy),
                       [pres, "xbc"], ["xbc"], "ev cdup")
                    st(S, lambda e, pb=pb: e.activation(dtraw[64:96, :],
                                                        pb[64:96, 0:SUP], AF.Copy),
                       [pres, "dtraw"], ["dtraw"], "ev dt")
            st(S, lambda e, i=di: e.activation(sptmp[64:96, :], dtraw[64:96, :], AF.Abs,
                                               bias=smallf[64:96, i:i + 1]),
               ["dtraw", "smallf", "sptmp"], ["sptmp"], "sp abs")
            st(S, lambda e: e.activation(sptmp[64:96, :], sptmp[64:96, :], AF.Exp, scale=-1.0),
               ["sptmp"], ["sptmp"], "sp exp")
            st(S, lambda e: e.activation(sptmp[64:96, :], sptmp[64:96, :], AF.Ln, bias=1.0),
               ["sptmp"], ["sptmp"], "sp ln")
            st(V, lambda e, i=di: e.tensor_scalar(dtsp[64:96, :], dtraw[64:96, :],
                                                  smallf[64:96, i:i + 1], 0.0,
                                                  op0=AL.add, op1=AL.max),
               ["dtraw", "smallf", "dtsp"], ["dtsp"], "sp relu")
            st(V, lambda e: e.tensor_tensor(dtsp[64:96, :], dtsp[64:96, :], sptmp[64:96, :],
                                            op=AL.add),
               ["dtsp", "sptmp"], ["dtsp"], "sp add")

            chunk_order = list(range(NCQ)) if fwd else list(range(NCQ - 1, -1, -1))

            def stage_A(ci):
                c_glob = sp * NCQ + ci
                toff = ci * Q
                ring = c_glob % 2
                XS_R = f"xsbc{ring}"
                xsb_o = ring * NXBC * 128
                dtlg = dtlg_tm[:, (ring) * 64:(ring + 1) * 64]
                DTLG = f"dtlg{ring}"
                cbmr = cbm[:, ring * 128:(ring + 1) * 128]
                CBM = f"cbm{ring}"

                cwv = convw[:, di * NXBC * 4:(di + 1) * NXBC * 4].rearrange(
                    "p (c k) -> p c k", k=4)

                xbcv = xbc[:].rearrange("p (c w) -> p c w", c=NXBC)

                def xview(k, ci=ci, fwd=fwd, xbcv=xbcv):
                    base = ci * Q + 3 + ((k - 3) if fwd else k)
                    return xbcv[:, :, base: base + Q]

                def wview(k, fwd=fwd):
                    kk = k if fwd else 3 - k
                    return cwv[:, :, kk:kk + 1].broadcast_to([128, NXBC, Q])

                cbv = convb[:, di * NXBC:(di + 1) * NXBC].unsqueeze(2) \
                    .broadcast_to([128, NXBC, Q])
                mtmp = xsbc[:, xsb_o:xsb_o + NXBC * 128].rearrange(
                    "p (c w) -> p c w", w=128)
                cvv = convt[:].rearrange("p (c w) -> p c w", w=128)
                st(V, lambda e: e.tensor_tensor(cvv, xview(0), wview(0), op=AL.mult),
                   ["xbc", f"convw{d}", "convt"], ["convt"], "conv0")
                for k in (1, 2, 3):
                    st(V, lambda e, k=k: e.tensor_tensor(mtmp, xview(k), wview(k), op=AL.mult),
                       ["xbc", f"convw{d}", XS_R], [XS_R], f"convm{k}")
                    st(V, lambda e: e.tensor_tensor(cvv, cvv, mtmp, op=AL.add),
                       [XS_R, "convt"], ["convt"], f"conva{k}")
                st(V, lambda e: e.tensor_tensor(cvv, cvv, cbv, op=AL.add),
                   [f"convb{d}", "convt"], ["convt"], "convb")
                st(S, lambda e, o=xsb_o: e.activation(xsbc[:, o:o + NXBC * 128], convt[:], AF.Silu),
                   ["convt", XS_R], [XS_R], "silu")
                if fwd:
                    st(V, lambda e, toff=toff, i=di: e.tensor_scalar(
                        sptmp[64:96, toff:toff + Q], dtsp[64:96, toff:toff + Q],
                        smallf[64:96, 2 + i:3 + i], None, op0=AL.mult),
                       ["dtsp", "smallf", "sptmp"], ["sptmp"], "dtA")
                else:
                    st(V, lambda e, toff=toff, i=di: e.tensor_scalar(
                        sptmp[64:96, toff:toff + Q], dtsp[64:96, toff:toff + Q],
                        smallf[64:96, 2 + i:3 + i], 0.0, op0=AL.mult, op1=AL.add,
                        accum_out=smallf[64:96, 8:9]),
                       ["dtsp", "smallf", "sptmp"], ["sptmp", "smallf"], "dtA")
                st(V, lambda e, toff=toff: e.tensor_tensor_scan(
                    lgfm[64:96, :], sptmp[64:96, toff:toff + Q], zeros128[64:96, :],
                    0.0, op0=AL.add, op1=AL.add),
                   ["sptmp", "zeros128", "lgfm"], ["lgfm"], "scan")
                if not fwd:
                    st(V, lambda e, toff=toff: e.scalar_tensor_tensor(
                        lgfm[64:96, :], lgfm[64:96, :], -1.0, sptmp[64:96, toff:toff + Q],
                        op0=AL.mult, op1=AL.add),
                       ["lgfm", "sptmp"], ["lgfm"], "suf1")
                    st(V, lambda e: e.tensor_scalar(lgfm[64:96, :], lgfm[64:96, :],
                                                    smallf[64:96, 8:9], None, op0=AL.add),
                       ["lgfm"], ["lgfm"], sreads=["smallf"], tag="suf2")
                dma("sync", lg_scr[di, c_glob, :, :], lgfm[64:96, :],
                    ["lgfm"], [f"lgscr{di}_{c_glob}"])
                id32 = masks[64:96, 256 + 64:256 + 96]
                st(T, lambda e, toff=toff, id32=id32: e.transpose(
                    pcb[:, 128:160], dtsp[64:96, toff:toff + Q], id32),
                   ["dtsp", "idf32", "pcb"], ["pcb"], "tp dt")
                st(T, lambda e, id32=id32: e.transpose(pcb[:, 160:192], lgfm[64:96, :], id32),
                   ["lgfm", "idf32", "pcb"], ["pcb"], "tp lg")
                st(V, lambda e, dtlg=dtlg: e.tensor_copy(dtlg, pcb[:, 128:192]),
                   ["pcb", DTLG], [DTLG], "ev dtlg")
                for fc in range(NXS):
                    dma("scalar", utm[:, ring * 2048 + fc * 128: ring * 2048 + (fc + 1) * 128],
                        xsbc[:, xsb_o + fc * 128: xsb_o + (fc + 1) * 128],
                        [XS_R, f"utm{ring}_{fc}"], [f"utm{ring}_{fc}"], tp=True)
                dma("scalar", bctm[:, ring * 128:(ring + 1) * 128],
                    xsbc[:, xsb_o + 16 * 128: xsb_o + 17 * 128],
                    [XS_R, f"bctm{ring}"], [f"bctm{ring}"], tp=True)

            def stage_A_post(ci):
                c_glob = sp * NCQ + ci
                ring = c_glob % 2
                XS_R = f"xsbc{ring}"
                xsb_o = ring * NXBC * 128
                dtlg = dtlg_tm[:, (ring) * 64:(ring + 1) * 64]
                DTLG = f"dtlg{ring}"
                cbmr = cbm[:, ring * 128:(ring + 1) * 128]
                CBM = f"cbm{ring}"
                UTM = [f"utm{ring}_{fc}" for fc in range(NXS)]
                ur = utm[:, ring * 2048:(ring + 1) * 2048]
                st(V, lambda e, ur=ur, dtlg=dtlg: e.tensor_tensor(
                    ur.rearrange("p (h w) -> p h w", h=NH),
                    ur.rearrange("p (h w) -> p h w", h=NH),
                    bcast_f(dtlg[:, 0:32], HP), op=AL.mult),
                   UTM + [DTLG], UTM, "U dt")
                st(T, lambda e, o=xsb_o: e.matmul(
                    pcb[:, 0:128],
                    xsbc[0:64, o + 16 * 128: o + 16 * 128 + 128],
                    xsbc[0:64, o + 17 * 128: o + 17 * 128 + 128],
                    start=True, stop=True),
                   [XS_R, "pcb"], ["pcb"], "CB")
                st(V, lambda e, cbmr=cbmr, m=(masks[:, 0:128] if fwd else masks[:, 128:256]):
                   e.tensor_tensor(cbmr, pcb[:, 0:128], m, op=AL.mult),
                   ["pcb", "maskF" if fwd else "maskB", CBM], [CBM], "CBm")

            def stage_B(ci, hh):
                c_glob = sp * NCQ + ci
                ring = c_glob % 2
                XS_R = f"xsbc{ring}"
                xsb_o = ring * NXBC * 128
                dtlg = dtlg_tm[:, ring * 64:(ring + 1) * 64]
                DTLG = f"dtlg{ring}"
                cbmr = cbm[:, ring * 128:(ring + 1) * 128]
                CBM = f"cbm{ring}"
                wt = (Q - 1) if fwd else 0
                dma("sync", alin[:],
                    lg_scr[di, c_glob, hh * HH:(hh + 1) * HH, :]
                    .rearrange("h t -> (h t)").unsqueeze(0).partition_broadcast(128),
                    [f"lgscr{di}_{c_glob}", "alin"], ["alin"])
                st(V, lambda e, hh=hh, dtlg=dtlg: e.tensor_tensor(
                    delt[:].rearrange("p (h t) -> p h t", h=HH),
                    alin[:].rearrange("p (h t) -> p h t", h=HH),
                    bcast_f(dtlg[:, 32 + hh * HH:32 + (hh + 1) * HH], Q),
                    op=AL.subtract),
                   ["alin", DTLG, "delt"], ["delt"], "dsub")
                st(S, lambda e: e.activation(gam[:], delt[:], AF.Exp),
                   ["delt", "gam"], ["gam"], "gamma")
                st(S, lambda e: e.activation(almeg[64:128, :], alin[64:128, :], AF.Exp),
                   ["alin", "almeg"], ["almeg"], "alpha")
                st(V, lambda e, cbmr=cbmr: e.scalar_tensor_tensor(
                    smeg[:].rearrange("p (h t) -> p h t", h=HH),
                    gam[:].rearrange("p (h t) -> p h t", h=HH),
                    1.0,
                    cbmr.unsqueeze(1).broadcast_to([128, HH, Q]),
                    op0=AL.min, op1=AL.mult),
                   ["gam", CBM, "smeg"], ["smeg"], "S")
                st(V, lambda e, wt=wt: e.tensor_scalar(
                    epst[:, 0:HH],
                    gam[:].rearrange("p (h t) -> p h t", h=HH)[:, :, wt],
                    1.0, None, op0=AL.min),
                   ["gam", "epst"], ["epst"], "gcl")
                st(V, lambda e, o=xsb_o: e.tensor_tensor(
                    cameg[64:128, :].rearrange("p (h t) -> p h t", h=HH),
                    xsbc[64:128, o + 16 * 128: o + 17 * 128]
                    .unsqueeze(1).broadcast_to([64, HH, Q]),
                    almeg[64:128, :].rearrange("p (h t) -> p h t", h=HH), op=AL.mult),
                   [XS_R, "almeg", "cameg"], ["cameg"], "Ca")
                st(V, lambda e, ring=ring: e.tensor_tensor(
                    bbar[:].rearrange("p (h w) -> p h w", h=HH),
                    bctm[:, ring * 128: ring * 128 + 64].unsqueeze(1)
                    .broadcast_to([128, HH, HP]),
                    bcast_f(epst[:, 0:HH], HP), op=AL.mult),
                   ["epst", f"bctm{ring}", "bbar"], ["bbar"], "Bbar")
                st(V, lambda e, wt=wt: e.tensor_tensor(
                    aqi[64:128, :].rearrange("p (h w) -> p h w", h=HH),
                    I64HI[64:128, :].unsqueeze(1).broadcast_to([64, HH, HP]),
                    almeg[64:128, :].rearrange("p (h t) -> p h t", h=HH)[:, :, wt:wt + 1]
                    .broadcast_to([64, HH, HP]), op=AL.mult),
                   ["almeg", "i64hi", "aqi"], ["aqi"], "aQI")

            def stage_C_mm(ci, hh):
                c_glob = sp * NCQ + ci
                ring = c_glob % 2
                XS_R = f"xsbc{ring}"
                xsb_o = ring * NXBC * 128

                def em_pairs(e, hh=hh, ring=ring):
                    last = None
                    started = {}
                    for pl in range(8):
                        pc = hh * 8 + pl
                        bidx = pl // 4
                        yb = py[bidx]
                        ycol = (pl % 4) * 128
                        for par in range(2):
                            hloc = 2 * pl + par
                            last = e.matmul(
                                yb[par * 64:(par + 1) * 64, ycol:ycol + 128],
                                utm[:, ring * 2048 + pc * 128 + par * 64:
                                    ring * 2048 + pc * 128 + par * 64 + 64],
                                smeg[:, hloc * Q:(hloc + 1) * Q],
                                start=not started.get((bidx, par), False), stop=False,
                                tile_position=(0, par * 64))
                            started[(bidx, par)] = True
                    for pl in range(8):
                        pc = hh * 8 + pl
                        yb = py[pl // 4]
                        ycol = (pl % 4) * 128
                        for par in range(2):
                            hloc = 2 * pl + par
                            h_glob = 2 * pc + par
                            last = e.matmul(
                                yb[par * 64:(par + 1) * 64, ycol:ycol + 128],
                                hstate[64:128, h_glob * HP:(h_glob + 1) * HP],
                                cameg[64:128, hloc * Q:(hloc + 1) * Q],
                                start=False,
                                stop=(pl % 4 == 3 and par == 1),
                                tile_position=(64, par * 64))
                    return last
                st(T, em_pairs, [f"utm{ring}_{fc}" for fc in range(NXS)]
                   + ["smeg", "hstate", "cameg", "py0", "py1"],
                   ["py0", "py1"], "Y mms")

                def em_upd(e, hh=hh, ring=ring):
                    last = None
                    started = [False, False]
                    for hl in range(HH):
                        h_glob = hh * HH + hl
                        bidx = hl // 8
                        hb = ph_[bidx]
                        hcol = (hl % 8) * 64
                        last = e.matmul(hb[64:128, hcol:hcol + 64],
                                        bbar[:, hl * HP:(hl + 1) * HP],
                                        utm[:, ring * 2048 + h_glob * HP:
                                            ring * 2048 + (h_glob + 1) * HP],
                                        start=not started[bidx], stop=False,
                                        tile_position=(0, 64))
                        started[bidx] = True
                    for hl in range(HH):
                        h_glob = hh * HH + hl
                        hb = ph_[hl // 8]
                        hcol = (hl % 8) * 64
                        last = e.matmul(hb[64:128, hcol:hcol + 64],
                                        aqi[64:128, hl * HP:(hl + 1) * HP],
                                        hstate[64:128, h_glob * HP:(h_glob + 1) * HP],
                                        start=False, stop=(hl % 8 == 7),
                                        tile_position=(64, 64))
                    return last
                st(T, em_upd, [f"utm{ring}_{fc}" for fc in range(NXS)]
                   + ["bbar", "aqi", "hstate", "ph0", "ph1"],
                   ["ph0", "ph1"], "h mms")

            def stage_C_ev(ci, hh):
                c_glob = sp * NCQ + ci
                ring = c_glob % 2
                XS_R = f"xsbc{ring}"
                xsb_o = ring * NXBC * 128
                for pl in range(8):
                    pc = hh * 8 + pl
                    yb = py[pl // 4]
                    ycol = (pl % 4) * 128
                    st(V, lambda e, pc=pc, yb=yb, ycol=ycol, ring=ring, i=di, o=xsb_o:
                       e.scalar_tensor_tensor(
                           ysb[:, ring * 2048 + pc * 128: ring * 2048 + (pc + 1) * 128],
                           xsbc[:, o + pc * 128: o + (pc + 1) * 128],
                           dcol[:, i * NXS + pc: i * NXS + pc + 1],
                           yb[:, ycol:ycol + 128], op0=AL.mult, op1=AL.add),
                       [f"py{pl // 4}", XS_R, f"dcol{d}", f"ysb{ring}"],
                       [f"ysb{ring}"], "Y ev")
                for bb in range(2):
                    st(V, lambda e, bb=bb, hh=hh: e.tensor_copy(
                        hstate[64:128, hh * HH * HP + bb * 512:
                               hh * HH * HP + (bb + 1) * 512],
                        ph_[bb][64:128, 0:512]),
                       [f"ph{bb}", "hstate"], ["hstate"], "h ev")

            def stage_store(ci):
                c_glob = sp * NCQ + ci
                ring = c_glob % 2
                dma("sync", yscr[di, :, :, c_glob * Q:(c_glob + 1) * Q].transpose([1, 0, 2]),
                    ysb[:, ring * 2048:(ring + 1) * 2048].rearrange("p (c t) -> p c t", c=NXS),
                    [f"ysb{ring}"], [f"yscr{di}_{c_glob}"])

            stage_A(chunk_order[0])
            stage_A_post(chunk_order[0])
            stage_B(chunk_order[0], 0)
            for idx, ci in enumerate(chunk_order):
                stage_C_mm(ci, 0)
                if idx + 1 < len(chunk_order):
                    stage_A(chunk_order[idx + 1])
                stage_B(ci, 1)
                stage_C_ev(ci, 0)
                if idx + 1 < len(chunk_order):
                    stage_A_post(chunk_order[idx + 1])
                stage_C_mm(ci, 1)
                if idx + 1 < len(chunk_order):
                    stage_B(chunk_order[idx + 1], 0)
                stage_C_ev(ci, 1)
                stage_store(ci)

    def pass_b(d):
        di = 0 if d == "f" else 1
        for k in range(8):
            dma("gpsimd", opstage[:, k * 2048:(k + 1) * 2048],
                dr[f"op_{d}"][k * 128:(k + 1) * 128, :], ["opstage"], ["opstage", f"op_{k}"])
        phr = [phring[:, 0:1024], phring[:, 1024:2048]]
        for grp in range(4):
            for k in range(8):
                sl = k % 2
                dma("gpsimd", phr[sl], dr[f"ph_{d}"][k * 128:(k + 1) * 128, :],
                    [f"phr{sl}"], [f"phr{sl}"])
                for fci in range(4):
                    fc = grp * 4 + fci
                    for half in range(2):
                        bi = fci * 2 + half
                        pb = PB8[bi]
                        pres = PB8N[bi]

                        def em_we(e, fc=fc, half=half, pb=pb, k=k, sl=sl):
                            return e.matmul(
                                pb[:, 0:512],
                                opstage[:, k * 2048 + fc * 128: k * 2048 + (fc + 1) * 128],
                                phr[sl][:, half * 512:(half + 1) * 512],
                                start=(k == 0), stop=(k == 7))
                        st(T, em_we, [f"op_{k}", f"phr{sl}"] + ([pres] if k == 0 else []),
                           [pres], "Weff mm")
            for fci in range(4):
                fc = grp * 4 + fci
                for half in range(2):
                    bi = fci * 2 + half
                    st(V, lambda e, fc=fc, half=half, pb=PB8[bi], i=di: e.tensor_scalar(
                        weff[:, fc * D + half * 512: fc * D + (half + 1) * 512],
                        pb[:, 0:512], rmscol[:, i * NXS + fc: i * NXS + fc + 1],
                        None, op0=AL.mult),
                       [PB8N[bi], f"rmscol{d}", "weff"], ["weff"], "Weff ev")
        for k in range(8):
            dma("gpsimd", wzb[:, k * 2048:(k + 1) * 2048],
                dr[f"wz_{d}"][k * 128:(k + 1) * 128, :],
                ["wzb", "xbc", "xsbc0", "xsbc1", "bctm0", "bctm1", "cbm", "bbar", "aqi"],
                ["wzb", f"wz_{k}"])
        for sp in range(NSUP):
            dma("sync", xsup[:].rearrange("p (c t) -> p c t", c=8),
                xfm_scr[:, :, sp * SUP:(sp + 1) * SUP].transpose([1, 0, 2]),
                XFM + ["xsup"], ["xsup"])
            dma("sync", ysup[:].rearrange("p (c t) -> p c t", c=NXS),
                yscr[di, :, :, sp * SUP:(sp + 1) * SUP].transpose([1, 0, 2]),
                [f"yscr{di}_{sp * NCQ + i}" for i in range(NCQ)] +
                ["ysup", "convt", "delt", "alin", "dtraw", "dtsp", "sptmp", "lgfm", "dtlg"],
                ["ysup"])
            for fc in range(NXS):
                pb = pmm[fc % 2]
                pres = f"pmm{fc % 2}"

                def em_z(e, fc=fc, pb=pb):
                    last = None
                    for k in range(8):
                        last = e.matmul(pb[:, 0:SUP],
                                        wzb[:, k * 2048 + fc * 128: k * 2048 + (fc + 1) * 128],
                                        xsup[:, k * SUP:(k + 1) * SUP],
                                        start=(k == 0), stop=(k == 7))
                    return last
                st(T, em_z, [f"wz_{k}" for k in range(8)] + ["xsup", pres], [pres], "z mm")
                st(S, lambda e, fc=fc, pb=pb: e.activation(
                    zsil[:, fc * SUP:(fc + 1) * SUP], pb[:, 0:SUP], AF.Silu),
                   [pres, "zsil"], ["zsil"], "z silu")
            st(V, lambda e: e.tensor_tensor(ygat[:], ysup[:], zsil[:], op=AL.mult),
               ["ysup", "zsil", "ygat", "phr0", "phr1"], ["ygat"], "gate")
            st(S, lambda e: e.activation(sqb[:], ygat[:], AF.Square),
               ["ygat", "ysup"], ["ysup"], "sq")

            def em_rms(e):
                last = None
                for fc in range(NXS):
                    last = e.matmul(pst[0:1, 0:SUP], ONES,
                                    sqb[:, fc * SUP:(fc + 1) * SUP],
                                    start=(fc == 0), stop=(fc == NXS - 1))
                return last
            st(T, em_rms, ["ysup", "ones", "pst"], ["pst"], "rms sum")
            st(V, lambda e: e.tensor_scalar(rstdr[0:1, 0:SUP], pst[0:1, 0:SUP],
                                            1.0 / 2048.0, EPS, op0=AL.mult, op1=AL.add),
               ["pst", "rstdr"], ["rstdr"], "rms pre")
            dma("sync", rs_scr[di, sp, :, :], rstdr[0:1, 0:SUP], ["rstdr"], [f"rs{di}{sp}"])
            dma("sync", rstdr[:], rs_scr[di, sp, :, :].partition_broadcast(128),
                [f"rs{di}{sp}", "rstdr"], ["rstdr"])
            st(V, lambda e: e.reciprocal(rstdr[:], rstdr[:]), ["rstdr"], ["rstdr"], "recip")
            st(S, lambda e: e.activation(rstdr[:], rstdr[:], AF.Sqrt), ["rstdr"], ["rstdr"], "sqrt")
            for dm in range(8):
                pb = pmm[dm % 2]
                pres = f"pmm{dm % 2}"

                def em_op(e, dm=dm, pb=pb):
                    last = None
                    for fc in range(NXS):
                        last = e.matmul(pb[:, 0:SUP],
                                        weff[:, fc * D + dm * 128: fc * D + (dm + 1) * 128],
                                        ygat[:, fc * SUP:(fc + 1) * SUP],
                                        start=(fc == 0), stop=(fc == NXS - 1))
                    return last
                st(T, em_op, ["weff", "ygat", pres], [pres], "oproj mm")
                st(V, lambda e, pb=pb: e.tensor_tensor(
                    evtmp[:], pb[:, 0:SUP], rstdr[:], op=AL.mult),
                   [pres, "rstdr", "evtmp"], ["evtmp"], "oproj ev")
                st(V, lambda e, dm=dm, sp=sp: e.tensor_tensor(
                    acc[:, dm * L + sp * SUP: dm * L + (sp + 1) * SUP],
                    acc[:, dm * L + sp * SUP: dm * L + (sp + 1) * SUP],
                    evtmp[:], op=AL.add),
                   ["evtmp", "acc"], ["acc"], "acc add")

    def epilogue(units=()):
        uit = iter(units)
        dma("sync", lnrep[:, 0:D], dr["lnw_a"].partition_broadcast(128),
            ["lnrep"], ["lnrep"])
        dma("sync", lnrep[:, D:2 * D], dr["lnb_a"].partition_broadcast(128),
            ["lnrep"], ["lnrep"])
        for tc in range(16):
            for dc in range(8):
                dma("scalar", etm[:, dc * 128:(dc + 1) * 128],
                    acc[:, dc * L + tc * 128: dc * L + (tc + 1) * 128],
                    ["acc", "etm"], ["etm"], tp=True)
            dma("sync", xtmf[:], x_in[tc * 128:(tc + 1) * 128, :], ["xtmf"], ["xtmf"])
            st(V, lambda e: e.tensor_reduce(epst[:, 0:1], etm[:, 0:D], axis=AX.X, op=AL.add),
               ["etm", "epst"], ["epst"])
            st(S, lambda e: e.activation(ep1[:], etm[:, 0:D], AF.Square),
               ["etm", "ep1"], ["ep1"])
            st(V, lambda e: e.tensor_reduce(epst[:, 1:2], ep1[:], axis=AX.X, op=AL.add),
               ["ep1", "epst"], ["epst"])
            st(V, lambda e: e.tensor_scalar(epst[:, 2:3], epst[:, 0:1], 1.0 / D, None,
                                            op0=AL.mult), ["epst"], ["epst"])
            st(V, lambda e: e.tensor_tensor(epst[:, 3:4], epst[:, 2:3], epst[:, 2:3],
                                            op=AL.mult), ["epst"], ["epst"])
            st(V, lambda e: e.tensor_scalar(epst[:, 4:5], epst[:, 1:2], 1.0 / D, EPS,
                                            op0=AL.mult, op1=AL.add), ["epst"], ["epst"])
            st(V, lambda e: e.tensor_tensor(epst[:, 4:5], epst[:, 4:5], epst[:, 3:4],
                                            op=AL.subtract), ["epst"], ["epst"])
            st(V, lambda e: e.reciprocal(epst[:, 5:6], epst[:, 4:5]), ["epst"], ["epst"])
            st(S, lambda e: e.activation(epst[:, 6:7], epst[:, 5:6], AF.Sqrt),
               ["epst"], ["epst"])
            st(V, lambda e: e.tensor_scalar(ep1[:], etm[:, 0:D], epst[:, 2:3], epst[:, 6:7],
                                            op0=AL.subtract, op1=AL.mult),
               ["etm", "ep1"], ["ep1"], sreads=["epst"], tag="ln1")
            st(V, lambda e: e.tensor_tensor(ep1[:], ep1[:], lnrep[:, 0:D], op=AL.mult),
               ["ep1", "lnrep"], ["ep1"])
            st(V, lambda e: e.tensor_tensor(ep1[:], ep1[:], lnrep[:, D:2 * D], op=AL.add),
               ["ep1", "lnrep"], ["ep1"])
            st(V, lambda e: e.tensor_tensor(ep1[:], ep1[:], xtmf[:], op=AL.add),
               ["ep1", "xtmf"], ["ep1"], "h1")
            st(V, lambda e: e.tensor_reduce(epst[:, 8:9], ep1[:], axis=AX.X, op=AL.add),
               ["ep1", "epst"], ["epst"])
            st(S, lambda e: e.activation(xtmf[:], ep1[:], AF.Square),
               ["ep1", "xtmf"], ["xtmf"])
            st(V, lambda e: e.tensor_reduce(epst[:, 9:10], xtmf[:], axis=AX.X, op=AL.add),
               ["xtmf", "epst"], ["epst"])
            st(V, lambda e: e.tensor_scalar(epst[:, 10:11], epst[:, 8:9], 1.0 / D, None,
                                            op0=AL.mult), ["epst"], ["epst"])
            st(V, lambda e: e.tensor_tensor(epst[:, 11:12], epst[:, 10:11], epst[:, 10:11],
                                            op=AL.mult), ["epst"], ["epst"])
            st(V, lambda e: e.tensor_scalar(epst[:, 12:13], epst[:, 9:10], 1.0 / D, EPS,
                                            op0=AL.mult, op1=AL.add), ["epst"], ["epst"])
            st(V, lambda e: e.tensor_tensor(epst[:, 12:13], epst[:, 12:13], epst[:, 11:12],
                                            op=AL.subtract), ["epst"], ["epst"])
            st(V, lambda e: e.reciprocal(epst[:, 13:14], epst[:, 12:13]), ["epst"], ["epst"])
            st(S, lambda e: e.activation(epst[:, 14:15], epst[:, 13:14], AF.Sqrt),
               ["epst"], ["epst"])
            st(V, lambda e: e.tensor_scalar(ep1[:], ep1[:], epst[:, 10:11], epst[:, 14:15],
                                            op0=AL.subtract, op1=AL.mult),
               ["ep1"], ["ep1"], sreads=["epst"], tag="x1")
            dma("sync", x1_scr[tc * 128:(tc + 1) * 128, :], ep1[:], ["ep1"], [f"x1s{tc}"])
            for _ in range(4):
                u = next(uit, None)
                if u is not None:
                    u()
        for u in uit:
            u()

    def quant(src_ap, width, res_in, dst_ap, dst_res, tag, ceng=V):
        nb = width // 16
        dump_fsc = tag == f"hq{DBGTC}_0" and "fsc0" in KDBG
        dump_q5 = tag == f"hq{DBGTC}_5" and "q5" in KDBG
        st(V, lambda e: e.tensor_scalar(fsgn[:, 0:width], src_ap, 0.0, 2.0,
                                        op0=AL.is_ge, op1=AL.mult),
           list(res_in) + ["fsgn"], ["fsgn"], tag + " sgn0")
        st(V, lambda e: e.scalar_tensor_tensor(ftA[:, 0:width], fsgn[:, 0:width],
                                               -1.0, src_ap, op0=AL.add, op1=AL.mult),
           list(res_in) + ["fsgn", "ftA"], ["ftA"], tag + " abs")

        av = ftA[:, 0:nb * 16].rearrange("p (b w) -> p b w", w=16)
        r8 = ftC[:, 0:nb * 8].rearrange("p (b w) -> p b w", w=8)
        r4 = ftC[:, nb * 8:nb * 12].rearrange("p (b w) -> p b w", w=4)
        r2 = ftC[:, nb * 12:nb * 14].rearrange("p (b w) -> p b w", w=2)
        st(V, lambda e, nb=nb, av=av: e.tensor_tensor(
            ftC[:, 0:nb * 8], av[:, :, 0:8], av[:, :, 8:16], op=AL.max),
           ["ftA", "ftC"], ["ftC"], tag + " bm0")
        st(V, lambda e, nb=nb, r8=r8: e.tensor_tensor(
            ftC[:, nb * 8:nb * 12], r8[:, :, 0:4], r8[:, :, 4:8], op=AL.max),
           ["ftC"], ["ftC"], tag + " bm1")
        st(V, lambda e, nb=nb, r4=r4: e.tensor_tensor(
            ftC[:, nb * 12:nb * 14], r4[:, :, 0:2], r4[:, :, 2:4], op=AL.max),
           ["ftC"], ["ftC"], tag + " bm2")
        st(V, lambda e, nb=nb, r2=r2: e.tensor_tensor(
            fsc[:, 0:nb], r2[:, :, 0], r2[:, :, 1], op=AL.max),
           ["ftC", "fsc"], ["fsc"], tag + " bm3")
        if dump_q5:
            dump("q5ftA", ftA[:], ["ftA"], F32, (128, 1024))
            dump("q5fscraw", fsc[:], ["fsc"], F32, (128, 192))
        st(V, lambda e: e.tensor_scalar(fsc[:, 64:64 + nb], fsc[:, 0:nb], 6e-12, None,
                                        op0=AL.max),
           ["fsc"], ["fsc"], tag + " clamp")
        st(V, lambda e: e.reciprocal(fsc[:, 128:128 + nb], fsc[:, 64:64 + nb]),
           ["fsc"], ["fsc"], tag + " recip")
        if dump_fsc:
            dump("fsc0", fsc[:], ["fsc"], F32, (128, 192))
        st(V, lambda e: e.tensor_tensor(
            ftB[:, 0:width].rearrange("p (b w) -> p b w", w=16),
            ftA[:, 0:width].rearrange("p (b w) -> p b w", w=16),
            bcast_f(fsc[:, 128:128 + nb], 16), op=AL.mult),
           ["ftA", "fsc", "ftB"], ["ftB"], tag + " xn")

        if ceng == V:
            def em_cmp(e):
                a = ftB[:, 0:width]
                g = ftC[:, 0:width]
                u = ftA[:, 0:width]
                e.tensor_scalar(g, a, TAU6[0], None, op0=AL.is_gt)
                for i in (1, 2, 3):
                    e.scalar_tensor_tensor(g, a, TAU6[i], g, op0=AL.is_gt, op1=AL.add)
                e.tensor_scalar(u, a, TAU6[4], None, op0=AL.is_gt)
                e.scalar_tensor_tensor(u, a, TAU6[5], u, op0=AL.is_gt, op1=AL.add)
                e.scalar_tensor_tensor(g, g, 0.5, u, op0=AL.mult, op1=AL.add)
                e.tensor_scalar(u, a, TAU6[6], 2.0, op0=AL.is_gt, op1=AL.mult)
                return e.tensor_tensor(g, g, u, op=AL.add)
        else:
            def em_cmp(e):
                a = ftB[:, 0:width]
                g = ftC[:, 0:width]
                u = ftA[:, 0:width]
                e.tensor_scalar(g, a, TAU6[0], 0.5, op0=AL.is_gt, op1=AL.mult)
                for i, w in ((1, 0.5), (2, 0.5), (3, 0.5), (4, 1.0), (5, 1.0), (6, 2.0)):
                    e.tensor_scalar(u, a, TAU6[i], w, op0=AL.is_gt, op1=AL.mult)
                    e.tensor_tensor(g, g, u, op=AL.add)
                return e.engine_nop()
        st(ceng, em_cmp, ["ftB", "ftA", "ftC"], ["ftC", "ftA"], tag + " cmp")
        st(V, lambda e: e.scalar_tensor_tensor(
            ftB[:, 0:width].rearrange("p (b w) -> p b w", w=16),
            ftC[:, 0:width].rearrange("p (b w) -> p b w", w=16),
            W6, bcast_f(fsc[:, 64:64 + nb], 16), op0=AL.mult, op1=AL.mult),
           ["ftC", "fsc", "ftB"], ["ftB"], tag + " scale")
        st(V, lambda e: e.scalar_tensor_tensor(
            dst_ap[:, 0:width] if dst_ap.shape[1] != width else dst_ap,
            fsgn[:, 0:width], -1.0, ftB[:, 0:width], op0=AL.add, op1=AL.mult),
           ["ftB", "fsgn", dst_res], [dst_res], tag + " q")

    wq_ping = [0]
    fstgr = [fstage, ARD[:, 11648:13696].bitcast(F32)]

    def wq_unit_gu(wname, pc):
        hbm = dr["wg"] if wname == "g" else dr["wu"]
        wregs = wq_g if wname == "g" else wq_u
        ID32 = masks[:, 256:384]
        sl = wq_ping[0] % 2
        wq_ping[0] += 1
        FS = "fstg" if sl == 0 else "fstg1"
        dma("synck", fstgr[sl][:, 0:1024], hbm[pc * 128:(pc + 1) * 128, :],
            [FS], [FS])
        quant(fstgr[sl][:, 0:1024], 1024, [FS], ftA, "ftA", f"wq{wname}{pc}")
        for kc in range(8):
            bk = py[kc % 2]
            bn = f"py{kc % 2}"
            col = (kc // 2) * 128
            st(T, lambda e, bk=bk, col=col, kc=kc: e.transpose(
                bk[:, col:col + 128], ftA[:, kc * 128:(kc + 1) * 128], ID32),
               ["ftA", "idf32", bn], [bn], "wtp")
            st(S, lambda e, bk=bk, col=col, kc=kc, pc=pc, wregs=wregs: e.activation(
                wregs[kc][:, pc * 128:(pc + 1) * 128], bk[:, col:col + 128], AF.Copy),
               [bn, f"wq{wname}_{pc}_{kc}"], [f"wq{wname}_{pc}_{kc}"], "wtpev")

    def wq_unit_d(pc, ss, wdt):
        ID32 = masks[:, 256:384]
        sl = wq_ping[0] % 2
        wq_ping[0] += 1
        FS = "fstg" if sl == 0 else "fstg1"
        dma("synck", fstgr[sl][:, 0:wdt],
            dr["wdn"][pc * 128:(pc + 1) * 128, ss:ss + wdt], [FS], [FS])
        quant(fstgr[sl][:, 0:wdt], wdt, [FS], ftA, "ftA", f"wqd{pc}_{ss}")
        for ic in range(wdt // 128):
            icg = ss // 128 + ic
            bk = py[ic % 2]
            bn = f"py{ic % 2}"
            col = (ic // 2) * 128
            st(T, lambda e, bk=bk, col=col, ic=ic: e.transpose(
                bk[:, col:col + 128], ftA[:, ic * 128:(ic + 1) * 128], ID32),
               ["ftA", "idf32", bn], [bn], "wtp")
            st(S, lambda e, bk=bk, col=col, icg=icg, pc=pc: e.activation(
                wq_d[icg][:, pc * 128:(pc + 1) * 128], bk[:, col:col + 128], AF.Copy),
               [bn, f"wqd_{pc}_{icg}"], [f"wqd_{pc}_{icg}"], "wtpev")

    def wq_safe_units():
        us = []
        for wname in ("g", "u"):
            for pc in range(22):
                us.append(lambda wname=wname, pc=pc: wq_unit_gu(wname, pc))
        for pc in range(8):
            us.append(lambda pc=pc: wq_unit_d(pc, 0, 1024))
        return us

    def ffn():
        for pc in range(8):
            for ss, wdt in ((1024, 1024), (2048, 768)):
                wq_unit_d(pc, ss, wdt)
        dump("wqg0", wq_g[0][:], [f"wqg_{pc}_0" for pc in range(22)], BF16, (128, 2816))
        WQG = [f"wq{w}_{pc}_{kc}" for w in "gu" for pc in range(22) for kc in range(8)]
        WQD = [f"wqd_{pc}_{icg}" for pc in range(8) for icg in range(22)]
        ISL = [(0, 512), (512, 512), (1024, 512), (1536, 512), (2048, 512), (2560, 256)]
        NH2 = 2
        gb = [pmm[0], pcb]
        ub = [pmm[1], pst]
        gbn = ["pmm0", "pcb"]
        ubn = ["pmm1", "pst"]

        def x1_load(tc):
            sl = tc % 2
            wr = ["fstg"] if sl == 1 else ["fstg1"]
            dma("sync", ep2fr[sl][:], x1_scr[tc * 128:(tc + 1) * 128, :],
                [f"ep2f{sl}"] + wr, [f"ep2f{sl}"] + wr)

        def ffn_mm(isl, which):
            io, iw = ISL[isl]
            b = (gb if which == "g" else ub)[isl % 2]
            bn = (gbn if which == "g" else ubn)[isl % 2]
            wq = wq_g if which == "g" else wq_u

            def em(e, io=io, iw=iw, b=b, wq=wq):
                last = None
                for kc in range(8):
                    last = e.matmul(b[:, 0:iw],
                                    xqfm[:, kc * 128:(kc + 1) * 128],
                                    wq[kc][:, io:io + iw],
                                    start=(kc == 0), stop=(kc == 7))
                return last
            st(T, em, [f"xqf{k}" for k in range(8)] + WQG + [bn], [bn],
               which + "mm")

        def ffn_act(tc, isl, cj):
            io, iw = ISL[isl]
            ep = ephr[isl % 2]
            EPR = f"eph{isl % 2}"
            st(S, lambda e, iw=iw, ep=ep: e.activation(ep[:, 0:iw],
                                                       gb[isl % 2][:, 0:iw], AF.Silu),
               [gbn[isl % 2], EPR], [EPR], "fsilu")
            st(V, lambda e, iw=iw, ep=ep: e.tensor_tensor(ep[:, 0:iw], ep[:, 0:iw],
                                                          ub[isl % 2][:, 0:iw], op=AL.mult),
               [ubn[isl % 2], EPR], [EPR], "hmul")
            quant(ep[:, 0:iw], iw, [EPR], hqtm[:, io:io + iw], "hqtm", f"hq{tc}_{isl}")
            for ic in range(io // 128, (io + iw) // 128):
                dma("scalar", hqfm[:, ic * NH2 * 128 + cj * 128:
                                   ic * NH2 * 128 + (cj + 1) * 128],
                    hqtm[:, ic * 128:(ic + 1) * 128],
                    ["hqtm", f"hqf{cj}_{ic}"], [f"hqf{cj}_{ic}"], tp=True)

        def ffn_dn(grp):
            HQF = [f"hqf{cj}_{ic}" for cj in range(NH2) for ic in range(22)]
            for dm in range(8):
                pb = pmm[dm % 2]
                pres = f"pmm{dm % 2}"

                def em_dn(e, dm=dm, pb=pb):
                    last = None
                    for ic in range(22):
                        last = e.matmul(pb[:, 0:NH2 * 128],
                                        wq_d[ic][:, dm * 128:(dm + 1) * 128],
                                        hqfm[:, ic * NH2 * 128:(ic + 1) * NH2 * 128],
                                        start=(ic == 0), stop=(ic == 21))
                    return last
                st(T, em_dn, HQF + WQD + [pres], [pres], "dn mm")
                st(S, lambda e, pb=pb: e.activation(dtmp[:, 0:NH2 * 128],
                                                    pb[:, 0:NH2 * 128], AF.Copy),
                   [pres, "dtmp"], ["dtmp"], "dn ev")
                for cj in range(NH2):
                    dma("scalar", dntm[:, cj * D + dm * 128: cj * D + (dm + 1) * 128],
                        dtmp[:, cj * 128:(cj + 1) * 128],
                        ["dtmp", f"dntp{cj}_{dm}"], [f"dntp{cj}_{dm}"], tp=True)
            if grp == DBGTC // 2:
                dump("dn0", dntm[:, (DBGTC % 2) * D:(DBGTC % 2) * D + D],
                     [f"dntp{DBGTC % 2}_{dm}" for dm in range(8)], BF16, (128, D))

        def ffn_ln2_chunk(grp, cj):
            if True:
                tc = grp * NH2 + cj
                DNT = [f"dntp{cj}_{dm}" for dm in range(8)]
                dma("gpsimd", ep1f[:], x1_scr[tc * 128:(tc + 1) * 128, :],
                    ["ep1f"], ["ep1f"])
                st(V, lambda e, cj=cj: e.tensor_tensor(
                    ep1f[:], ep1f[:], dntm[:, cj * D:(cj + 1) * D], op=AL.add),
                   ["ep1f"] + DNT, ["ep1f"], "h2")
                st(V, lambda e: e.tensor_reduce(epst[:, 0:1], ep1f[:], axis=AX.X, op=AL.add),
                   ["ep1f", "epst"], ["epst"])
                st(S, lambda e: e.activation(ftB[:, 0:D], ep1f[:], AF.Square),
                   ["ep1f", "ftB"], ["ftB"])
                st(V, lambda e: e.tensor_reduce(epst[:, 1:2], ftB[:, 0:D], axis=AX.X, op=AL.add),
                   ["ftB", "epst"], ["epst"])
                st(V, lambda e: e.tensor_scalar(epst[:, 2:3], epst[:, 0:1], 1.0 / D, None,
                                                op0=AL.mult), ["epst"], ["epst"])
                st(V, lambda e: e.tensor_tensor(epst[:, 3:4], epst[:, 2:3], epst[:, 2:3],
                                                op=AL.mult), ["epst"], ["epst"])
                st(V, lambda e: e.tensor_scalar(epst[:, 4:5], epst[:, 1:2], 1.0 / D, EPS,
                                                op0=AL.mult, op1=AL.add), ["epst"], ["epst"])
                st(V, lambda e: e.tensor_tensor(epst[:, 4:5], epst[:, 4:5], epst[:, 3:4],
                                                op=AL.subtract), ["epst"], ["epst"])
                st(V, lambda e: e.reciprocal(epst[:, 5:6], epst[:, 4:5]), ["epst"], ["epst"])
                st(S, lambda e: e.activation(epst[:, 6:7], epst[:, 5:6], AF.Sqrt),
                   ["epst"], ["epst"])
                st(V, lambda e: e.tensor_scalar(ep1f[:], ep1f[:], epst[:, 2:3], epst[:, 6:7],
                                                op0=AL.subtract, op1=AL.mult),
                   ["ep1f"], ["ep1f"], sreads=["epst"], tag="x2")
                dma("gpsimd", out[tc * 128:(tc + 1) * 128, :], ep1f[:], ["ep1f"], [f"o{tc}"])

        def ffn_ln2(grp):
            for cj in range(NH2):
                ffn_ln2_chunk(grp, cj)

        x1_load(0)
        for grp in range(L // (NH2 * 128)):
            for cj in range(NH2):
                tc = grp * NH2 + cj
                sl = tc % 2
                quant(ep2fr[sl][:], D, [f"ep2f{sl}"], xqtm, "xqtm", f"x1q{tc}")
                if grp > 0:
                    ffn_ln2_chunk(grp - 1, cj)
                if tc + 1 < 16:
                    x1_load(tc + 1)
                if tc == DBGTC:
                    dump("xq0", xqtm[:], ["xqtm"], BF16, (128, D))
                for kc in range(8):
                    dma("scalar", xqfm[:, kc * 128:(kc + 1) * 128],
                        xqtm[:, kc * 128:(kc + 1) * 128], ["xqtm", f"xqf{kc}"],
                        [f"xqf{kc}"], tp=True)
                ffn_mm(0, "g")
                ffn_mm(0, "u")
                for isl in range(len(ISL)):
                    if isl + 1 < len(ISL):
                        ffn_mm(isl + 1, "g")
                        ffn_mm(isl + 1, "u")
                    ffn_act(tc, isl, cj)
                if tc == DBGTC:
                    dump("hq0", hqtm[:], ["hqtm"], BF16, (128, 2816))
            ffn_dn(grp)
        ffn_ln2(L // (NH2 * 128) - 1)

    p.fence()
    pass_a("f")
    dump("yscrf", yscr[0], [f"yscr0_{c}" for c in range(16)], BF16, (NXS, 128, L))
    dump("lgscrf", lg_scr[0], [f"lgscr0_{c}" for c in range(16)], F32, (16, 32, Q))
    p.fence()
    pass_b("f")
    dump("accf", acc[:], ["acc"], BF16, (128, 8 * L))
    p.fence()
    pass_a("b")
    p.fence()
    pass_b("b")
    dump("yscrb", yscr[1], [f"yscr1_{c}" for c in range(16)], BF16, (NXS, 128, L))
    dump("lgscrb", lg_scr[1], [f"lgscr1_{c}" for c in range(16)], F32, (16, 32, Q))
    dump("accb", acc[:], ["acc"], BF16, (128, 8 * L))
    p.fence()
    epilogue(wq_safe_units())
    dump("x1", x1_scr, [f"x1s{tc}" for tc in range(16)], F32, (L, D))
    dump("xfm", xfm_scr, XFM, BF16, (8, 128, L))
    p.fence()
    ffn()
    st(G, lambda e: e.engine_nop(), [f"o{tc}" for tc in range(16)], [], "final")

    with (
        nc.semaphore("s_sync") as s0,
        nc.semaphore("s_scalar") as s1,
        nc.semaphore("s_vector") as s2,
        nc.semaphore("s_tensor") as s3,
        nc.semaphore("s_gpsimd") as s4,
        nc.Block() as blk,
    ):
        sems = dict(zip(ENGS, (s0, s1, s2, s3, s4)))

        def body_for(name):
            def body(e):
                my = sems[name]
                for eng, waits, emit, inc, tag in p.steps:
                    if eng != name:
                        continue
                    for s, v in waits.items():
                        e.wait_ge(sems[s], v)
                    last = emit(e)
                    last.then_inc(my, inc)
            return body

        blk.sync(body_for("sync"))
        blk.scalar(body_for("scalar"))
        blk.vector(body_for("vector"))
        blk.tensor(body_for("tensor"))
        blk.gpsimd(body_for("gpsimd"))

    for t in reversed(ctxs):
        t.__exit__(None, None, None)
    return nc, dbg_outs


def host_inputs(inputs, b):
    g = lambda k: np.asarray(inputs[k])
    f32 = np.float32
    m = {}
    m["x"] = np.ascontiguousarray(g("L0_attn_x")[b], f32)
    m["xT"] = np.ascontiguousarray(m["x"].T)
    for d in ("f", "b"):
        W = np.ascontiguousarray(g(f"L0_attn_{d}_in_proj_w"), f32)
        WT = np.ascontiguousarray(W.T)
        wa = np.concatenate([
            WT[:, 2048:4224],
            WT[:, 4160:4224],
            WT[:, 4224:4256],
            np.zeros((D, 32), f32),
        ], axis=1)
        m[f"wa_{d}"] = np.ascontiguousarray(wa)
        m[f"wz_{d}"] = np.ascontiguousarray(WT[:, 0:2048])
        m[f"op_{d}"] = np.ascontiguousarray(g(f"L0_attn_{d}_out_proj_w"), f32)
        half = slice(0, D) if d == "f" else slice(D, 2 * D)
        m[f"ph_{d}"] = np.ascontiguousarray(g("L0_attn_proj_w")[:, half].T, f32)
        cw = np.asarray(g(f"L0_attn_{d}_conv_w"), f32)[:, 0, :]
        cb = np.asarray(g(f"L0_attn_{d}_conv_b"), f32)
        cwc = np.zeros((128, NXBC * 4), f32)
        cbc = np.zeros((128, NXBC), f32)
        for ch in range(2176):
            c, pp = divmod(ch, 128)
            cwc[pp, c * 4:(c + 1) * 4] = cw[:, ch]
            cbc[pp, c] = cb[ch]
        for j in range(64):
            ch = 2112 + j
            cwc[j, 17 * 4:18 * 4] = cw[:, ch]
            cbc[j, 17] = cb[ch]
        m[f"convw_{d}"] = cwc
        m[f"convb_{d}"] = cbc
        m[f"dtb_{d}"] = np.ascontiguousarray(g(f"L0_attn_{d}_dt_bias").reshape(32, 1), f32)
        m[f"alog_{d}"] = np.ascontiguousarray(g(f"L0_attn_{d}_A_log").reshape(32, 1), f32)
        Dp = np.asarray(g(f"L0_attn_{d}_D"), f32)
        dc = np.zeros((128, NXS), f32)
        for pc in range(NXS):
            dc[0:64, pc] = Dp[2 * pc]
            dc[64:128, pc] = Dp[2 * pc + 1]
        m[f"dcol_{d}"] = dc
        rw = np.asarray(g(f"L0_attn_{d}_rms_w"), f32)
        rc = np.zeros((128, NXS), f32)
        for fc in range(NXS):
            rc[:, fc] = rw[fc * 128:(fc + 1) * 128]
        m[f"rmscol_{d}"] = rc
    pb = np.asarray(g("L0_attn_proj_b"), f32)
    pbc = np.zeros((128, 8), f32)
    for dcx in range(8):
        pbc[:, dcx] = pb[dcx * 128:(dcx + 1) * 128]
    m["pbcol"] = pbc
    m["lnw_a"] = np.ascontiguousarray(g("L0_attn_ln_w").reshape(1, D), f32)
    m["lnb_a"] = np.ascontiguousarray(g("L0_attn_ln_b").reshape(1, D), f32)
    m["wg"] = np.ascontiguousarray(g("L0_ffn_w_gate"), f32)
    m["wu"] = np.ascontiguousarray(g("L0_ffn_w_up"), f32)
    m["wdn"] = np.ascontiguousarray(g("L0_ffn_w_down"), f32)
    si = np.arange(128)[:, None]
    ti = np.arange(128)[None, :]
    m["maskF"] = (si <= ti).astype(f32)
    m["maskB"] = (si >= ti).astype(f32)
    m["idf32"] = np.eye(128, dtype=f32)
    m["ones_bf"] = np.ones((128, 1), np.float32).astype(ml_dtypes.bfloat16)
    ih = np.zeros((128, HP), np.float32)
    ih[64:128, :] = np.eye(64, dtype=np.float32)
    m["i64hi"] = ih.astype(ml_dtypes.bfloat16)
    return m


def kernel(**inputs):
    nc, _ = build_program()
    in_maps = [host_inputs(inputs, b) for b in range(8)]
    res = run_bass_kernel_spmd(nc, in_maps, core_ids=list(range(8)))
    return np.stack([res.results[b]["out"] for b in range(8)], axis=0)


# version 26
# speedup vs baseline: 1.0428x; 1.0013x over previous
import numpy as np
from collections import defaultdict

import concourse.bass as bass
import concourse.mybir as mybir
from concourse.bass_utils import run_bass_kernel_spmd
import ml_dtypes

F32 = mybir.dt.float32
BF16 = mybir.dt.bfloat16
AL = mybir.AluOpType
AF = mybir.ActivationFunctionType
AX = mybir.AxisListType

L = 2048
D = 1024
Q = 128
SUP = 512
NSUP = L // SUP
NCQ = SUP // Q
NH = 32
HP = 64
NXS = 16
NXBC = 18
WA_COLS = NXBC * 128
HH = 16
EPS = 1e-5
W6 = float(np.float32(1.0) / np.float32(6.0))
TAU6 = [float(np.float32(t) / np.float32(6.0)) for t in (0.25, 0.75, 1.25, 1.75, 2.5, 3.5, 5.0)]
ENGS = ("sync", "scalar", "vector", "tensor", "gpsimd")
import os
KDBG = set(x for x in os.environ.get("KDBG", "").split(",") if x)
DBGTC = int(os.environ.get("DBGTC", "0"))


class Prog:
    def __init__(self):
        self.steps = []
        self.cnt = defaultdict(int)
        self.res = {}
        self.last_wait = defaultdict(dict)
        self.pending = {}

    def fence(self):
        snap = dict(self.cnt)
        for eng in ENGS:
            pf = self.pending.setdefault(eng, {})
            for o, v in snap.items():
                if o != eng and v > 0 and pf.get(o, -1) < v:
                    pf[o] = v

    def step(self, eng, emit, reads=(), writes=(), dma=False, sreads=(), tag=""):
        waits = {}

        def need(deng, dval, force=False):
            if deng == eng and not dma and not force:
                return
            if waits.get(deng, -1) < dval:
                waits[deng] = dval

        for o, v in self.pending.pop(eng, {}).items():
            need(o, v)
        for r in sreads:
            e = self.res.get(r)
            if e and e["w"]:
                need(*e["w"], force=True)
        for r in reads:
            e = self.res.get(r)
            if e and e["w"]:
                need(*e["w"], force=(e["w"][0] == eng and eng != "tensor"))
        for w in writes:
            e = self.res.get(w)
            if e:
                if e["w"]:
                    need(*e["w"])
                for rr in e["r"]:
                    need(*rr)
        fw = {}
        for s, v in waits.items():
            if self.last_wait[eng].get(s, -1) < v:
                fw[s] = v
                self.last_wait[eng][s] = v
        inc = 16 if dma else 1
        self.cnt[eng] += inc
        self.steps.append((eng, fw, emit, inc, tag))
        dep = (eng, self.cnt[eng])
        for w in writes:
            self.res[w] = {"w": dep, "r": []}
        for r in list(reads) + list(sreads):
            self.res.setdefault(r, {"w": None, "r": []})["r"].append(dep)


def build_program():
    nc = bass.Bass("TRN2", target_bir_lowering=False)
    dr = {}

    def din(name, shape, dt=F32):
        dr[name] = nc.dram_tensor(name, list(shape), dt, kind="ExternalInput").ap()
        return dr[name]

    x_in = din("x", (L, D))
    din("xT", (D, L))
    for d in ("f", "b"):
        din(f"wa_{d}", (D, WA_COLS))
        din(f"wz_{d}", (D, 2048))
        din(f"op_{d}", (D, 2048))
        din(f"ph_{d}", (D, D))
        din(f"convw_{d}", (128, NXBC * 4))
        din(f"convb_{d}", (128, NXBC))
        din(f"dtb_{d}", (32, 1))
        din(f"alog_{d}", (32, 1))
        din(f"dcol_{d}", (128, NXS))
        din(f"rmscol_{d}", (128, NXS))
    din("pbcol", (128, 8))
    din("lnw_a", (1, D))
    din("lnb_a", (1, D))
    din("wg", (2816, D))
    din("wu", (2816, D))
    din("wdn", (D, 2816))
    din("maskF", (128, 128))
    din("maskB", (128, 128))
    din("idf32", (128, 128))
    din("ones_bf", (128, 1), BF16)
    din("i64hi", (128, HP), BF16)

    out = nc.dram_tensor("out", [L, D], F32, kind="ExternalOutput").ap()
    xfm_scr = nc.dram_tensor("xfm_scr", [8, 128, L], BF16, kind="Internal").ap()
    yscr = nc.dram_tensor("yscr", [2, NXS, 128, L], BF16, kind="Internal").ap()
    lg_scr = nc.dram_tensor("lg_scr", [2, 16, 32, Q], F32, kind="Internal").ap()
    x1_scr = nc.dram_tensor("x1_scr", [L, D], F32, kind="Internal").ap()
    rs_scr = nc.dram_tensor("rs_scr", [2, NSUP, 1, SUP], F32, kind="Internal").ap()

    ctxs = []

    def sbuf(name, shape, dt):
        t = nc.sbuf_tensor(name, list(shape), dt)
        h = t.__enter__()
        ctxs.append(t)
        return h

    def psum_t(name):
        t = nc.psum_tensor(name, [128, 512], F32)
        h = t.__enter__()
        ctxs.append(t)
        return h

    consts_b = sbuf("consts_b", [128, 2 + HP], BF16)
    ONES = consts_b[:, 0:1]
    I64HI = consts_b[:, 2:2 + HP]
    masks = sbuf("masks", [128, 3 * 128], F32)
    convw = sbuf("convw", [128, 2 * NXBC * 4], F32)
    convb = sbuf("convb", [128, 2 * NXBC], F32)
    smallf = sbuf("smallf", [128, 16], F32)
    zeros128 = sbuf("zeros128", [128, 128], F32)
    dcol = sbuf("dcol", [128, 2 * NXS], F32)
    rmscol = sbuf("rmscol", [128, 2 * NXS], F32)
    pbcol = sbuf("pbcol_s", [128, 8], F32)
    epst = sbuf("epst", [128, 16], F32)
    hstate = sbuf("hstate", [128, NH * HP], BF16)
    acc = sbuf("acc", [128, 8 * L], BF16)

    ARA = sbuf("arA", [128, 22784], BF16)
    ARB = sbuf("arB", [128, 18432], BF16)
    ARC = sbuf("arC", [128, 18432], BF16)
    ARD = sbuf("arD", [128, 16384], BF16)
    ARE = sbuf("arE", [128, 10240], BF16)

    wab = ARA[:, 0:18432]
    xsup = ARA[:, 18432:22528]
    xbc = ARB[:, 0:9324]
    xsbc = ARB[:, 9344:13952]
    bctm = ARB[:, 13952:14208]
    cbm = ARB[:, 16384:16640]
    bbar = ARB[:, 14336:15360]
    aqi = ARB[:, 15360:16384]
    utm = ARC[:, 0:4096]
    ysb = ARC[:, 4096:8192]
    gam = ARC[:, 8192:10240]
    smeg = ARC[:, 10240:12288]
    almeg = ARC[:, 12288:14336]
    cameg = ARC[:, 14336:16384]
    adf = ARD[:].bitcast(F32)
    convt = adf[:, 0:2304]
    delt = adf[:, 2304:4352]
    alin = adf[:, 4352:6400]
    dtraw = adf[:, 6400:6912]
    dtsp = adf[:, 6912:7424]
    sptmp = adf[:, 7424:7936]
    lgfm = adf[:, 7936:8064]
    dtlg_tm = adf[:, 8064:8192]
    opstage = ARA[:, 0:16384]
    phring = ARA[:, 16384:18432]
    wzb = ARB[:, 0:16384]
    rstdr = ARB[:, 16384:17408].bitcast(F32)
    evtmp = ARB[:, 17408:17920]
    weff = ARC[:, 0:16384]
    ysup = ARD[:, 0:8192]
    zsil = ARD[:, 8192:16384]
    sqb = ysup
    ygat = ARE[:, 0:8192]
    etm = ARE[:, 0:2048]
    xtmf = ARE[:, 2048:4096].bitcast(F32)
    ep1 = ARE[:, 4096:6144].bitcast(F32)
    lnrep = ARE[:, 6144:10240].bitcast(F32)
    wq_g = [ARA[:, i * 2816:(i + 1) * 2816] for i in range(8)]
    wq_u = [ARB[:, i * 2816:(i + 1) * 2816] for i in range(6)] + \
           [ARC[:, i * 2816:(i + 1) * 2816] for i in range(2)]
    wq_d = [ARC[:, 5632 + i * 1024: 5632 + (i + 1) * 1024] for i in range(12)] + \
           [acc[:, i * 1024:(i + 1) * 1024] for i in range(10)]
    ftA = ARD[:, 0:2048].bitcast(F32)
    ftB = ARD[:, 2048:4096].bitcast(F32)
    ftC = ARD[:, 4096:6144].bitcast(F32)
    fsgn = ARD[:, 6144:8192].bitcast(F32)
    fstage = ARD[:, 8192:10240].bitcast(F32)
    fsc = ARD[:, 10240:10624].bitcast(F32)
    ephr = [ARD[:, 10624:11648].bitcast(F32), ARD[:, 13696:14720].bitcast(F32)]
    ep2fr = [ARD[:, 11648:13696].bitcast(F32)[:, 0:1024],
             ARD[:, 8192:10240].bitcast(F32)]
    qout = acc[:, 10240:11264]
    hqtm = acc[:, 11264:14080]
    xqtm = acc[:, 14080:15104]
    xqfm = acc[:, 15104:16128]
    hqfm = ARE[:, 0:5632]
    dntm = ARE[:, 5632:7680]
    ep1f = ARE[:, 7680:9728].bitcast(F32)
    dtmp = ARE[:, 9728:10240]

    pmm = [psum_t("ps_mm0"), psum_t("ps_mm1")]
    pcb = psum_t("ps_cb")
    py = [psum_t("ps_y0"), psum_t("ps_y1")]
    ph_ = [psum_t("ps_h0"), psum_t("ps_h1")]
    pst = psum_t("ps_st")
    PB8 = [pmm[0], pmm[1], pcb, py[0], py[1], ph_[0], ph_[1], pst]
    PB8N = ["pmm0", "pmm1", "pcb", "py0", "py1", "ph0", "ph1", "pst"]

    p = Prog()
    V, S, G, T = "vector", "scalar", "gpsimd", "tensor"
    dbg_outs = []

    def dump(name, src_ap, reads, dt, shape):
        if name not in KDBG:
            return
        o = nc.dram_tensor("dbg_" + name, list(shape), dt, kind="ExternalOutput").ap()
        dbg_outs.append("dbg_" + name)

        def em(e, o=o, s=src_ap):
            return e.dma_start(out=o, in_=s)
        p.step("sync", em, reads=reads, writes=[f"dbg{name}"], dma=True, tag="dump " + name)

    def dma(eng, outap, inap, reads, writes, tag="", tp=False):
        if tp:
            eng = "sync"
        elif eng == "synck":
            eng = "sync"
        elif eng == "sync":
            eng = "scalar"

        def em(e, o=outap, i=inap, tp=tp):
            if tp:
                return e.dma_start(out=o, in_=i, transpose=True)
            return e.dma_start(out=o, in_=i)
        p.step(eng, em, reads=reads, writes=writes, dma=True, tag=tag)

    def st(eng, emit, reads=(), writes=(), sreads=(), tag=""):
        p.step(eng, emit, reads=reads, writes=writes, sreads=sreads, tag=tag)

    def bcast_f(ap2d, inner):
        return ap2d.unsqueeze(2).broadcast_to([ap2d.shape[0], ap2d.shape[1], inner])

    dma("sync", consts_b[:, 0:1], dr["ones_bf"], [], ["ones"])
    dma("sync", consts_b[:, 2:2 + HP], dr["i64hi"], [], ["i64hi"])
    dma("sync", masks[:, 0:128], dr["maskF"], [], ["maskF"])
    dma("sync", masks[:, 128:256], dr["maskB"], [], ["maskB"])
    dma("sync", masks[:, 256:384], dr["idf32"], [], ["idf32"])
    for i, d in enumerate(("f", "b")):
        dma("sync", convw[:, i * NXBC * 4:(i + 1) * NXBC * 4], dr[f"convw_{d}"], [], [f"convw{d}"])
        dma("sync", convb[:, i * NXBC:(i + 1) * NXBC], dr[f"convb_{d}"], [], [f"convb{d}"])
        dma("sync", smallf[64:96, i:i + 1], dr[f"dtb_{d}"], ["smallf"], ["smallf"])
        dma("sync", smallf[64:96, 4 + i:5 + i], dr[f"alog_{d}"], ["smallf"], ["smallf"])
        dma("sync", dcol[:, i * NXS:(i + 1) * NXS], dr[f"dcol_{d}"], [], [f"dcol{d}"])
        dma("sync", rmscol[:, i * NXS:(i + 1) * NXS], dr[f"rmscol_{d}"], [], [f"rmscol{d}"])
    dma("sync", pbcol[:], dr["pbcol"], [], ["pbcol"])
    st(V, lambda e: e.memset(zeros128[:], 0.0), [], ["zeros128"])
    for i, d in enumerate(("f", "b")):
        st(S, lambda e, i=i: e.activation(smallf[64:96, 6 + i:7 + i],
                                          smallf[64:96, 4 + i:5 + i], AF.Exp),
           ["smallf"], ["smallf"], "Aexp")
        st(V, lambda e, i=i: e.tensor_scalar(smallf[64:96, 2 + i:3 + i],
                                             smallf[64:96, 6 + i:7 + i],
                                             -1.0, None, op0=AL.mult),
           ["smallf"], ["smallf"], "Aneg")

    p0f = [ARE[:, 0:4096].bitcast(F32), ARE[:, 4096:8192].bitcast(F32)]
    p0b = [ARD[:, 0:2048], ARD[:, 2048:4096]]
    for fc in range(8):
        r = fc % 2
        dma("gpsimd", p0f[r][:], dr["xT"][fc * 128:(fc + 1) * 128, :],
            [f"p0f{r}"], [f"p0f{r}"])
        st(V, lambda e, r=r: e.tensor_copy(p0b[r][:], p0f[r][:]),
           [f"p0f{r}", f"p0b{r}"], [f"p0b{r}"], "x cast")
        dma("sync", xfm_scr[fc, :, :], p0b[r][:], [f"p0b{r}"], [f"xfms{fc}", f"p0b{r}"])
        st(V, lambda e, fc=fc, r=r: e.tensor_scalar(
            acc[:, fc * L:(fc + 1) * L], p0f[r][:], pbcol[:, fc:fc + 1], None, op0=AL.add),
           [f"p0f{r}", "pbcol"], ["acc"], "accinit")
    XFM = [f"xfms{fc}" for fc in range(8)]

    def pass_a(d):
        di = 0 if d == "f" else 1
        fwd = d == "f"
        for k in range(8):
            dma("gpsimd", wab[:, k * WA_COLS:(k + 1) * WA_COLS],
                dr[f"wa_{d}"][k * 128:(k + 1) * 128, :], ["wab"], ["wab", f"wa_{k}"])
        st(V, lambda e: e.memset(xbc[:], 0.0), ["xbc"], ["xbc"], "zero xbc")
        st(V, lambda e: e.memset(hstate[64:128, :], 0.0), ["hstate"], ["hstate"], "zero h")

        sup_order = list(range(NSUP)) if fwd else list(range(NSUP - 1, -1, -1))
        for si, sp in enumerate(sup_order):
            if si > 0:
                xv = xbc.rearrange("p (c w) -> p c w", c=NXBC)
                if fwd:
                    st(V, lambda e, xv=xv: e.tensor_copy(xv[:, :, 0:3], xv[:, :, SUP:SUP + 3]),
                       ["xbc"], ["xbc"], "halo")
                else:
                    st(V, lambda e, xv=xv: e.tensor_copy(xv[:, :, SUP + 3:SUP + 6], xv[:, :, 3:6]),
                       ["xbc"], ["xbc"], "halo")
            dma("sync", xsup[:].rearrange("p (c t) -> p c t", c=8),
                xfm_scr[:, :, sp * SUP:(sp + 1) * SUP].transpose([1, 0, 2]),
                XFM + ["xsup"], ["xsup"])
            for oc in range(NXBC):
                pb = pmm[oc % 2]
                pres = f"pmm{oc % 2}"

                def em_mm(e, oc=oc, pb=pb):
                    last = None
                    for k in range(8):
                        last = e.matmul(pb[:, 0:SUP],
                                        wab[:, k * WA_COLS + oc * 128: k * WA_COLS + (oc + 1) * 128],
                                        xsup[:, k * SUP:(k + 1) * SUP],
                                        start=(k == 0), stop=(k == 7))
                    return last
                st(T, em_mm, [f"wa_{k}" for k in range(8)] + ["xsup", pres], [pres], "A mm")
                if oc < NXBC - 1:
                    st(S, lambda e, oc=oc, pb=pb: e.activation(
                        xbc[:, oc * (SUP + 6) + 3: oc * (SUP + 6) + 3 + SUP],
                        pb[:, 0:SUP], AF.Copy),
                       [pres, "xbc"], ["xbc"], "ev xbc")
                else:
                    st(S, lambda e, oc=oc, pb=pb: e.activation(
                        xbc[0:64, oc * (SUP + 6) + 3: oc * (SUP + 6) + 3 + SUP],
                        pb[0:64, 0:SUP], AF.Copy),
                       [pres, "xbc"], ["xbc"], "ev cdup")
                    st(S, lambda e, pb=pb: e.activation(dtraw[64:96, :],
                                                        pb[64:96, 0:SUP], AF.Copy),
                       [pres, "dtraw"], ["dtraw"], "ev dt")
            st(S, lambda e, i=di: e.activation(sptmp[64:96, :], dtraw[64:96, :], AF.Abs,
                                               bias=smallf[64:96, i:i + 1]),
               ["dtraw", "smallf", "sptmp"], ["sptmp"], "sp abs")
            st(S, lambda e: e.activation(sptmp[64:96, :], sptmp[64:96, :], AF.Exp, scale=-1.0),
               ["sptmp"], ["sptmp"], "sp exp")
            st(S, lambda e: e.activation(sptmp[64:96, :], sptmp[64:96, :], AF.Ln, bias=1.0),
               ["sptmp"], ["sptmp"], "sp ln")
            st(V, lambda e, i=di: e.tensor_scalar(dtsp[64:96, :], dtraw[64:96, :],
                                                  smallf[64:96, i:i + 1], 0.0,
                                                  op0=AL.add, op1=AL.max),
               ["dtraw", "smallf", "dtsp"], ["dtsp"], "sp relu")
            st(V, lambda e: e.tensor_tensor(dtsp[64:96, :], dtsp[64:96, :], sptmp[64:96, :],
                                            op=AL.add),
               ["dtsp", "sptmp"], ["dtsp"], "sp add")

            chunk_order = list(range(NCQ)) if fwd else list(range(NCQ - 1, -1, -1))

            def stage_A(ci):
                c_glob = sp * NCQ + ci
                toff = ci * Q
                ring = c_glob % 2
                XS_R = f"xsbc{ring}"
                xsb_o = ring * NXBC * 128
                dtlg = dtlg_tm[:, (ring) * 64:(ring + 1) * 64]
                DTLG = f"dtlg{ring}"
                cbmr = cbm[:, ring * 128:(ring + 1) * 128]
                CBM = f"cbm{ring}"

                cwv = convw[:, di * NXBC * 4:(di + 1) * NXBC * 4].rearrange(
                    "p (c k) -> p c k", k=4)

                xbcv = xbc[:].rearrange("p (c w) -> p c w", c=NXBC)

                def xview(k, ci=ci, fwd=fwd, xbcv=xbcv):
                    base = ci * Q + 3 + ((k - 3) if fwd else k)
                    return xbcv[:, :, base: base + Q]

                def wview(k, fwd=fwd):
                    kk = k if fwd else 3 - k
                    return cwv[:, :, kk:kk + 1].broadcast_to([128, NXBC, Q])

                cbv = convb[:, di * NXBC:(di + 1) * NXBC].unsqueeze(2) \
                    .broadcast_to([128, NXBC, Q])
                mtmp = xsbc[:, xsb_o:xsb_o + NXBC * 128].rearrange(
                    "p (c w) -> p c w", w=128)
                cvv = convt[:].rearrange("p (c w) -> p c w", w=128)
                st(V, lambda e: e.tensor_tensor(cvv, xview(0), wview(0), op=AL.mult),
                   ["xbc", f"convw{d}", "convt"], ["convt"], "conv0")
                for k in (1, 2, 3):
                    st(V, lambda e, k=k: e.tensor_tensor(mtmp, xview(k), wview(k), op=AL.mult),
                       ["xbc", f"convw{d}", XS_R], [XS_R], f"convm{k}")
                    st(V, lambda e: e.tensor_tensor(cvv, cvv, mtmp, op=AL.add),
                       [XS_R, "convt"], ["convt"], f"conva{k}")
                st(V, lambda e: e.tensor_tensor(cvv, cvv, cbv, op=AL.add),
                   [f"convb{d}", "convt"], ["convt"], "convb")
                st(S, lambda e, o=xsb_o: e.activation(xsbc[:, o:o + NXBC * 128], convt[:], AF.Silu),
                   ["convt", XS_R], [XS_R], "silu")
                if fwd:
                    st(V, lambda e, toff=toff, i=di: e.tensor_scalar(
                        sptmp[64:96, toff:toff + Q], dtsp[64:96, toff:toff + Q],
                        smallf[64:96, 2 + i:3 + i], None, op0=AL.mult),
                       ["dtsp", "smallf", "sptmp"], ["sptmp"], "dtA")
                else:
                    st(V, lambda e, toff=toff, i=di: e.tensor_scalar(
                        sptmp[64:96, toff:toff + Q], dtsp[64:96, toff:toff + Q],
                        smallf[64:96, 2 + i:3 + i], 0.0, op0=AL.mult, op1=AL.add,
                        accum_out=smallf[64:96, 8:9]),
                       ["dtsp", "smallf", "sptmp"], ["sptmp", "smallf"], "dtA")
                st(V, lambda e, toff=toff: e.tensor_tensor_scan(
                    lgfm[64:96, :], sptmp[64:96, toff:toff + Q], zeros128[64:96, :],
                    0.0, op0=AL.add, op1=AL.add),
                   ["sptmp", "zeros128", "lgfm"], ["lgfm"], "scan")
                if not fwd:
                    st(V, lambda e, toff=toff: e.scalar_tensor_tensor(
                        lgfm[64:96, :], lgfm[64:96, :], -1.0, sptmp[64:96, toff:toff + Q],
                        op0=AL.mult, op1=AL.add),
                       ["lgfm", "sptmp"], ["lgfm"], "suf1")
                    st(V, lambda e: e.tensor_scalar(lgfm[64:96, :], lgfm[64:96, :],
                                                    smallf[64:96, 8:9], None, op0=AL.add),
                       ["lgfm"], ["lgfm"], sreads=["smallf"], tag="suf2")
                dma("sync", lg_scr[di, c_glob, :, :], lgfm[64:96, :],
                    ["lgfm"], [f"lgscr{di}_{c_glob}"])
                id32 = masks[64:96, 256 + 64:256 + 96]
                st(T, lambda e, toff=toff, id32=id32: e.transpose(
                    pcb[:, 128:160], dtsp[64:96, toff:toff + Q], id32),
                   ["dtsp", "idf32", "pcb"], ["pcb"], "tp dt")
                st(T, lambda e, id32=id32: e.transpose(pcb[:, 160:192], lgfm[64:96, :], id32),
                   ["lgfm", "idf32", "pcb"], ["pcb"], "tp lg")
                st(S, lambda e, dtlg=dtlg: e.activation(dtlg, pcb[:, 128:192], AF.Copy),
                   ["pcb", DTLG], [DTLG], "ev dtlg")
                for fc in range(NXS):
                    dma("scalar", utm[:, ring * 2048 + fc * 128: ring * 2048 + (fc + 1) * 128],
                        xsbc[:, xsb_o + fc * 128: xsb_o + (fc + 1) * 128],
                        [XS_R, f"utm{ring}_{fc}"], [f"utm{ring}_{fc}"], tp=True)
                dma("scalar", bctm[:, ring * 128:(ring + 1) * 128],
                    xsbc[:, xsb_o + 16 * 128: xsb_o + 17 * 128],
                    [XS_R, f"bctm{ring}"], [f"bctm{ring}"], tp=True)

            def stage_A_post(ci):
                c_glob = sp * NCQ + ci
                ring = c_glob % 2
                XS_R = f"xsbc{ring}"
                xsb_o = ring * NXBC * 128
                dtlg = dtlg_tm[:, (ring) * 64:(ring + 1) * 64]
                DTLG = f"dtlg{ring}"
                cbmr = cbm[:, ring * 128:(ring + 1) * 128]
                CBM = f"cbm{ring}"
                UTM = [f"utm{ring}_{fc}" for fc in range(NXS)]
                ur = utm[:, ring * 2048:(ring + 1) * 2048]
                st(V, lambda e, ur=ur, dtlg=dtlg: e.tensor_tensor(
                    ur.rearrange("p (h w) -> p h w", h=NH),
                    ur.rearrange("p (h w) -> p h w", h=NH),
                    bcast_f(dtlg[:, 0:32], HP), op=AL.mult),
                   UTM + [DTLG], UTM, "U dt")
                st(T, lambda e, o=xsb_o: e.matmul(
                    pcb[:, 0:128],
                    xsbc[0:64, o + 16 * 128: o + 16 * 128 + 128],
                    xsbc[0:64, o + 17 * 128: o + 17 * 128 + 128],
                    start=True, stop=True),
                   [XS_R, "pcb"], ["pcb"], "CB")
                st(V, lambda e, cbmr=cbmr, m=(masks[:, 0:128] if fwd else masks[:, 128:256]):
                   e.tensor_tensor(cbmr, pcb[:, 0:128], m, op=AL.mult),
                   ["pcb", "maskF" if fwd else "maskB", CBM], [CBM], "CBm")

            def stage_B(ci, hh):
                c_glob = sp * NCQ + ci
                ring = c_glob % 2
                XS_R = f"xsbc{ring}"
                xsb_o = ring * NXBC * 128
                dtlg = dtlg_tm[:, ring * 64:(ring + 1) * 64]
                DTLG = f"dtlg{ring}"
                cbmr = cbm[:, ring * 128:(ring + 1) * 128]
                CBM = f"cbm{ring}"
                wt = (Q - 1) if fwd else 0
                dma("sync", alin[:],
                    lg_scr[di, c_glob, hh * HH:(hh + 1) * HH, :]
                    .rearrange("h t -> (h t)").unsqueeze(0).partition_broadcast(128),
                    [f"lgscr{di}_{c_glob}", "alin"], ["alin"])
                st(V, lambda e, hh=hh, dtlg=dtlg: e.tensor_tensor(
                    delt[:].rearrange("p (h t) -> p h t", h=HH),
                    alin[:].rearrange("p (h t) -> p h t", h=HH),
                    bcast_f(dtlg[:, 32 + hh * HH:32 + (hh + 1) * HH], Q),
                    op=AL.subtract),
                   ["alin", DTLG, "delt"], ["delt"], "dsub")
                st(S, lambda e: e.activation(gam[:], delt[:], AF.Exp),
                   ["delt", "gam"], ["gam"], "gamma")
                st(S, lambda e: e.activation(almeg[64:128, :], alin[64:128, :], AF.Exp),
                   ["alin", "almeg"], ["almeg"], "alpha")
                st(V, lambda e, cbmr=cbmr: e.scalar_tensor_tensor(
                    smeg[:].rearrange("p (h t) -> p h t", h=HH),
                    gam[:].rearrange("p (h t) -> p h t", h=HH),
                    1.0,
                    cbmr.unsqueeze(1).broadcast_to([128, HH, Q]),
                    op0=AL.min, op1=AL.mult),
                   ["gam", CBM, "smeg"], ["smeg"], "S")
                st(V, lambda e, wt=wt: e.tensor_scalar(
                    epst[:, 0:HH],
                    gam[:].rearrange("p (h t) -> p h t", h=HH)[:, :, wt],
                    1.0, None, op0=AL.min),
                   ["gam", "epst"], ["epst"], "gcl")
                st(V, lambda e, o=xsb_o: e.tensor_tensor(
                    cameg[64:128, :].rearrange("p (h t) -> p h t", h=HH),
                    xsbc[64:128, o + 16 * 128: o + 17 * 128]
                    .unsqueeze(1).broadcast_to([64, HH, Q]),
                    almeg[64:128, :].rearrange("p (h t) -> p h t", h=HH), op=AL.mult),
                   [XS_R, "almeg", "cameg"], ["cameg"], "Ca")
                st(V, lambda e, ring=ring: e.tensor_tensor(
                    bbar[:].rearrange("p (h w) -> p h w", h=HH),
                    bctm[:, ring * 128: ring * 128 + 64].unsqueeze(1)
                    .broadcast_to([128, HH, HP]),
                    bcast_f(epst[:, 0:HH], HP), op=AL.mult),
                   ["epst", f"bctm{ring}", "bbar"], ["bbar"], "Bbar")
                st(V, lambda e, wt=wt: e.tensor_tensor(
                    aqi[64:128, :].rearrange("p (h w) -> p h w", h=HH),
                    I64HI[64:128, :].unsqueeze(1).broadcast_to([64, HH, HP]),
                    almeg[64:128, :].rearrange("p (h t) -> p h t", h=HH)[:, :, wt:wt + 1]
                    .broadcast_to([64, HH, HP]), op=AL.mult),
                   ["almeg", "i64hi", "aqi"], ["aqi"], "aQI")

            def stage_C_mm(ci, hh):
                c_glob = sp * NCQ + ci
                ring = c_glob % 2
                XS_R = f"xsbc{ring}"
                xsb_o = ring * NXBC * 128

                def em_pairs(e, hh=hh, ring=ring):
                    last = None
                    started = {}
                    for pl in range(8):
                        pc = hh * 8 + pl
                        bidx = pl // 4
                        yb = py[bidx]
                        ycol = (pl % 4) * 128
                        for par in range(2):
                            hloc = 2 * pl + par
                            last = e.matmul(
                                yb[par * 64:(par + 1) * 64, ycol:ycol + 128],
                                utm[:, ring * 2048 + pc * 128 + par * 64:
                                    ring * 2048 + pc * 128 + par * 64 + 64],
                                smeg[:, hloc * Q:(hloc + 1) * Q],
                                start=not started.get((bidx, par), False), stop=False,
                                tile_position=(0, par * 64))
                            started[(bidx, par)] = True
                    for pl in range(8):
                        pc = hh * 8 + pl
                        yb = py[pl // 4]
                        ycol = (pl % 4) * 128
                        for par in range(2):
                            hloc = 2 * pl + par
                            h_glob = 2 * pc + par
                            last = e.matmul(
                                yb[par * 64:(par + 1) * 64, ycol:ycol + 128],
                                hstate[64:128, h_glob * HP:(h_glob + 1) * HP],
                                cameg[64:128, hloc * Q:(hloc + 1) * Q],
                                start=False,
                                stop=(pl % 4 == 3 and par == 1),
                                tile_position=(64, par * 64))
                    return last
                st(T, em_pairs, [f"utm{ring}_{fc}" for fc in range(NXS)]
                   + ["smeg", "hstate", "cameg", "py0", "py1"],
                   ["py0", "py1"], "Y mms")

                def em_upd(e, hh=hh, ring=ring):
                    last = None
                    started = [False, False]
                    for hl in range(HH):
                        h_glob = hh * HH + hl
                        bidx = hl // 8
                        hb = ph_[bidx]
                        hcol = (hl % 8) * 64
                        last = e.matmul(hb[64:128, hcol:hcol + 64],
                                        bbar[:, hl * HP:(hl + 1) * HP],
                                        utm[:, ring * 2048 + h_glob * HP:
                                            ring * 2048 + (h_glob + 1) * HP],
                                        start=not started[bidx], stop=False,
                                        tile_position=(0, 64))
                        started[bidx] = True
                    for hl in range(HH):
                        h_glob = hh * HH + hl
                        hb = ph_[hl // 8]
                        hcol = (hl % 8) * 64
                        last = e.matmul(hb[64:128, hcol:hcol + 64],
                                        aqi[64:128, hl * HP:(hl + 1) * HP],
                                        hstate[64:128, h_glob * HP:(h_glob + 1) * HP],
                                        start=False, stop=(hl % 8 == 7),
                                        tile_position=(64, 64))
                    return last
                st(T, em_upd, [f"utm{ring}_{fc}" for fc in range(NXS)]
                   + ["bbar", "aqi", "hstate", "ph0", "ph1"],
                   ["ph0", "ph1"], "h mms")

            def stage_C_ev(ci, hh):
                c_glob = sp * NCQ + ci
                ring = c_glob % 2
                XS_R = f"xsbc{ring}"
                xsb_o = ring * NXBC * 128
                for pl in range(8):
                    pc = hh * 8 + pl
                    yb = py[pl // 4]
                    ycol = (pl % 4) * 128
                    st(V, lambda e, pc=pc, yb=yb, ycol=ycol, ring=ring, i=di, o=xsb_o:
                       e.scalar_tensor_tensor(
                           ysb[:, ring * 2048 + pc * 128: ring * 2048 + (pc + 1) * 128],
                           xsbc[:, o + pc * 128: o + (pc + 1) * 128],
                           dcol[:, i * NXS + pc: i * NXS + pc + 1],
                           yb[:, ycol:ycol + 128], op0=AL.mult, op1=AL.add),
                       [f"py{pl // 4}", XS_R, f"dcol{d}", f"ysb{ring}"],
                       [f"ysb{ring}"], "Y ev")
                for bb in range(2):
                    st(S, lambda e, bb=bb, hh=hh: e.activation(
                        hstate[64:128, hh * HH * HP + bb * 512:
                               hh * HH * HP + (bb + 1) * 512],
                        ph_[bb][64:128, 0:512], AF.Copy),
                       [f"ph{bb}", "hstate"], ["hstate"], "h ev")

            def stage_store(ci):
                c_glob = sp * NCQ + ci
                ring = c_glob % 2
                dma("sync", yscr[di, :, :, c_glob * Q:(c_glob + 1) * Q].transpose([1, 0, 2]),
                    ysb[:, ring * 2048:(ring + 1) * 2048].rearrange("p (c t) -> p c t", c=NXS),
                    [f"ysb{ring}"], [f"yscr{di}_{c_glob}"])

            stage_A(chunk_order[0])
            stage_A_post(chunk_order[0])
            stage_B(chunk_order[0], 0)
            for idx, ci in enumerate(chunk_order):
                stage_C_mm(ci, 0)
                if idx + 1 < len(chunk_order):
                    stage_A(chunk_order[idx + 1])
                stage_B(ci, 1)
                stage_C_ev(ci, 0)
                if idx + 1 < len(chunk_order):
                    stage_A_post(chunk_order[idx + 1])
                stage_C_mm(ci, 1)
                if idx + 1 < len(chunk_order):
                    stage_B(chunk_order[idx + 1], 0)
                stage_C_ev(ci, 1)
                stage_store(ci)

    def pass_b(d):
        di = 0 if d == "f" else 1
        for k in range(8):
            dma("gpsimd", opstage[:, k * 2048:(k + 1) * 2048],
                dr[f"op_{d}"][k * 128:(k + 1) * 128, :], ["opstage"], ["opstage", f"op_{k}"])
        phr = [phring[:, 0:1024], phring[:, 1024:2048]]
        for grp in range(4):
            for k in range(8):
                sl = k % 2
                dma("gpsimd", phr[sl], dr[f"ph_{d}"][k * 128:(k + 1) * 128, :],
                    [f"phr{sl}"], [f"phr{sl}"])
                for fci in range(4):
                    fc = grp * 4 + fci
                    for half in range(2):
                        bi = fci * 2 + half
                        pb = PB8[bi]
                        pres = PB8N[bi]

                        def em_we(e, fc=fc, half=half, pb=pb, k=k, sl=sl):
                            return e.matmul(
                                pb[:, 0:512],
                                opstage[:, k * 2048 + fc * 128: k * 2048 + (fc + 1) * 128],
                                phr[sl][:, half * 512:(half + 1) * 512],
                                start=(k == 0), stop=(k == 7))
                        st(T, em_we, [f"op_{k}", f"phr{sl}"] + ([pres] if k == 0 else []),
                           [pres], "Weff mm")
            for fci in range(4):
                fc = grp * 4 + fci
                for half in range(2):
                    bi = fci * 2 + half
                    st(V, lambda e, fc=fc, half=half, pb=PB8[bi], i=di: e.tensor_scalar(
                        weff[:, fc * D + half * 512: fc * D + (half + 1) * 512],
                        pb[:, 0:512], rmscol[:, i * NXS + fc: i * NXS + fc + 1],
                        None, op0=AL.mult),
                       [PB8N[bi], f"rmscol{d}", "weff"], ["weff"], "Weff ev")
        for k in range(8):
            dma("gpsimd", wzb[:, k * 2048:(k + 1) * 2048],
                dr[f"wz_{d}"][k * 128:(k + 1) * 128, :],
                ["wzb", "xbc", "xsbc0", "xsbc1", "bctm0", "bctm1", "cbm", "bbar", "aqi"],
                ["wzb", f"wz_{k}"])
        for sp in range(NSUP):
            dma("sync", xsup[:].rearrange("p (c t) -> p c t", c=8),
                xfm_scr[:, :, sp * SUP:(sp + 1) * SUP].transpose([1, 0, 2]),
                XFM + ["xsup"], ["xsup"])
            dma("sync", ysup[:].rearrange("p (c t) -> p c t", c=NXS),
                yscr[di, :, :, sp * SUP:(sp + 1) * SUP].transpose([1, 0, 2]),
                [f"yscr{di}_{sp * NCQ + i}" for i in range(NCQ)] +
                ["ysup", "convt", "delt", "alin", "dtraw", "dtsp", "sptmp", "lgfm", "dtlg"],
                ["ysup"])
            for fc in range(NXS):
                pb = pmm[fc % 2]
                pres = f"pmm{fc % 2}"

                def em_z(e, fc=fc, pb=pb):
                    last = None
                    for k in range(8):
                        last = e.matmul(pb[:, 0:SUP],
                                        wzb[:, k * 2048 + fc * 128: k * 2048 + (fc + 1) * 128],
                                        xsup[:, k * SUP:(k + 1) * SUP],
                                        start=(k == 0), stop=(k == 7))
                    return last
                st(T, em_z, [f"wz_{k}" for k in range(8)] + ["xsup", pres], [pres], "z mm")
                st(S, lambda e, fc=fc, pb=pb: e.activation(
                    zsil[:, fc * SUP:(fc + 1) * SUP], pb[:, 0:SUP], AF.Silu),
                   [pres, "zsil"], ["zsil"], "z silu")
            st(V, lambda e: e.tensor_tensor(ygat[:], ysup[:], zsil[:], op=AL.mult),
               ["ysup", "zsil", "ygat", "phr0", "phr1"], ["ygat"], "gate")
            st(S, lambda e: e.activation(sqb[:], ygat[:], AF.Square),
               ["ygat", "ysup"], ["ysup"], "sq")

            def em_rms(e):
                last = None
                for fc in range(NXS):
                    last = e.matmul(pst[0:1, 0:SUP], ONES,
                                    sqb[:, fc * SUP:(fc + 1) * SUP],
                                    start=(fc == 0), stop=(fc == NXS - 1))
                return last
            st(T, em_rms, ["ysup", "ones", "pst"], ["pst"], "rms sum")
            st(V, lambda e: e.tensor_scalar(rstdr[0:1, 0:SUP], pst[0:1, 0:SUP],
                                            1.0 / 2048.0, EPS, op0=AL.mult, op1=AL.add),
               ["pst", "rstdr"], ["rstdr"], "rms pre")
            dma("sync", rs_scr[di, sp, :, :], rstdr[0:1, 0:SUP], ["rstdr"], [f"rs{di}{sp}"])
            dma("sync", rstdr[:], rs_scr[di, sp, :, :].partition_broadcast(128),
                [f"rs{di}{sp}", "rstdr"], ["rstdr"])
            st(V, lambda e: e.reciprocal(rstdr[:], rstdr[:]), ["rstdr"], ["rstdr"], "recip")
            st(S, lambda e: e.activation(rstdr[:], rstdr[:], AF.Sqrt), ["rstdr"], ["rstdr"], "sqrt")
            for dm in range(8):
                pb = pmm[dm % 2]
                pres = f"pmm{dm % 2}"

                def em_op(e, dm=dm, pb=pb):
                    last = None
                    for fc in range(NXS):
                        last = e.matmul(pb[:, 0:SUP],
                                        weff[:, fc * D + dm * 128: fc * D + (dm + 1) * 128],
                                        ygat[:, fc * SUP:(fc + 1) * SUP],
                                        start=(fc == 0), stop=(fc == NXS - 1))
                    return last
                st(T, em_op, ["weff", "ygat", pres], [pres], "oproj mm")
                st(V, lambda e, pb=pb: e.tensor_tensor(
                    evtmp[:], pb[:, 0:SUP], rstdr[:], op=AL.mult),
                   [pres, "rstdr", "evtmp"], ["evtmp"], "oproj ev")
                st(V, lambda e, dm=dm, sp=sp: e.tensor_tensor(
                    acc[:, dm * L + sp * SUP: dm * L + (sp + 1) * SUP],
                    acc[:, dm * L + sp * SUP: dm * L + (sp + 1) * SUP],
                    evtmp[:], op=AL.add),
                   ["evtmp", "acc"], ["acc"], "acc add")

    def epilogue(units=()):
        uit = iter(units)
        dma("sync", lnrep[:, 0:D], dr["lnw_a"].partition_broadcast(128),
            ["lnrep"], ["lnrep"])
        dma("sync", lnrep[:, D:2 * D], dr["lnb_a"].partition_broadcast(128),
            ["lnrep"], ["lnrep"])
        for tc in range(16):
            for dc in range(8):
                dma("scalar", etm[:, dc * 128:(dc + 1) * 128],
                    acc[:, dc * L + tc * 128: dc * L + (tc + 1) * 128],
                    ["acc", "etm"], ["etm"], tp=True)
            dma("sync", xtmf[:], x_in[tc * 128:(tc + 1) * 128, :], ["xtmf"], ["xtmf"])
            st(V, lambda e: e.tensor_reduce(epst[:, 0:1], etm[:, 0:D], axis=AX.X, op=AL.add),
               ["etm", "epst"], ["epst"])
            st(S, lambda e: e.activation(ep1[:], etm[:, 0:D], AF.Square),
               ["etm", "ep1"], ["ep1"])
            st(V, lambda e: e.tensor_reduce(epst[:, 1:2], ep1[:], axis=AX.X, op=AL.add),
               ["ep1", "epst"], ["epst"])
            st(V, lambda e: e.tensor_scalar(epst[:, 2:3], epst[:, 0:1], 1.0 / D, None,
                                            op0=AL.mult), ["epst"], ["epst"])
            st(V, lambda e: e.tensor_tensor(epst[:, 3:4], epst[:, 2:3], epst[:, 2:3],
                                            op=AL.mult), ["epst"], ["epst"])
            st(V, lambda e: e.tensor_scalar(epst[:, 4:5], epst[:, 1:2], 1.0 / D, EPS,
                                            op0=AL.mult, op1=AL.add), ["epst"], ["epst"])
            st(V, lambda e: e.tensor_tensor(epst[:, 4:5], epst[:, 4:5], epst[:, 3:4],
                                            op=AL.subtract), ["epst"], ["epst"])
            st(V, lambda e: e.reciprocal(epst[:, 5:6], epst[:, 4:5]), ["epst"], ["epst"])
            st(S, lambda e: e.activation(epst[:, 6:7], epst[:, 5:6], AF.Sqrt),
               ["epst"], ["epst"])
            st(V, lambda e: e.tensor_scalar(ep1[:], etm[:, 0:D], epst[:, 2:3], epst[:, 6:7],
                                            op0=AL.subtract, op1=AL.mult),
               ["etm", "ep1"], ["ep1"], sreads=["epst"], tag="ln1")
            st(V, lambda e: e.tensor_tensor(ep1[:], ep1[:], lnrep[:, 0:D], op=AL.mult),
               ["ep1", "lnrep"], ["ep1"])
            st(V, lambda e: e.tensor_tensor(ep1[:], ep1[:], lnrep[:, D:2 * D], op=AL.add),
               ["ep1", "lnrep"], ["ep1"])
            st(V, lambda e: e.tensor_tensor(ep1[:], ep1[:], xtmf[:], op=AL.add),
               ["ep1", "xtmf"], ["ep1"], "h1")
            st(V, lambda e: e.tensor_reduce(epst[:, 8:9], ep1[:], axis=AX.X, op=AL.add),
               ["ep1", "epst"], ["epst"])
            st(S, lambda e: e.activation(xtmf[:], ep1[:], AF.Square),
               ["ep1", "xtmf"], ["xtmf"])
            st(V, lambda e: e.tensor_reduce(epst[:, 9:10], xtmf[:], axis=AX.X, op=AL.add),
               ["xtmf", "epst"], ["epst"])
            st(V, lambda e: e.tensor_scalar(epst[:, 10:11], epst[:, 8:9], 1.0 / D, None,
                                            op0=AL.mult), ["epst"], ["epst"])
            st(V, lambda e: e.tensor_tensor(epst[:, 11:12], epst[:, 10:11], epst[:, 10:11],
                                            op=AL.mult), ["epst"], ["epst"])
            st(V, lambda e: e.tensor_scalar(epst[:, 12:13], epst[:, 9:10], 1.0 / D, EPS,
                                            op0=AL.mult, op1=AL.add), ["epst"], ["epst"])
            st(V, lambda e: e.tensor_tensor(epst[:, 12:13], epst[:, 12:13], epst[:, 11:12],
                                            op=AL.subtract), ["epst"], ["epst"])
            st(V, lambda e: e.reciprocal(epst[:, 13:14], epst[:, 12:13]), ["epst"], ["epst"])
            st(S, lambda e: e.activation(epst[:, 14:15], epst[:, 13:14], AF.Sqrt),
               ["epst"], ["epst"])
            st(V, lambda e: e.tensor_scalar(ep1[:], ep1[:], epst[:, 10:11], epst[:, 14:15],
                                            op0=AL.subtract, op1=AL.mult),
               ["ep1"], ["ep1"], sreads=["epst"], tag="x1")
            dma("sync", x1_scr[tc * 128:(tc + 1) * 128, :], ep1[:], ["ep1"], [f"x1s{tc}"])
            for _ in range(4):
                u = next(uit, None)
                if u is not None:
                    u()
        for u in uit:
            u()

    def quant(src_ap, width, res_in, dst_ap, dst_res, tag, ceng=V):
        nb = width // 16
        dump_fsc = tag == f"hq{DBGTC}_0" and "fsc0" in KDBG
        dump_q5 = tag == f"hq{DBGTC}_5" and "q5" in KDBG
        st(V, lambda e: e.tensor_scalar(fsgn[:, 0:width], src_ap, 0.0, 2.0,
                                        op0=AL.is_ge, op1=AL.mult),
           list(res_in) + ["fsgn"], ["fsgn"], tag + " sgn0")
        st(V, lambda e: e.scalar_tensor_tensor(ftA[:, 0:width], fsgn[:, 0:width],
                                               -1.0, src_ap, op0=AL.add, op1=AL.mult),
           list(res_in) + ["fsgn", "ftA"], ["ftA"], tag + " abs")

        av = ftA[:, 0:nb * 16].rearrange("p (b w) -> p b w", w=16)
        r8 = ftC[:, 0:nb * 8].rearrange("p (b w) -> p b w", w=8)
        r4 = ftC[:, nb * 8:nb * 12].rearrange("p (b w) -> p b w", w=4)
        r2 = ftC[:, nb * 12:nb * 14].rearrange("p (b w) -> p b w", w=2)
        st(V, lambda e, nb=nb, av=av: e.tensor_tensor(
            ftC[:, 0:nb * 8], av[:, :, 0:8], av[:, :, 8:16], op=AL.max),
           ["ftA", "ftC"], ["ftC"], tag + " bm0")
        st(V, lambda e, nb=nb, r8=r8: e.tensor_tensor(
            ftC[:, nb * 8:nb * 12], r8[:, :, 0:4], r8[:, :, 4:8], op=AL.max),
           ["ftC"], ["ftC"], tag + " bm1")
        st(V, lambda e, nb=nb, r4=r4: e.tensor_tensor(
            ftC[:, nb * 12:nb * 14], r4[:, :, 0:2], r4[:, :, 2:4], op=AL.max),
           ["ftC"], ["ftC"], tag + " bm2")
        st(V, lambda e, nb=nb, r2=r2: e.tensor_tensor(
            fsc[:, 0:nb], r2[:, :, 0], r2[:, :, 1], op=AL.max),
           ["ftC", "fsc"], ["fsc"], tag + " bm3")
        if dump_q5:
            dump("q5ftA", ftA[:], ["ftA"], F32, (128, 1024))
            dump("q5fscraw", fsc[:], ["fsc"], F32, (128, 192))
        st(V, lambda e: e.tensor_scalar(fsc[:, 64:64 + nb], fsc[:, 0:nb], 6e-12, None,
                                        op0=AL.max),
           ["fsc"], ["fsc"], tag + " clamp")
        st(V, lambda e: e.reciprocal(fsc[:, 128:128 + nb], fsc[:, 64:64 + nb]),
           ["fsc"], ["fsc"], tag + " recip")
        if dump_fsc:
            dump("fsc0", fsc[:], ["fsc"], F32, (128, 192))
        st(V, lambda e: e.tensor_tensor(
            ftB[:, 0:width].rearrange("p (b w) -> p b w", w=16),
            ftA[:, 0:width].rearrange("p (b w) -> p b w", w=16),
            bcast_f(fsc[:, 128:128 + nb], 16), op=AL.mult),
           ["ftA", "fsc", "ftB"], ["ftB"], tag + " xn")

        if ceng == V:
            def em_cmp(e):
                a = ftB[:, 0:width]
                g = ftC[:, 0:width]
                u = ftA[:, 0:width]
                e.tensor_scalar(g, a, TAU6[0], None, op0=AL.is_gt)
                for i in (1, 2, 3):
                    e.scalar_tensor_tensor(g, a, TAU6[i], g, op0=AL.is_gt, op1=AL.add)
                e.tensor_scalar(u, a, TAU6[4], None, op0=AL.is_gt)
                e.scalar_tensor_tensor(u, a, TAU6[5], u, op0=AL.is_gt, op1=AL.add)
                e.scalar_tensor_tensor(g, g, 0.5, u, op0=AL.mult, op1=AL.add)
                e.tensor_scalar(u, a, TAU6[6], 2.0, op0=AL.is_gt, op1=AL.mult)
                return e.tensor_tensor(g, g, u, op=AL.add)
        else:
            def em_cmp(e):
                a = ftB[:, 0:width]
                g = ftC[:, 0:width]
                u = ftA[:, 0:width]
                e.tensor_scalar(g, a, TAU6[0], 0.5, op0=AL.is_gt, op1=AL.mult)
                for i, w in ((1, 0.5), (2, 0.5), (3, 0.5), (4, 1.0), (5, 1.0), (6, 2.0)):
                    e.tensor_scalar(u, a, TAU6[i], w, op0=AL.is_gt, op1=AL.mult)
                    e.tensor_tensor(g, g, u, op=AL.add)
                return e.engine_nop()
        st(ceng, em_cmp, ["ftB", "ftA", "ftC"], ["ftC", "ftA"], tag + " cmp")
        st(V, lambda e: e.scalar_tensor_tensor(
            ftB[:, 0:width].rearrange("p (b w) -> p b w", w=16),
            ftC[:, 0:width].rearrange("p (b w) -> p b w", w=16),
            W6, bcast_f(fsc[:, 64:64 + nb], 16), op0=AL.mult, op1=AL.mult),
           ["ftC", "fsc", "ftB"], ["ftB"], tag + " scale")
        st(V, lambda e: e.scalar_tensor_tensor(
            dst_ap[:, 0:width] if dst_ap.shape[1] != width else dst_ap,
            fsgn[:, 0:width], -1.0, ftB[:, 0:width], op0=AL.add, op1=AL.mult),
           ["ftB", "fsgn", dst_res], [dst_res], tag + " q")

    wq_ping = [0]
    fstgr = [fstage, ARD[:, 11648:13696].bitcast(F32)]

    def wq_unit_gu(wname, pc):
        hbm = dr["wg"] if wname == "g" else dr["wu"]
        wregs = wq_g if wname == "g" else wq_u
        ID32 = masks[:, 256:384]
        sl = wq_ping[0] % 2
        wq_ping[0] += 1
        FS = "fstg" if sl == 0 else "fstg1"
        dma("synck", fstgr[sl][:, 0:1024], hbm[pc * 128:(pc + 1) * 128, :],
            [FS], [FS])
        quant(fstgr[sl][:, 0:1024], 1024, [FS], ftA, "ftA", f"wq{wname}{pc}")
        for kc in range(8):
            bk = py[kc % 2]
            bn = f"py{kc % 2}"
            col = (kc // 2) * 128
            st(T, lambda e, bk=bk, col=col, kc=kc: e.transpose(
                bk[:, col:col + 128], ftA[:, kc * 128:(kc + 1) * 128], ID32),
               ["ftA", "idf32", bn], [bn], "wtp")
            st(S, lambda e, bk=bk, col=col, kc=kc, pc=pc, wregs=wregs: e.activation(
                wregs[kc][:, pc * 128:(pc + 1) * 128], bk[:, col:col + 128], AF.Copy),
               [bn, f"wq{wname}_{pc}_{kc}"], [f"wq{wname}_{pc}_{kc}"], "wtpev")

    def wq_unit_d(pc, ss, wdt):
        ID32 = masks[:, 256:384]
        sl = wq_ping[0] % 2
        wq_ping[0] += 1
        FS = "fstg" if sl == 0 else "fstg1"
        dma("synck", fstgr[sl][:, 0:wdt],
            dr["wdn"][pc * 128:(pc + 1) * 128, ss:ss + wdt], [FS], [FS])
        quant(fstgr[sl][:, 0:wdt], wdt, [FS], ftA, "ftA", f"wqd{pc}_{ss}")
        for ic in range(wdt // 128):
            icg = ss // 128 + ic
            bk = py[ic % 2]
            bn = f"py{ic % 2}"
            col = (ic // 2) * 128
            st(T, lambda e, bk=bk, col=col, ic=ic: e.transpose(
                bk[:, col:col + 128], ftA[:, ic * 128:(ic + 1) * 128], ID32),
               ["ftA", "idf32", bn], [bn], "wtp")
            st(S, lambda e, bk=bk, col=col, icg=icg, pc=pc: e.activation(
                wq_d[icg][:, pc * 128:(pc + 1) * 128], bk[:, col:col + 128], AF.Copy),
               [bn, f"wqd_{pc}_{icg}"], [f"wqd_{pc}_{icg}"], "wtpev")

    def wq_safe_units():
        us = []
        for wname in ("g", "u"):
            for pc in range(22):
                us.append(lambda wname=wname, pc=pc: wq_unit_gu(wname, pc))
        for pc in range(8):
            us.append(lambda pc=pc: wq_unit_d(pc, 0, 1024))
        return us

    def ffn():
        for pc in range(8):
            for ss, wdt in ((1024, 1024), (2048, 768)):
                wq_unit_d(pc, ss, wdt)
        dump("wqg0", wq_g[0][:], [f"wqg_{pc}_0" for pc in range(22)], BF16, (128, 2816))
        WQG = [f"wq{w}_{pc}_{kc}" for w in "gu" for pc in range(22) for kc in range(8)]
        WQD = [f"wqd_{pc}_{icg}" for pc in range(8) for icg in range(22)]
        ISL = [(0, 512), (512, 512), (1024, 512), (1536, 512), (2048, 512), (2560, 256)]
        NH2 = 2
        gb = [pmm[0], pcb]
        ub = [pmm[1], pst]
        gbn = ["pmm0", "pcb"]
        ubn = ["pmm1", "pst"]

        def x1_load(tc):
            sl = tc % 2
            wr = ["fstg"] if sl == 1 else ["fstg1"]
            dma("sync", ep2fr[sl][:], x1_scr[tc * 128:(tc + 1) * 128, :],
                [f"ep2f{sl}"] + wr, [f"ep2f{sl}"] + wr)

        def ffn_mm(isl, which):
            io, iw = ISL[isl]
            b = (gb if which == "g" else ub)[isl % 2]
            bn = (gbn if which == "g" else ubn)[isl % 2]
            wq = wq_g if which == "g" else wq_u

            def em(e, io=io, iw=iw, b=b, wq=wq):
                last = None
                for kc in range(8):
                    last = e.matmul(b[:, 0:iw],
                                    xqfm[:, kc * 128:(kc + 1) * 128],
                                    wq[kc][:, io:io + iw],
                                    start=(kc == 0), stop=(kc == 7))
                return last
            st(T, em, [f"xqf{k}" for k in range(8)] + WQG + [bn], [bn],
               which + "mm")

        def ffn_act(tc, isl, cj):
            io, iw = ISL[isl]
            ep = ephr[isl % 2]
            EPR = f"eph{isl % 2}"
            st(S, lambda e, iw=iw, ep=ep: e.activation(ep[:, 0:iw],
                                                       gb[isl % 2][:, 0:iw], AF.Silu),
               [gbn[isl % 2], EPR], [EPR], "fsilu")
            st(V, lambda e, iw=iw, ep=ep: e.tensor_tensor(ep[:, 0:iw], ep[:, 0:iw],
                                                          ub[isl % 2][:, 0:iw], op=AL.mult),
               [ubn[isl % 2], EPR], [EPR], "hmul")
            quant(ep[:, 0:iw], iw, [EPR], hqtm[:, io:io + iw], "hqtm", f"hq{tc}_{isl}")
            for ic in range(io // 128, (io + iw) // 128):
                dma("scalar", hqfm[:, ic * NH2 * 128 + cj * 128:
                                   ic * NH2 * 128 + (cj + 1) * 128],
                    hqtm[:, ic * 128:(ic + 1) * 128],
                    ["hqtm", f"hqf{cj}_{ic}"], [f"hqf{cj}_{ic}"], tp=True)

        def ffn_dn(grp):
            HQF = [f"hqf{cj}_{ic}" for cj in range(NH2) for ic in range(22)]
            for dm in range(8):
                pb = pmm[dm % 2]
                pres = f"pmm{dm % 2}"

                def em_dn(e, dm=dm, pb=pb):
                    last = None
                    for ic in range(22):
                        last = e.matmul(pb[:, 0:NH2 * 128],
                                        wq_d[ic][:, dm * 128:(dm + 1) * 128],
                                        hqfm[:, ic * NH2 * 128:(ic + 1) * NH2 * 128],
                                        start=(ic == 0), stop=(ic == 21))
                    return last
                st(T, em_dn, HQF + WQD + [pres], [pres], "dn mm")
                st(S, lambda e, pb=pb: e.activation(dtmp[:, 0:NH2 * 128],
                                                    pb[:, 0:NH2 * 128], AF.Copy),
                   [pres, "dtmp"], ["dtmp"], "dn ev")
                for cj in range(NH2):
                    dma("scalar", dntm[:, cj * D + dm * 128: cj * D + (dm + 1) * 128],
                        dtmp[:, cj * 128:(cj + 1) * 128],
                        ["dtmp", f"dntp{cj}_{dm}"], [f"dntp{cj}_{dm}"], tp=True)
            if grp == DBGTC // 2:
                dump("dn0", dntm[:, (DBGTC % 2) * D:(DBGTC % 2) * D + D],
                     [f"dntp{DBGTC % 2}_{dm}" for dm in range(8)], BF16, (128, D))

        def ffn_ln2_chunk(grp, cj):
            if True:
                tc = grp * NH2 + cj
                DNT = [f"dntp{cj}_{dm}" for dm in range(8)]
                dma("gpsimd", ep1f[:], x1_scr[tc * 128:(tc + 1) * 128, :],
                    ["ep1f"], ["ep1f"])
                st(V, lambda e, cj=cj: e.tensor_tensor(
                    ep1f[:], ep1f[:], dntm[:, cj * D:(cj + 1) * D], op=AL.add),
                   ["ep1f"] + DNT, ["ep1f"], "h2")
                st(V, lambda e: e.tensor_reduce(epst[:, 0:1], ep1f[:], axis=AX.X, op=AL.add),
                   ["ep1f", "epst"], ["epst"])
                st(S, lambda e: e.activation(ftB[:, 0:D], ep1f[:], AF.Square),
                   ["ep1f", "ftB"], ["ftB"])
                st(V, lambda e: e.tensor_reduce(epst[:, 1:2], ftB[:, 0:D], axis=AX.X, op=AL.add),
                   ["ftB", "epst"], ["epst"])
                st(V, lambda e: e.tensor_scalar(epst[:, 2:3], epst[:, 0:1], 1.0 / D, None,
                                                op0=AL.mult), ["epst"], ["epst"])
                st(V, lambda e: e.tensor_tensor(epst[:, 3:4], epst[:, 2:3], epst[:, 2:3],
                                                op=AL.mult), ["epst"], ["epst"])
                st(V, lambda e: e.tensor_scalar(epst[:, 4:5], epst[:, 1:2], 1.0 / D, EPS,
                                                op0=AL.mult, op1=AL.add), ["epst"], ["epst"])
                st(V, lambda e: e.tensor_tensor(epst[:, 4:5], epst[:, 4:5], epst[:, 3:4],
                                                op=AL.subtract), ["epst"], ["epst"])
                st(V, lambda e: e.reciprocal(epst[:, 5:6], epst[:, 4:5]), ["epst"], ["epst"])
                st(S, lambda e: e.activation(epst[:, 6:7], epst[:, 5:6], AF.Sqrt),
                   ["epst"], ["epst"])
                st(V, lambda e: e.tensor_scalar(ep1f[:], ep1f[:], epst[:, 2:3], epst[:, 6:7],
                                                op0=AL.subtract, op1=AL.mult),
                   ["ep1f"], ["ep1f"], sreads=["epst"], tag="x2")
                dma("gpsimd", out[tc * 128:(tc + 1) * 128, :], ep1f[:], ["ep1f"], [f"o{tc}"])

        def ffn_ln2(grp):
            for cj in range(NH2):
                ffn_ln2_chunk(grp, cj)

        x1_load(0)
        for grp in range(L // (NH2 * 128)):
            for cj in range(NH2):
                tc = grp * NH2 + cj
                sl = tc % 2
                quant(ep2fr[sl][:], D, [f"ep2f{sl}"], xqtm, "xqtm", f"x1q{tc}")
                if tc + 1 < 16:
                    x1_load(tc + 1)
                if tc == DBGTC:
                    dump("xq0", xqtm[:], ["xqtm"], BF16, (128, D))
                for kc in range(8):
                    dma("scalar", xqfm[:, kc * 128:(kc + 1) * 128],
                        xqtm[:, kc * 128:(kc + 1) * 128], ["xqtm", f"xqf{kc}"],
                        [f"xqf{kc}"], tp=True)
                ffn_mm(0, "g")
                ffn_mm(0, "u")
                if grp > 0:
                    ffn_ln2_chunk(grp - 1, cj)
                for isl in range(len(ISL)):
                    if isl + 1 < len(ISL):
                        ffn_mm(isl + 1, "g")
                        ffn_mm(isl + 1, "u")
                    ffn_act(tc, isl, cj)
                if tc == DBGTC:
                    dump("hq0", hqtm[:], ["hqtm"], BF16, (128, 2816))
            ffn_dn(grp)
        ffn_ln2(L // (NH2 * 128) - 1)

    p.fence()
    pass_a("f")
    dump("yscrf", yscr[0], [f"yscr0_{c}" for c in range(16)], BF16, (NXS, 128, L))
    dump("lgscrf", lg_scr[0], [f"lgscr0_{c}" for c in range(16)], F32, (16, 32, Q))
    p.fence()
    pass_b("f")
    dump("accf", acc[:], ["acc"], BF16, (128, 8 * L))
    p.fence()
    pass_a("b")
    p.fence()
    pass_b("b")
    dump("yscrb", yscr[1], [f"yscr1_{c}" for c in range(16)], BF16, (NXS, 128, L))
    dump("lgscrb", lg_scr[1], [f"lgscr1_{c}" for c in range(16)], F32, (16, 32, Q))
    dump("accb", acc[:], ["acc"], BF16, (128, 8 * L))
    p.fence()
    epilogue(wq_safe_units())
    dump("x1", x1_scr, [f"x1s{tc}" for tc in range(16)], F32, (L, D))
    dump("xfm", xfm_scr, XFM, BF16, (8, 128, L))
    p.fence()
    ffn()
    st(G, lambda e: e.engine_nop(), [f"o{tc}" for tc in range(16)], [], "final")

    with (
        nc.semaphore("s_sync") as s0,
        nc.semaphore("s_scalar") as s1,
        nc.semaphore("s_vector") as s2,
        nc.semaphore("s_tensor") as s3,
        nc.semaphore("s_gpsimd") as s4,
        nc.Block() as blk,
    ):
        sems = dict(zip(ENGS, (s0, s1, s2, s3, s4)))

        def body_for(name):
            def body(e):
                my = sems[name]
                for eng, waits, emit, inc, tag in p.steps:
                    if eng != name:
                        continue
                    for s, v in waits.items():
                        e.wait_ge(sems[s], v)
                    last = emit(e)
                    last.then_inc(my, inc)
            return body

        blk.sync(body_for("sync"))
        blk.scalar(body_for("scalar"))
        blk.vector(body_for("vector"))
        blk.tensor(body_for("tensor"))
        blk.gpsimd(body_for("gpsimd"))

    for t in reversed(ctxs):
        t.__exit__(None, None, None)
    return nc, dbg_outs


def host_inputs(inputs, b):
    g = lambda k: np.asarray(inputs[k])
    f32 = np.float32
    m = {}
    m["x"] = np.ascontiguousarray(g("L0_attn_x")[b], f32)
    m["xT"] = np.ascontiguousarray(m["x"].T)
    for d in ("f", "b"):
        W = np.ascontiguousarray(g(f"L0_attn_{d}_in_proj_w"), f32)
        WT = np.ascontiguousarray(W.T)
        wa = np.concatenate([
            WT[:, 2048:4224],
            WT[:, 4160:4224],
            WT[:, 4224:4256],
            np.zeros((D, 32), f32),
        ], axis=1)
        m[f"wa_{d}"] = np.ascontiguousarray(wa)
        m[f"wz_{d}"] = np.ascontiguousarray(WT[:, 0:2048])
        m[f"op_{d}"] = np.ascontiguousarray(g(f"L0_attn_{d}_out_proj_w"), f32)
        half = slice(0, D) if d == "f" else slice(D, 2 * D)
        m[f"ph_{d}"] = np.ascontiguousarray(g("L0_attn_proj_w")[:, half].T, f32)
        cw = np.asarray(g(f"L0_attn_{d}_conv_w"), f32)[:, 0, :]
        cb = np.asarray(g(f"L0_attn_{d}_conv_b"), f32)
        cwc = np.zeros((128, NXBC * 4), f32)
        cbc = np.zeros((128, NXBC), f32)
        for ch in range(2176):
            c, pp = divmod(ch, 128)
            cwc[pp, c * 4:(c + 1) * 4] = cw[:, ch]
            cbc[pp, c] = cb[ch]
        for j in range(64):
            ch = 2112 + j
            cwc[j, 17 * 4:18 * 4] = cw[:, ch]
            cbc[j, 17] = cb[ch]
        m[f"convw_{d}"] = cwc
        m[f"convb_{d}"] = cbc
        m[f"dtb_{d}"] = np.ascontiguousarray(g(f"L0_attn_{d}_dt_bias").reshape(32, 1), f32)
        m[f"alog_{d}"] = np.ascontiguousarray(g(f"L0_attn_{d}_A_log").reshape(32, 1), f32)
        Dp = np.asarray(g(f"L0_attn_{d}_D"), f32)
        dc = np.zeros((128, NXS), f32)
        for pc in range(NXS):
            dc[0:64, pc] = Dp[2 * pc]
            dc[64:128, pc] = Dp[2 * pc + 1]
        m[f"dcol_{d}"] = dc
        rw = np.asarray(g(f"L0_attn_{d}_rms_w"), f32)
        rc = np.zeros((128, NXS), f32)
        for fc in range(NXS):
            rc[:, fc] = rw[fc * 128:(fc + 1) * 128]
        m[f"rmscol_{d}"] = rc
    pb = np.asarray(g("L0_attn_proj_b"), f32)
    pbc = np.zeros((128, 8), f32)
    for dcx in range(8):
        pbc[:, dcx] = pb[dcx * 128:(dcx + 1) * 128]
    m["pbcol"] = pbc
    m["lnw_a"] = np.ascontiguousarray(g("L0_attn_ln_w").reshape(1, D), f32)
    m["lnb_a"] = np.ascontiguousarray(g("L0_attn_ln_b").reshape(1, D), f32)
    m["wg"] = np.ascontiguousarray(g("L0_ffn_w_gate"), f32)
    m["wu"] = np.ascontiguousarray(g("L0_ffn_w_up"), f32)
    m["wdn"] = np.ascontiguousarray(g("L0_ffn_w_down"), f32)
    si = np.arange(128)[:, None]
    ti = np.arange(128)[None, :]
    m["maskF"] = (si <= ti).astype(f32)
    m["maskB"] = (si >= ti).astype(f32)
    m["idf32"] = np.eye(128, dtype=f32)
    m["ones_bf"] = np.ones((128, 1), np.float32).astype(ml_dtypes.bfloat16)
    ih = np.zeros((128, HP), np.float32)
    ih[64:128, :] = np.eye(64, dtype=np.float32)
    m["i64hi"] = ih.astype(ml_dtypes.bfloat16)
    return m


def kernel(**inputs):
    nc, _ = build_program()
    in_maps = [host_inputs(inputs, b) for b in range(8)]
    res = run_bass_kernel_spmd(nc, in_maps, core_ids=list(range(8)))
    return np.stack([res.results[b]["out"] for b in range(8)], axis=0)
